# version 07 (final)
# baseline (speedup 1.0000x reference)
import math
import numpy as np
import concourse.bass as bass
import concourse.mybir as mybir
from concourse.bass_utils import run_bass_kernel_spmd
from contextlib import ExitStack

F32 = mybir.dt.float32
BF16 = mybir.dt.bfloat16
AF = mybir.ActivationFunctionType
ALU = mybir.AluOpType
AX = mybir.AxisListType

MAGIC = 3 * 2.0**22
TWO_PI = 2 * math.pi
CW1 = float(np.float32(6.28125))
CW2 = float(np.float32(TWO_PI - 6.28125))
CW3 = float(TWO_PI - CW1 - float(np.float32(TWO_PI - 6.28125)))

T = 4096
C = 2048
TC = 512
NTT = TC // 128
NCH = T // TC
NCC = C // 128

N_CORES = 8



class Buf:
    __slots__ = ("name", "writer", "readers")

    def __init__(self, name=""):
        self.name = name
        self.writer = None
        self.readers = []


class Prog:
    ENGINES = ("te", "ve", "sc", "gp", "sy")
    _SECT = {"te": "tensor", "ve": "vector", "sc": "scalar",
             "gp": "gpsimd", "sy": "sync"}

    def __init__(self, nc, stack):
        self.nc = nc
        self.q = {"te": nc.tensor, "ve": nc.vector, "sc": nc.scalar,
                  "gp": nc.gpsimd, "sy": nc.sync}
        self.sem = {}
        for e in self.ENGINES:
            self.sem[e] = stack.enter_context(nc.semaphore(f"s_{e}"))
            self.sem["d" + e] = stack.enter_context(nc.semaphore(f"sd_{e}"))
        self.cnt = {k: 0 for k in self.sem}
        self.wm = {e: {} for e in self.ENGINES}
        self.pending = {e: [] for e in self.ENGINES}
        self.unresolved = {e: [] for e in self.ENGINES}
        self.n_ins = 0

    def op(self, eng, fn, reads=(), writes=(), deps=(), inc=True, dma=False,
           flush=False):
        toks = list(deps)
        for b in reads:
            if b.writer is not None:
                toks.append(b.writer)
        for b in writes:
            if b.writer is not None:
                toks.append(b.writer)
            toks.extend(b.readers)
        best = {}
        for t in toks:
            k, v = t[0], t[1]
            if k == eng:
                continue
            if v is None:
                raise RuntimeError(f"{eng} dep on un-inc'd op on {k}")
            if best.get(k, -1) < v:
                best[k] = v
        for k in sorted(best):
            v = best[k]
            if self.wm[eng].get(k, 0) < v:
                self.pending[eng].append(("w", k, v))
                self.wm[eng][k] = v
        key = ("d" + eng) if dma else eng
        if flush:
            self.pending[eng].append(("i", fn, None, 0))
            self.cnt[key] += 1
            tok = [key, self.cnt[key]]
            self.pending[eng].append(("i", lambda q: q.drain(), key, 1))
            for ent in self.unresolved[eng]:
                ent[0] = key
                ent[1] = self.cnt[key]
            self.unresolved[eng] = []
            self.n_ins += 2
            for b in reads:
                b.readers.append(tok)
            for b in writes:
                b.writer = tok
                b.readers = []
            return tok
        if inc:
            amt = 16 if dma else 1
            self.cnt[key] += amt
            tok = [key, self.cnt[key]]
            self.pending[eng].append(("i", fn, key, amt))
            if not dma:
                for ent in self.unresolved[eng]:
                    ent[0] = key
                    ent[1] = self.cnt[key]
                self.unresolved[eng] = []
        else:
            tok = [key, None]
            self.pending[eng].append(("i", fn, None, 0))
            self.unresolved[eng].append(tok)
        self.n_ins += 1
        for b in reads:
            b.readers.append(tok)
        for b in writes:
            b.writer = tok
            b.readers = []
        return tok

    def dma(self, eng, out, in_, reads=(), writes=(), deps=(), **kw):
        return self.op(eng, lambda q: q.dma_start(out=out, in_=in_, **kw),
                       reads=reads, writes=writes, deps=deps, dma=True)

    def tok(self, eng):
        return (eng, self.cnt[eng])

    def barrier(self):
        snap = dict(self.cnt)
        for e in self.ENGINES:
            for k, v in snap.items():
                if v > 0 and k != e and self.wm[e].get(k, 0) < v:
                    self.pending[e].append(("w", k, v))
                    self.wm[e][k] = v

    def wait_all(self, eng):
        for k, v in list(self.cnt.items()):
            if v > 0 and k != eng and self.wm[eng].get(k, 0) < v:
                self.pending[eng].append(("w", k, v))
                self.wm[eng][k] = v

    def finalize(self, block):
        for eng in self.ENGINES:
            items = self.pending[eng]
            if not items:
                continue

            def emit(q, items=items):
                for it in items:
                    if it[0] == "w":
                        q.wait_ge(self.sem[it[1]], it[2])
                    else:
                        ins = it[1](q)
                        if it[2] is not None:
                            ins.then_inc(self.sem[it[2]], it[3])

            getattr(block, self._SECT[eng])(emit)



def emit_ident(p, ident, b_ident):
    p.op("gp", lambda q: q.memset(ident[:], 0.0), writes=[b_ident])
    p.op("gp", lambda q: q.affine_select(
        out=ident[:], in_=ident[:], compare_op=ALU.not_equal, fill=1.0,
        base=0, pattern=[[-1, 128]], channel_multiplier=1),
        reads=[b_ident], writes=[b_ident])


def emit_ln_stats(p, x_ap, sq_scr, b_sq, sums, sumsq, b_sums, b_sumsq, i,
                  b_x, eps=1e-5, n=C):
    nsl = sq_scr.shape[1] // n
    sl = i % nsl
    scr = sq_scr[:, sl * n:(sl + 1) * n]
    p.op("ve", lambda q: q.reduce_sum(sums[:, i:i + 1], x_ap, axis=AX.X),
         reads=[b_x], writes=[b_sums], flush=True)
    p.op("sc", lambda q: q.activation(out=scr, in_=x_ap, func=AF.Square),
         reads=[b_x], writes=[b_sq[sl]])
    p.op("ve", lambda q: q.reduce_sum(sumsq[:, i:i + 1], scr, axis=AX.X),
         reads=[b_sq[sl]], writes=[b_sumsq], flush=True)


def emit_ln_coeffs(p, sums, sumsq, b_sums, b_sumsq, acol, bcol, tmp,
                   b_a, b_b, b_tmp, ncols, eps_col=None, b_eps=None, n=C):
    s = slice(0, ncols)
    p.op("ve", lambda q: q.tensor_scalar(out=tmp[:, s], in0=sums[:, s],
                                         scalar1=1.0 / n, scalar2=None,
                                         op0=ALU.mult),
         reads=[b_sums], writes=[b_tmp], flush=True)
    p.op("ve", lambda q: q.tensor_mul(acol[:, s], tmp[:, s], tmp[:, s]),
         reads=[b_tmp], writes=[b_a], flush=True)
    p.op("ve", lambda q: q.scalar_tensor_tensor(
        out=acol[:, s], in0=sumsq[:, s], scalar=1.0 / n, in1=acol[:, s],
        op0=ALU.mult, op1=ALU.subtract),
        reads=[b_sumsq, b_a], writes=[b_a], flush=True)
    p.op("sc", lambda q: q.activation(out=acol[:, s], in_=acol[:, s],
                                      func=AF.Sqrt, bias=eps_col),
         reads=[b_a, b_eps], writes=[b_a], flush=True)
    p.op("ve", lambda q: q.reciprocal(acol[:, s], acol[:, s]),
         reads=[b_a], writes=[b_a], flush=True)
    p.op("ve", lambda q: q.scalar_tensor_tensor(
        out=bcol[:, s], in0=tmp[:, s], scalar=-1.0, in1=acol[:, s],
        op0=ALU.mult, op1=ALU.mult),
        reads=[b_tmp, b_a], writes=[b_b], flush=True)


def emit_transpose_chunk(p, ident, b_ident, ln_tm, b_lntm, ln_fm, b_lnfm,
                         pt, b_pt, ntt=NTT, tc=TC, ncc=NCC, c=C):
    for j in range(ncc):
        pb = j % 2
        for i in range(ntt):
            src = ln_tm[:, i * c + j * 128: i * c + (j + 1) * 128]
            dst = pt[pb][:, i * 128:(i + 1) * 128]
            rd = b_lntm[i] if isinstance(b_lntm, list) else b_lntm
            p.op("te", lambda q, src=src, dst=dst: q.transpose(dst, src, ident[:]),
                 reads=[rd, b_ident], writes=[b_pt[pb]], inc=(i == ntt - 1))
        p.op("ve", lambda q, j=j, pb=pb: q.tensor_copy(
            ln_fm[:, j * tc: j * tc + ntt * 128], pt[pb][:, 0:ntt * 128]),
            reads=[b_pt[pb]], writes=[b_lnfm])



def emit_l0f(p, nc, stack, io, debug=False):
    x = io["x"]
    wT = io["wT"]
    bias = io["bias"]
    y = io["y"]
    e = stack.enter_context
    wt_bf = e(nc.sbuf_tensor("f0_wt_bf", [128, NCC * C], BF16))
    wq_f = [e(nc.sbuf_tensor(f"f0_wq_f{i}", [128, C], F32)) for i in range(2)]
    x_sb = [e(nc.sbuf_tensor(f"f0_x_sb{i}", [128, NTT * C], F32)) for i in range(2)]
    ln_tm = e(nc.sbuf_tensor("f0_ln_tm", [128, NTT * C], BF16))
    ln_fm = e(nc.sbuf_tensor("f0_ln_fm", [128, NCC * TC], BF16))
    sq_scr = e(nc.sbuf_tensor("f0_sq_scr", [128, 2 * C], F32))
    ident = e(nc.sbuf_tensor("f0_ident", [128, 128], BF16))
    onesr = e(nc.sbuf_tensor("f0_onesr", [1, 128], BF16))
    epsc = e(nc.sbuf_tensor("f0_epsc", [128, 1], F32))
    stat = e(nc.sbuf_tensor("f0_stat", [128, 4 * NTT], F32))
    stat_t = e(nc.sbuf_tensor("f0_stat_t", [128, NTT], F32))
    absr = e(nc.sbuf_tensor("f0_absr", [128, 18], F32))
    srow = e(nc.sbuf_tensor("f0_srow", [1, 130], F32))
    scol = e(nc.sbuf_tensor("f0_scol", [128, 1], F32))
    brow = e(nc.sbuf_tensor("f0_brow", [1, C], BF16))
    ps = [e(nc.psum_tensor(f"f0_ps{i}", [128, 512], F32)) for i in range(4)]
    pt = [e(nc.psum_tensor(f"f0_pt{i}", [128, 512], BF16)) for i in range(2)]
    b_ps = [Buf(f"ps{i}") for i in range(4)]
    b_pt = [Buf(f"pt{i}") for i in range(2)]
    b_wt = Buf("wt"); b_wq = [Buf("wq0"), Buf("wq1")]
    b_x = [Buf("x0"), Buf("x1")]
    b_lntm = Buf("ln_tm"); b_lnfm = Buf("ln_fm")
    b_sq = [Buf("sq0"), Buf("sq1")]
    b_ident = Buf("ident"); b_ones = Buf("ones")
    b_stat = Buf("stat"); b_statt = Buf("stat_t")
    b_abs = Buf("abs"); b_srow = Buf("srow"); b_scol = Buf("scol")
    b_brow = Buf("brow")

    b_eps = Buf("eps")
    emit_ident(p, ident, b_ident)
    p.op("gp", lambda q: q.memset(onesr[:], 1.0), writes=[b_ones])
    p.op("gp", lambda q: q.memset(epsc[:], 1e-5), writes=[b_eps])

    for k in range(16):
        pb = k % 2
        p.dma("sy", wq_f[pb][:],
              wT.rearrange("(k p) c -> k p c", k=16, p=128)[k],
              writes=[b_wq[pb]])
        p.op("sc", lambda q, k=k, pb=pb: q.activation(
            out=sq_scr[:, (k % 2) * C:(k % 2 + 1) * C], in_=wq_f[pb][:],
            func=AF.Abs),
            reads=[b_wq[pb]], writes=[b_sq[k % 2]])
        p.op("ve", lambda q, k=k: q.reduce_sum(
            absr[:, k:k + 1], sq_scr[:, (k % 2) * C:(k % 2 + 1) * C],
            axis=AX.X),
            reads=[b_sq[k % 2]], writes=[b_abs], flush=True)
    p.op("ve", lambda q: q.reduce_sum(absr[:, 16:17], absr[:, 0:16], axis=AX.X),
         reads=[b_abs], writes=[b_abs], flush=True)
    pcol = ps[3]
    idf = sq_scr
    p.op("gp", lambda q: q.memset(idf[:, 0:128], 0.0), writes=[b_sq[0]])
    p.op("gp", lambda q: q.affine_select(
        out=idf[:, 0:128], in_=idf[:, 0:128], compare_op=ALU.not_equal,
        fill=1.0, base=0, pattern=[[-1, 128]], channel_multiplier=1),
        reads=[b_sq[0]], writes=[b_sq[0]])
    p.op("te", lambda q: q.transpose(pcol[0:1, 0:128], absr[:, 16:17],
                                     idf[:, 0:128]),
         reads=[b_abs, b_sq[0]], writes=[b_ps[3]])
    p.op("ve", lambda q: q.tensor_copy(srow[0:1, 0:128], pcol[0:1, 0:128]),
         reads=[b_ps[3]], writes=[b_srow], flush=True)
    p.op("ve", lambda q: q.reduce_sum(srow[0:1, 128:129], srow[0:1, 0:128],
                                      axis=AX.X),
         reads=[b_srow], writes=[b_srow], flush=True)
    p.op("ve", lambda q: q.tensor_scalar(out=srow[0:1, 128:129],
                                         in0=srow[0:1, 128:129],
                                         scalar1=1.0 / (C * C),
                                         scalar2=None, op0=ALU.mult),
         reads=[b_srow], writes=[b_srow], flush=True)
    p.op("ve", lambda q: q.tensor_scalar(out=srow[0:1, 129:130],
                                         in0=srow[0:1, 128:129],
                                         scalar1=1e-10, scalar2=None,
                                         op0=ALU.add),
         reads=[b_srow], writes=[b_srow], flush=True)
    p.op("ve", lambda q: q.reciprocal(srow[0:1, 129:130], srow[0:1, 129:130]),
         reads=[b_srow], writes=[b_srow], flush=True)
    onesr_f = srow[0:1, 0:128]
    p.op("ve", lambda q: q.tensor_scalar(out=onesr_f, in0=onesr_f,
                                         scalar1=0.0, scalar2=1.0,
                                         op0=ALU.mult, op1=ALU.add),
         reads=[b_srow], writes=[b_srow], flush=True)
    p.op("te", lambda q: q.matmul(ps[3][:, 4:5], onesr_f, srow[0:1, 129:130],
                                  start=True, stop=True),
         reads=[b_srow], writes=[b_ps[3]])
    p.op("ve", lambda q: q.tensor_copy(scol[:], ps[3][:, 4:5]),
         reads=[b_ps[3]], writes=[b_scol], flush=True)
    p.op("te", lambda q: q.matmul(ps[3][:, 8:9], onesr_f, srow[0:1, 128:129],
                                  start=True, stop=True),
         reads=[b_srow], writes=[b_ps[3]])
    scol_s = absr[:, 17:18]
    p.op("ve", lambda q: q.tensor_copy(scol_s, ps[3][:, 8:9]),
         reads=[b_ps[3]], writes=[b_abs], flush=True)

    for k in range(16):
        pb = k % 2
        p.dma("sy", wq_f[pb][:],
              wT.rearrange("(k p) c -> k p c", k=16, p=128)[k],
              writes=[b_wq[pb]])
        p.op("ve", lambda q, pb=pb: q.tensor_scalar(
            out=wq_f[pb][:], in0=wq_f[pb][:], scalar1=scol[:, 0:1],
            scalar2=MAGIC, op0=ALU.mult, op1=ALU.add),
            reads=[b_wq[pb], b_scol], writes=[b_wq[pb]])
        p.op("ve", lambda q, pb=pb: q.tensor_scalar(
            out=wq_f[pb][:], in0=wq_f[pb][:], scalar1=MAGIC, scalar2=1.0,
            op0=ALU.subtract, op1=ALU.min),
            reads=[b_wq[pb]], writes=[b_wq[pb]])
        p.op("ve", lambda q, k=k, pb=pb: q.tensor_scalar(
            out=wt_bf[:, k * C:(k + 1) * C],
            in0=wq_f[pb][:],
            scalar1=-1.0, scalar2=None, op0=ALU.max),
            reads=[b_wq[pb]], writes=[b_wt])

    brow_f = wq_f[0][0:1, 0:C]
    p.dma("sy", brow_f, bias, writes=[b_wq[0]])
    p.op("ve", lambda q: q.reciprocal(srow[0:1, 129:130], srow[0:1, 128:129]),
         reads=[b_srow], writes=[b_srow], flush=True)
    p.op("ve", lambda q: q.tensor_scalar(out=brow[:], in0=brow_f,
                                         scalar1=srow[0:1, 129:130],
                                         scalar2=None, op0=ALU.mult),
         reads=[b_wq[0], b_srow], writes=[b_brow], flush=True)

    sums = stat[:, 0:NTT]; sumsq = stat[:, NTT:2 * NTT]
    acol = stat[:, 2 * NTT:3 * NTT]; bcol = stat[:, 3 * NTT:4 * NTT]
    for ch in range(NCH):
        pb = ch % 2
        bx = b_x[pb]
        p.dma("sy", x_sb[pb][:].rearrange("p (i c) -> p i c", i=NTT),
              x.rearrange("(ch i p) c -> ch p i c", ch=NCH, p=128)[ch],
              writes=[bx])
        for i in range(NTT):
            emit_ln_stats(p, x_sb[pb][:, i * C:(i + 1) * C], sq_scr, b_sq,
                          sums, sumsq, b_stat, b_stat, i, bx)
        emit_ln_coeffs(p, sums, sumsq, b_stat, b_stat, acol, bcol, stat_t,
                       b_stat, b_stat, b_statt, NTT, epsc[:, 0:1], b_eps)
        for i in range(NTT):
            p.op("sc", lambda q, i=i, pb=pb: q.activation(
                out=ln_tm[:, i * C:(i + 1) * C],
                in_=x_sb[pb][:, i * C:(i + 1) * C], func=AF.Identity,
                scale=acol[:, i:i + 1], bias=bcol[:, i:i + 1]),
                reads=[bx, b_stat], writes=[b_lntm])
        emit_transpose_chunk(p, ident, b_ident, ln_tm, b_lntm, ln_fm,
                             b_lnfm, pt, b_pt)
        for i in range(NTT):
            for ob in range(4):
                bank = b_ps[(i * 4 + ob) % 2]
                pbank = ps[(i * 4 + ob) % 2]
                p.op("te", lambda q, ob=ob, pbank=pbank: q.matmul(
                    pbank[:], onesr[:], brow[0:1, ob * 512:(ob + 1) * 512],
                    start=True, stop=False),
                    reads=[b_ones, b_brow], writes=[bank], inc=False)
                for j in range(NCC):
                    p.op("te", lambda q, i=i, ob=ob, j=j, pbank=pbank: q.matmul(
                        pbank[:],
                        ln_fm[:, j * TC + i * 128: j * TC + (i + 1) * 128],
                        wt_bf[:, j * C + ob * 512: j * C + (ob + 1) * 512],
                        start=False, stop=(j == NCC - 1)),
                        reads=[b_lnfm, b_wt], writes=[bank],
                        inc=(j == NCC - 1))
                xa = x_sb[pb][:, i * C + ob * 512: i * C + (ob + 1) * 512]
                p.op("ve", lambda q, xa=xa, pbank=pbank: q.scalar_tensor_tensor(
                    out=xa, in0=pbank[:], scalar=scol_s[:, 0:1], in1=xa,
                    op0=ALU.mult, op1=ALU.add),
                    reads=[bank, b_abs], writes=[bx])
            p.dma("sy", y.rearrange("(ch i p) c -> ch p i c", ch=NCH, p=128)[ch][:, i, :],
                  x_sb[pb][:, i * C:(i + 1) * C], reads=[bx])


def build_l0f(debug=False):
    nc = bass.Bass("TRN2", target_bir_lowering=False)
    x = nc.dram_tensor("x", [T, C], F32, kind="ExternalInput").ap()
    wT = nc.dram_tensor("wT", [C, C], F32, kind="ExternalInput").ap()
    bias = nc.dram_tensor("bias", [1, C], F32, kind="ExternalInput").ap()
    y = nc.dram_tensor("y", [T, C], F32, kind="ExternalOutput").ap()
    io = {"x": x, "wT": wT, "bias": bias, "y": y}
    with ExitStack() as stack:
        p = Prog(nc, stack)
        with ExitStack() as sstack:
            emit_l0f(p, nc, sstack, io, debug)
        p.wait_all("sy")
        block = stack.enter_context(nc.Block())
        p.finalize(block)
    return nc


def emit_transpose_tile(p, ident, b_ident, src_ap, b_src, ln_fm, b_lnfm,
                        i, pt, b_pt, tc=TC, ncc=NCC):
    for jg in range(ncc // 4):
        pb = jg % 2
        for jj in range(4):
            j = jg * 4 + jj
            p.op("te", lambda q, j=j, jj=jj, pb=pb: q.transpose(
                pt[pb][:, jj * 128:(jj + 1) * 128],
                src_ap[:, j * 128:(j + 1) * 128], ident[:]),
                reads=[b_src, b_ident], writes=[b_pt[pb]], inc=(jj == 3))
        dst = ln_fm[:].rearrange("p (j t) -> p j t", j=ncc)[
            :, jg * 4:(jg + 1) * 4, i * 128:(i + 1) * 128]
        p.op("ve", lambda q, pb=pb, dst=dst: q.tensor_copy(
            dst, pt[pb][:].rearrange("p (a b) -> p a b", a=4)),
            reads=[b_pt[pb]], writes=[b_lnfm])


def emit_convert(p, nc, pairs):
    for (srcT, dstT) in pairs:
        rows = srcT.shape[0]
        for r0 in range(0, rows, 512):
            p.dma("gp", dstT[r0:r0 + 512, :], srcT[r0:r0 + 512, :])



H0 = 16
G0 = 4
D0 = 128
KV0 = G0 * D0
SCAP = 50.0
RSQD = 1.0 / math.sqrt(D0)


def emit_l0a(p, nc, stack, io, debug=False):
    x = io["x"]
    wqT = io["wqTb"]
    wkT = io["wkTb"]
    wvT = io["wvTb"]
    woT = io["woTb"]
    bo = io["bo"]
    qwc = io["qwc"]
    kwc = io["kwc"]
    y = io["y"]
    Qs = io["Qs"]
    Ys = io["Ys"]
    e = stack.enter_context
    x_sb = e(nc.sbuf_tensor("a0_x_sb", [128, NTT * C], F32))
    sq_scr = e(nc.sbuf_tensor("a0_sq_scr", [128, C], F32))
    ln_tm = e(nc.sbuf_tensor("a0_ln_tm", [128, 2 * C], BF16))
    ln_fm = e(nc.sbuf_tensor("a0_ln_fm", [128, NCC * TC], BF16))
    k_sb = e(nc.sbuf_tensor("a0_k_sb", [128, G0 * T], BF16))
    v_sb = e(nc.sbuf_tensor("a0_v_sb", [128, (T // 128) * KV0], BF16))
    wv_sb = e(nc.sbuf_tensor("a0_wv_sb", [128, NCC * KV0], BF16))
    qh_sb = None
    qro = e(nc.sbuf_tensor("a0_qro", [128, 2 * 512], BF16))
    ws = [e(nc.sbuf_tensor(f"a0_ws{i}", [128, NCC * 128], BF16)) for i in range(2)]
    cosT = e(nc.sbuf_tensor("a0_cosT", [128, T], BF16))
    sinT = e(nc.sbuf_tensor("a0_sinT", [128, T], BF16))
    masks = e(nc.sbuf_tensor("a0_masks", [128, 4 * 512], BF16))
    ident = e(nc.sbuf_tensor("a0_ident", [128, 128], BF16))
    stat = e(nc.sbuf_tensor("a0_stat", [128, 4 * NTT], F32))
    stat_t = e(nc.sbuf_tensor("a0_stat_t", [128, NTT], F32))
    epsc = e(nc.sbuf_tensor("a0_epsc", [128, 1], F32))
    m50c = e(nc.sbuf_tensor("a0_m50c", [128, 1], F32))
    qw_sb = e(nc.sbuf_tensor("a0_qw_sb", [128, 2], F32))
    onesr = e(nc.sbuf_tensor("a0_onesr", [1, 128], BF16))
    onesc = e(nc.sbuf_tensor("a0_onesc", [128, 1], BF16))
    rowsc = e(nc.sbuf_tensor("a0_rowsc", [1, 2 * 512], F32))
    rrbf = e(nc.sbuf_tensor("a0_rrbf", [1, 2 * 512], BF16))
    qf = e(nc.sbuf_tensor("a0_qf", [128, 2 * 512], F32))
    qn = e(nc.sbuf_tensor("a0_qn", [128, 512], F32))
    qsw = e(nc.sbuf_tensor("a0_qsw", [128, 512], F32))
    rtmp = e(nc.sbuf_tensor("a0_rtmp", [128, 1024], F32))
    q2b = e(nc.sbuf_tensor("a0_q2b", [128, 512], BF16))
    pt_sb = [e(nc.sbuf_tensor(f"a0_pt_sb{i}", [128, 1024], BF16)) for i in range(2)]
    zb_sb = e(nc.sbuf_tensor("a0_zb_sb", [128, 512], BF16))
    yo_sb = e(nc.sbuf_tensor("a0_yo_sb", [128, 2 * 512], BF16))
    brow = e(nc.sbuf_tensor("a0_brow", [1, C], BF16))
    ifr = e(nc.sbuf_tensor("a0_ifr", [128, 2], F32))
    psA = e(nc.psum_tensor("a0_psA", [128, 1024], F32))
    psB = e(nc.psum_tensor("a0_psB", [128, 1024], F32))
    psY = e(nc.psum_tensor("a0_psY", [128, 512], F32))
    psZ = e(nc.psum_tensor("a0_psZ", [128, 512], F32))
    psR = e(nc.psum_tensor("a0_psR", [128, 512], F32))
    pt = e(nc.psum_tensor("a0_pt", [128, 1024], BF16))
    b_psA = Buf("psA"); b_psB = Buf("psB"); b_psY = Buf("psY")
    b_psZ = Buf("psZ"); b_psR = Buf("psR"); b_pt = Buf("pt")
    mmslots = [(psA[:, 0:512], b_psA), (psA[:, 512:1024], b_psA),
               (psB[:, 0:512], b_psB), (psB[:, 512:1024], b_psB)]
    b_x = Buf("x")
    b_sq = [Buf("sq0")]; b_sq.append(b_sq[0])
    b_lntm = [Buf("ln_tm0"), Buf("ln_tm1")]; b_lnfm = Buf("ln_fm")
    b_k = Buf("k"); b_v = Buf("v"); b_wv = Buf("wv")
    b_qh = Buf("qh"); b_qro = Buf("qro")
    b_ws = [Buf("ws0"), Buf("ws1")]
    b_cos = Buf("cos"); b_sin = Buf("sin"); b_mask = Buf("mask")
    b_ident = Buf("ident"); b_stat = Buf("stat"); b_statt = Buf("stat_t")
    b_eps = Buf("eps"); b_m50 = Buf("m50"); b_qw = Buf("qw")
    b_ones = Buf("ones"); b_onesc = Buf("onesc"); b_rows = Buf("rows")
    b_rrb = Buf("rrb")
    b_qf = Buf("qf"); b_qn = Buf("qn"); b_qsw = Buf("qsw"); b_rt = Buf("rt")
    b_q2 = Buf("q2")
    b_ptb = [Buf("ptb0"), Buf("ptb1")]
    b_zb = Buf("zb"); b_yo = Buf("yo"); b_brow = Buf("brow"); b_ifr = Buf("ifr")

    emit_ident(p, ident, b_ident)
    p.op("gp", lambda q: q.memset(onesr[:], 1.0), writes=[b_ones])
    p.op("gp", lambda q: q.memset(onesc[:], 1.0), writes=[b_onesc])
    p.op("gp", lambda q: q.memset(epsc[:], 1e-5), writes=[b_eps])
    p.op("gp", lambda q: q.memset(m50c[:], -SCAP), writes=[b_m50])
    p.dma("sy", qw_sb[:, 0:1], qwc[:], writes=[b_qw])
    p.dma("sy", qw_sb[:, 1:2], kwc[:], writes=[b_qw])
    p.dma("gp", brow[:], bo[:], writes=[b_brow])
    p.dma("gp", wv_sb[:].rearrange("p (j o) -> p j o", j=NCC),
          wvT.rearrange("(j p) o -> p j o", p=128), writes=[b_wv])
    p.op("gp", lambda q: q.memset(masks[:], 1.0), writes=[b_mask])
    for m in range(4):
        p.op("gp", lambda q, m=m: q.affine_select(
            out=masks[:, m * 512:(m + 1) * 512],
            in_=masks[:, m * 512:(m + 1) * 512],
            compare_op=ALU.is_ge, fill=0.0, base=-m * 128,
            pattern=[[1, 512]], channel_multiplier=-1),
            reads=[b_mask], writes=[b_mask])
    p.op("gp", lambda q: q.iota(ifr[:, 0:1], pattern=[[0, 1]], base=0,
                                channel_multiplier=1,
                                allow_small_or_imprecise_dtypes=True),
         writes=[b_ifr])
    p.op("ve", lambda q: q.tensor_scalar(out=ifr[:, 1:2], in0=ifr[:, 0:1],
                                         scalar1=63.5, scalar2=-64.0,
                                         op0=ALU.is_gt, op1=ALU.mult),
         reads=[b_ifr], writes=[b_ifr], flush=True)
    p.op("ve", lambda q: q.tensor_add(ifr[:, 0:1], ifr[:, 0:1], ifr[:, 1:2]),
         reads=[b_ifr], writes=[b_ifr], flush=True)
    p.op("ve", lambda q: q.tensor_scalar(out=ifr[:, 1:2], in0=ifr[:, 1:2],
                                         scalar1=1.0 / 32.0, scalar2=1.0,
                                         op0=ALU.mult, op1=ALU.add),
         reads=[b_ifr], writes=[b_ifr], flush=True)
    p.op("sc", lambda q: q.activation(out=ifr[:, 0:1], in_=ifr[:, 0:1],
                                      func=AF.Exp,
                                      scale=-math.log(10000.0) / 64),
         reads=[b_ifr], writes=[b_ifr], flush=True)
    frq = x_sb[:, 0:T]
    nsc = x_sb[:, T:2 * T]
    b_frq = Buf("frq"); b_nsc = Buf("nsc")
    p.op("gp", lambda q: q.iota(frq, pattern=[[1, T]], base=0,
                                channel_multiplier=0,
                                allow_small_or_imprecise_dtypes=True),
         writes=[b_frq])
    p.op("ve", lambda q: q.tensor_scalar(out=frq, in0=frq,
                                         scalar1=ifr[:, 0:1], scalar2=None,
                                         op0=ALU.mult),
         reads=[b_ifr], writes=[b_frq])
    p.op("ve", lambda q: q.tensor_scalar(out=nsc, in0=frq,
                                         scalar1=1.0 / TWO_PI, scalar2=MAGIC,
                                         op0=ALU.mult, op1=ALU.add),
         reads=[b_frq], writes=[b_nsc])
    p.op("ve", lambda q: q.tensor_scalar(out=nsc, in0=nsc,
                                         scalar1=MAGIC, scalar2=None,
                                         op0=ALU.subtract),
         reads=[], writes=[b_nsc])
    for cw in (CW1, CW2, CW3):
        p.op("ve", lambda q, cw=cw: q.scalar_tensor_tensor(
            out=frq, in0=nsc, scalar=-cw, in1=frq,
            op0=ALU.mult, op1=ALU.add),
            reads=[b_nsc], writes=[b_frq])
    p.op("ve", lambda q: q.tensor_scalar(out=frq, in0=frq,
                                         scalar1=math.pi, scalar2=-math.pi,
                                         op0=ALU.min, op1=ALU.max),
         reads=[], writes=[b_frq])
    p.op("sc", lambda q: q.activation(out=sinT[:], in_=frq, func=AF.Sin,
                                      scale=ifr[:, 1:2]),
         reads=[b_frq, b_ifr], writes=[b_sin])
    p.op("sc", lambda q: q.activation(out=frq, in_=frq, func=AF.Abs),
         reads=[b_frq], writes=[b_frq])
    p.op("ve", lambda q: q.tensor_scalar(out=frq, in0=frq,
                                         scalar1=-1.0, scalar2=math.pi / 2,
                                         op0=ALU.mult, op1=ALU.add),
         reads=[], writes=[b_frq])
    p.op("sc", lambda q: q.activation(out=cosT[:], in_=frq, func=AF.Sin),
         reads=[b_frq], writes=[b_cos])
    p.barrier()

    sums = stat[:, 0:NTT]; sumsq = stat[:, NTT:2 * NTT]
    acol = stat[:, 2 * NTT:3 * NTT]; bcol = stat[:, 3 * NTT:4 * NTT]

    def rms_a(ch, job, sl):
        (kind, ot, psrc, b_psrc) = job
        s5 = slice(sl * 512, (sl + 1) * 512)
        p.op("sc", lambda q: q.activation(out=q2b[:], in_=psrc,
                                          func=AF.Square),
             reads=[b_psrc], writes=[b_q2])
        p.op("sc", lambda q: q.activation(out=qf[:, s5], in_=psrc,
                                          func=AF.Identity),
             reads=[b_psrc], writes=[b_qf])
        p.op("te", lambda q: q.matmul(psZ[0:1, :], onesc[:], q2b[:],
                                      start=True, stop=True),
             reads=[b_onesc, b_q2], writes=[b_psZ])
        p.op("sc", lambda q: q.activation(out=rowsc[0:1, s5],
                                          in_=psZ[0:1, :], func=AF.Sqrt,
                                          scale=1.0 / D0),
             reads=[b_psZ], writes=[b_rows], flush=True)
        p.op("ve", lambda q: q.tensor_scalar(out=rowsc[0:1, s5],
                                             in0=rowsc[0:1, s5],
                                             scalar1=1e-6, scalar2=None,
                                             op0=ALU.add),
             reads=[], writes=[b_rows], flush=True)
        p.op("ve", lambda q: q.reciprocal(rowsc[0:1, s5], rowsc[0:1, s5]),
             reads=[b_rows], writes=[b_rows], flush=True)
        p.op("ve", lambda q: q.tensor_copy(rrbf[0:1, s5], rowsc[0:1, s5]),
             reads=[b_rows], writes=[b_rrb], flush=True)

    def rms_b(ch, job, sl):
        (kind, ot, psrc, b_psrc) = job
        s5 = slice(sl * 512, (sl + 1) * 512)
        csl = slice(ch * TC, (ch + 1) * TC)
        cs = cosT[:, csl]; sn = sinT[:, csl]
        wcol = qw_sb[:, 0:1] if kind == "q" else qw_sb[:, 1:2]
        p.op("te", lambda q: q.matmul(psR[:, :], onesr[:], rrbf[0:1, s5],
                                      start=True, stop=True),
             reads=[b_ones, b_rrb], writes=[b_psR])
        p.op("ve", lambda q: q.scalar_tensor_tensor(
            out=qn[:], in0=qf[:, s5], scalar=wcol, in1=psR[:, :],
            op0=ALU.mult, op1=ALU.mult),
            reads=[b_qf, b_qw, b_psR], writes=[b_qn])
        p.dma("sy", qsw[0:64, :], qn[64:128, :], reads=[b_qn], writes=[b_qsw])
        p.dma("sy", qsw[64:128, :], qn[0:64, :], reads=[b_qn], writes=[b_qsw])
        if kind == "q":
            dst, b_dst, dcol = qro, b_qro, (ot % 2) * 512
        else:
            dst, b_dst, dcol = k_sb, b_k, (ot * NCH + ch) * 512
        p.op("ve", lambda q: q.tensor_mul(rtmp[:, 0:512], qn[:, :], cs),
             reads=[b_qn, b_cos], writes=[b_rt])
        p.op("ve", lambda q: q.tensor_mul(rtmp[:, 512:1024], qsw[:, :], sn),
             reads=[b_qsw, b_sin], writes=[b_rt])
        p.op("ve", lambda q: q.tensor_add(dst[:, dcol:dcol + 512],
                                          rtmp[:, 0:512],
                                          rtmp[:, 512:1024]),
             reads=[b_rt], writes=[b_dst])
        if kind == "q":
            p.dma("sy", Qs[ot * 128:(ot + 1) * 128, ch * TC:(ch + 1) * TC],
                  qro[:, dcol:dcol + 512], reads=[b_qro])

    def v_epilogue(ch, job):
        (kind, i, psrc, b_psrc) = job
        p.op("ve", lambda q: q.tensor_copy(
            v_sb[:, (ch * NTT + i) * KV0:(ch * NTT + i + 1) * KV0], psrc),
            reads=[b_psrc], writes=[b_v])

    for ch in range(NCH):
        p.dma("sy", x_sb[:].rearrange("p (i c) -> p i c", i=NTT),
              x.rearrange("(ch i p) c -> ch p i c", ch=NCH, p=128)[ch],
              writes=[b_x])
        for i in range(NTT):
            emit_ln_stats(p, x_sb[:, i * C:(i + 1) * C], sq_scr, b_sq,
                          sums, sumsq, b_stat, b_stat, i, b_x, n=C)
        emit_ln_coeffs(p, sums, sumsq, b_stat, b_stat, acol, bcol, stat_t,
                       b_stat, b_stat, b_statt, NTT, epsc[:, 0:1], b_eps)
        for i in range(NTT):
            sl2 = i % 2
            p.op("sc", lambda q, i=i, sl2=sl2: q.activation(
                out=ln_tm[:, sl2 * C:(sl2 + 1) * C],
                in_=x_sb[:, i * C:(i + 1) * C], func=AF.Identity,
                scale=acol[:, i:i + 1], bias=bcol[:, i:i + 1]),
                reads=[b_x, b_stat], writes=[b_lntm[sl2]])
            for jg in range(2):
                for jj in range(8):
                    j = jg * 8 + jj
                    p.op("te", lambda q, j=j, jj=jj, sl2=sl2: q.transpose(
                        pt[:, jj * 128:(jj + 1) * 128],
                        ln_tm[:, sl2 * C + j * 128: sl2 * C + (j + 1) * 128],
                        ident[:]),
                        reads=[b_lntm[sl2], b_ident], writes=[b_pt],
                        inc=(jj == 7))
                dst = ln_fm[:].rearrange("p (j t) -> p j t", j=NCC)[
                    :, jg * 8:(jg + 1) * 8, i * 128:(i + 1) * 128]
                p.op("ve", lambda q, dst=dst: q.tensor_copy(
                    dst, pt[:].rearrange("p (a b) -> p a b", a=8)),
                    reads=[b_pt], writes=[b_lnfm])
        jobs = []
        for ot in range(H0):
            jobs.append(("q", ot, wqT))
        for ot in range(G0):
            jobs.append(("k", ot, wkT))
        for i in range(NTT):
            jobs.append(("v", i, None))
        em = []
        for jid, (kind, ot, wT_ap) in enumerate(jobs):
            slot, b_slot = mmslots[jid % 4]
            if kind in ("q", "k"):
                wb = jid % 2
                p.dma("gp", ws[wb][:].rearrange("p (j o) -> p j o", j=NCC),
                      wT_ap.rearrange("(j p) o -> p j o", p=128)[:, :, ot * 128:(ot + 1) * 128],
                      writes=[b_ws[wb]])
                for j in range(NCC):
                    p.op("te", lambda q, j=j, wb=wb, slot=slot: q.matmul(
                        slot, ws[wb][:, j * 128:(j + 1) * 128],
                        ln_fm[:, j * TC:(j + 1) * TC],
                        start=(j == 0), stop=(j == NCC - 1)),
                        reads=[b_ws[wb], b_lnfm], writes=[b_slot],
                        inc=(j == NCC - 1))
                em.append((kind, ot, slot, b_slot))
                rms_a(ch, em[-1], jid % 2)
            else:
                for j in range(NCC):
                    p.op("te", lambda q, j=j, ot=ot, slot=slot: q.matmul(
                        slot,
                        ln_fm[:, j * TC + ot * 128: j * TC + (ot + 1) * 128],
                        wv_sb[:, j * KV0:(j + 1) * KV0],
                        start=(j == 0), stop=(j == NCC - 1)),
                        reads=[b_wv, b_lnfm], writes=[b_slot],
                        inc=(j == NCC - 1))
                em.append((kind, ot, slot, b_slot))
            if len(em) >= 2:
                pj = em[-2]
                if pj[0] == "v":
                    v_epilogue(ch, pj)
                else:
                    rms_b(ch, pj, (jid - 1) % 2)
        pj = em[-1]
        if pj[0] == "v":
            v_epilogue(ch, pj)
        else:
            rms_b(ch, pj, (len(jobs) - 1) % 2)
    p.barrier()

    if debug:
        d_Q = io["d_Q"]; d_K = io["d_K"]; d_V = io["d_V"]; d_Y = io["d_Y"]
        d_qh = io["d_qh"]
        p.barrier()
        p.dma("gp", d_Q[:], Qs[:])
        p.dma("sy", d_K[:], k_sb[:], reads=[b_k])
        p.dma("sy", d_V[:], v_sb[:], reads=[b_v])
        p.barrier()

    qh_sb = ln_tm[:, 0:T]
    NQB = T // 512
    for h in range(H0):
        g = h // 4
        p.dma("sy", qh_sb, Qs[h * 128:(h + 1) * 128, :], writes=[b_qh])
        if debug and h == 0:
            p.dma("sy", d_qh[:], qh_sb, reads=[b_qh])
        for qb in range(NQB):
            ntk = 4 * (qb + 1)
            npr = ntk // 2
            prev = None

            def sm_pair(pr, bank, b_bank, pbuf):
                p.op("sc", lambda q, bank=bank: q.activation(
                    out=rtmp[:], in_=bank[:], func=AF.Tanh,
                    scale=RSQD / SCAP),
                    reads=[b_bank], writes=[b_rt])
                p.op("sc", lambda q, bank=bank, pbuf=pbuf: q.activation(
                    out=pt_sb[pbuf][:], in_=rtmp[:], func=AF.Exp,
                    scale=SCAP, bias=m50c[:, 0:1]),
                    reads=[b_rt, b_m50], writes=[b_ptb[pbuf]])
                for half in range(2):
                    kt = 2 * pr + half
                    m = kt - 4 * qb
                    if m >= 0:
                        p.op("ve", lambda q, half=half, m=m, pbuf=pbuf: q.tensor_mul(
                            pt_sb[pbuf][:, half * 512:(half + 1) * 512],
                            pt_sb[pbuf][:, half * 512:(half + 1) * 512],
                            masks[:, m * 512:(m + 1) * 512]),
                            reads=[b_ptb[pbuf], b_mask], writes=[b_ptb[pbuf]])
                for half in range(2):
                    kt = 2 * pr + half
                    p.op("te", lambda q, kt=kt, half=half, pbuf=pbuf, g=g,
                         ntk=ntk: q.matmul(
                        psY[:],
                        v_sb[:, kt * KV0 + g * 128: kt * KV0 + (g + 1) * 128],
                        pt_sb[pbuf][:, half * 512:(half + 1) * 512],
                        start=(kt == 0), stop=(kt == ntk - 1)),
                        reads=[b_v, b_ptb[pbuf]], writes=[b_psY],
                        inc=(kt == ntk - 1))
                    p.op("te", lambda q, kt=kt, half=half, pbuf=pbuf,
                         ntk=ntk: q.matmul(
                        psZ[0:1, :], onesc[:],
                        pt_sb[pbuf][:, half * 512:(half + 1) * 512],
                        start=(kt == 0), stop=(kt == ntk - 1)),
                        reads=[b_onesc, b_ptb[pbuf]], writes=[b_psZ],
                        inc=(kt == ntk - 1))

            for pr in range(npr):
                bank, b_bank = (psA, b_psA) if pr % 2 == 0 else (psB, b_psB)
                for half in range(2):
                    kt = 2 * pr + half
                    p.op("te", lambda q, g=g, kt=kt, qb=qb, bank=bank,
                         half=half: q.matmul(
                             bank[:, half * 512:(half + 1) * 512],
                             k_sb[:, g * T + kt * 128: g * T + (kt + 1) * 128],
                             qh_sb[:, qb * 512:(qb + 1) * 512],
                             start=True, stop=True),
                         reads=[b_k, b_qh], writes=[b_bank])
                if prev is not None:
                    sm_pair(*prev)
                prev = (pr, bank, b_bank, pr % 2)
            sm_pair(*prev)
            p.op("ve", lambda q: q.reciprocal(rowsc[0:1, 0:512], psZ[0:1, :]),
                 reads=[b_psZ], writes=[b_rows], flush=True)
            p.op("ve", lambda q: q.tensor_copy(rrbf[0:1, 0:512],
                                               rowsc[0:1, 0:512]),
                 reads=[b_rows], writes=[b_rrb], flush=True)
            p.op("te", lambda q: q.matmul(psR[:, :], onesr[:],
                                          rrbf[0:1, 0:512], start=True,
                                          stop=True),
                 reads=[b_ones, b_rrb], writes=[b_psR])
            p.op("sc", lambda q: q.activation(out=zb_sb[:], in_=psR[:, :],
                                              func=AF.Identity),
                 reads=[b_psR], writes=[b_zb])
            yb = qb % 2
            p.op("ve", lambda q, yb=yb: q.tensor_mul(
                yo_sb[:, yb * 512:(yb + 1) * 512], psY[:], zb_sb[:]),
                reads=[b_psY, b_zb], writes=[b_yo])
            p.dma("sy", Ys[h * 128:(h + 1) * 128, qb * 512:(qb + 1) * 512],
                  yo_sb[:, yb * 512:(yb + 1) * 512], reads=[b_yo])
    p.barrier()

    if debug:
        p.barrier()
        p.dma("gp", d_Y[:], Ys[:])
        p.barrier()

    b_yc = Buf("yc"); b_wo = [Buf("wo0"), Buf("wo1")]
    p3slots = [(psA[:, 0:512], b_psA), (psB[:, 0:512], b_psB),
               (psA[:, 512:1024], b_psA), (psB[:, 512:1024], b_psB)]
    for ch in range(NCH):
        p.dma("sy", x_sb[:].rearrange("p (i c) -> p i c", i=NTT),
              x.rearrange("(ch i p) c -> ch p i c", ch=NCH, p=128)[ch],
              writes=[b_x])
        p.dma("sy", ln_fm[:].rearrange("p (j t) -> p j t", j=NCC),
              Ys.rearrange("(j p) t -> p j t", p=128)[:, :, ch * TC:(ch + 1) * TC],
              writes=[b_yc])
        for ob in range(4):
            wb = ob % 2
            p.dma("gp", k_sb[:, wb * 8192: wb * 8192 + NCC * 512].rearrange("p (j o) -> p j o", j=NCC),
                  woT.rearrange("(j p) o -> p j o", p=128)[:, :, ob * 512:(ob + 1) * 512],
                  writes=[b_wo[wb]])
            for i in range(NTT):
                slot, b_slot = p3slots[(ob * NTT + i) % 4]
                p.op("te", lambda q, ob=ob, slot=slot: q.matmul(
                    slot, onesr[:],
                    brow[0:1, ob * 512:(ob + 1) * 512],
                    start=True, stop=False),
                    reads=[b_ones, b_brow], writes=[b_slot], inc=False)
                for j in range(NCC):
                    p.op("te", lambda q, i=i, j=j, wb=wb, slot=slot: q.matmul(
                        slot,
                        ln_fm[:, j * TC + i * 128: j * TC + (i + 1) * 128],
                        k_sb[:, wb * 8192 + j * 512: wb * 8192 + (j + 1) * 512],
                        start=False, stop=(j == NCC - 1)),
                        reads=[b_yc, b_wo[wb]], writes=[b_slot],
                        inc=(j == NCC - 1))
                xa = x_sb[:, i * C + ob * 512: i * C + (ob + 1) * 512]
                p.op("ve", lambda q, xa=xa, slot=slot: q.tensor_add(
                    xa, slot, xa),
                    reads=[b_slot], writes=[b_x])
        for i in range(NTT):
            p.dma("sy",
                  y.rearrange("(ch i p) c -> ch p i c", ch=NCH, p=128)[ch][:, i, :],
                  x_sb[:, i * C:(i + 1) * C], reads=[b_x])


def build_l0a(debug=False):
    nc = bass.Bass("TRN2", target_bir_lowering=False)
    x = nc.dram_tensor("x", [T, C], F32, kind="ExternalInput").ap()
    wqT = nc.dram_tensor("wqT", [C, C], F32, kind="ExternalInput").ap()
    wkT = nc.dram_tensor("wkT", [C, KV0], F32, kind="ExternalInput").ap()
    wvT = nc.dram_tensor("wvT", [C, KV0], F32, kind="ExternalInput").ap()
    woT = nc.dram_tensor("woT", [C, C], F32, kind="ExternalInput").ap()
    wqTb = nc.dram_tensor("wqTb", [C, C], BF16, kind="Internal").ap()
    if debug:
        dbgio = {
            "d_Q": nc.dram_tensor("d_Q", [C, T], BF16, kind="ExternalOutput").ap(),
            "d_K": nc.dram_tensor("d_K", [128, G0 * T], BF16, kind="ExternalOutput").ap(),
            "d_V": nc.dram_tensor("d_V", [128, (T // 128) * KV0], BF16, kind="ExternalOutput").ap(),
            "d_Y": nc.dram_tensor("d_Y", [C, T], BF16, kind="ExternalOutput").ap(),
            "d_qh": nc.dram_tensor("d_qh", [128, T], BF16, kind="ExternalOutput").ap(),
        }
    else:
        dbgio = {}
    wkTb = nc.dram_tensor("wkTb", [C, KV0], BF16, kind="Internal").ap()
    wvTb = nc.dram_tensor("wvTb", [C, KV0], BF16, kind="Internal").ap()
    woTb = nc.dram_tensor("woTb", [C, C], BF16, kind="Internal").ap()
    bo = nc.dram_tensor("bo", [1, C], F32, kind="ExternalInput").ap()
    qwc = nc.dram_tensor("qwc", [128, 1], F32, kind="ExternalInput").ap()
    kwc = nc.dram_tensor("kwc", [128, 1], F32, kind="ExternalInput").ap()
    y = nc.dram_tensor("y", [T, C], F32, kind="ExternalOutput").ap()
    Qs = nc.dram_tensor("Qs", [C, T], BF16, kind="Internal").ap()
    Ys = nc.dram_tensor("Ys", [C, T], BF16, kind="Internal").ap()
    io = {"x": x, "wqT": wqT, "wkT": wkT, "wvT": wvT, "woT": woT,
          "wqTb": wqTb, "wkTb": wkTb, "wvTb": wvTb, "woTb": woTb,
          "bo": bo, "qwc": qwc, "kwc": kwc, "y": y, "Qs": Qs, "Ys": Ys,
          **dbgio}
    with ExitStack() as stack:
        p = Prog(nc, stack)
        emit_convert(p, nc, [(wqT, wqTb), (wkT, wkTb), (wvT, wvTb),
                             (woT, woTb)])
        with ExitStack() as sstack:
            emit_l0a(p, nc, sstack, io, debug)
        p.wait_all("sy")
        block = stack.enter_context(nc.Block())
        p.finalize(block)
    return nc



H1 = 16
D1 = 128
R1 = 16
WIN = 512
LSC = 1.0 / 16.0


def emit_l1a(p, nc, stack, io, debug=False, upto=3):
    x = io["x"]
    wqT = io["wqTb"]
    wkT = io["wkTb"]
    wvT = io["wvTb"]
    woT = io["woTb"]
    qAT = io["qAT"]
    qBT = io["qBT"]
    kAT = io["kAT"]
    kBT = io["kBT"]
    vAT = io["vAT"]
    vBT = io["vBT"]
    oAT = io["oAT"]
    oBT = io["oBT"]
    nwc = io["nwc"]
    y = io["y"]
    Qs = io["Qs"]
    Ks = io["Ks"]
    Vs = io["Vs"]
    Ys = io["Ys"]
    Ls = io["Ls"]
    e = stack.enter_context
    RSQD1 = 1.0 / math.sqrt(D1)
    x_sb = e(nc.sbuf_tensor("a1_x_sb", [128, NTT * C], F32))
    sq_scr = e(nc.sbuf_tensor("a1_sq_scr", [128, C], F32))
    ln_tm = e(nc.sbuf_tensor("a1_ln_tm", [128, 2 * C], BF16))
    lnst = e(nc.sbuf_tensor("a1_lnst", [128, 2 * C], BF16))
    ln_fm = e(nc.sbuf_tensor("a1_ln_fm", [128, NCC * TC], BF16))
    kwin = e(nc.sbuf_tensor("a1_kwin", [128, H1 * 2 * 512], BF16))
    cosT = kwin[:, 0:T]
    sinT = kwin[:, T:2 * T]
    vwin = e(nc.sbuf_tensor("a1_vwin", [128, 2 * NTT * C], BF16))
    qblk = e(nc.sbuf_tensor("a1_qblk", [128, H1 * 512], BF16))
    ws = [e(nc.sbuf_tensor(f"a1_ws{i}", [128, NCC * 128], BF16)) for i in range(2)]
    lA = e(nc.sbuf_tensor("a1_lA", [128, 3 * NCC * R1], BF16))
    oA_sb = e(nc.sbuf_tensor("a1_oA_sb", [128, NCC * R1], BF16))
    lB = e(nc.sbuf_tensor("a1_lB", [R1, 4 * C], BF16))
    u_bf = e(nc.sbuf_tensor("a1_u_bf", [R1, 512], BF16))
    nw_sb = e(nc.sbuf_tensor("a1_nw_sb", [128, NCC], F32))
    ident = e(nc.sbuf_tensor("a1_ident", [128, 128], BF16))
    stat = e(nc.sbuf_tensor("a1_stat", [128, 6 * NTT], F32))
    stat_t = e(nc.sbuf_tensor("a1_stat_t", [128, NTT], F32))
    epsc = e(nc.sbuf_tensor("a1_epsc", [128, 1], F32))
    mrep = e(nc.sbuf_tensor("a1_mrep", [128, 2 * 512], BF16))
    qro = e(nc.sbuf_tensor("a1_qro", [128, 2 * 512], BF16))
    qn = e(nc.sbuf_tensor("a1_qn", [128, 512], F32))
    qsw = e(nc.sbuf_tensor("a1_qsw", [128, 512], F32))
    rtmp = e(nc.sbuf_tensor("a1_rtmp", [128, 1024], F32))
    pbf = e(nc.sbuf_tensor("a1_pbf", [128, 4 * 512], BF16))
    vro = pbf
    pbt = e(nc.sbuf_tensor("a1_pbt", [128, 4 * 128], BF16))
    ptb = e(nc.sbuf_tensor("a1_ptb", [128, 8 * 512], BF16))
    dgc = e(nc.sbuf_tensor("a1_dgc", [128, 4 * 128], BF16))
    mxq = e(nc.sbuf_tensor("a1_mxq", [128, 16], F32))
    zq = e(nc.sbuf_tensor("a1_zq", [128, 16], F32))
    ystage = e(nc.sbuf_tensor("a1_ystage", [128, 512], BF16))
    ifr = e(nc.sbuf_tensor("a1_ifr", [128, 2], F32))
    psA = e(nc.psum_tensor("a1_psA", [128, 2048], F32))
    psB = e(nc.psum_tensor("a1_psB", [128, 512], F32))
    psC = e(nc.psum_tensor("a1_psC", [128, 512], F32))
    pt = [e(nc.psum_tensor(f"a1_pt{i}", [128, 512], BF16)) for i in range(2)]
    b_psA = [Buf(f"psA{i}") for i in range(4)]
    b_psB = Buf("psB"); b_psC = Buf("psC")
    b_pt = [Buf("ptp0"), Buf("ptp1")]
    b_x = Buf("x")
    b_sq = [Buf("sq0")]; b_sq.append(b_sq[0])
    b_lntm = [Buf("ln_tm0"), Buf("ln_tm1")]
    b_lnst = [Buf("lnst0"), Buf("lnst1")]
    b_lnfm = Buf("ln_fm")
    b_cos = Buf("cos"); b_sin = Buf("sin")
    b_kw = [Buf("kw0"), Buf("kw1")]; b_vw = [Buf("vw0"), Buf("vw1")]
    b_qb = [Buf("qblk0"), Buf("qblk1")]
    b_ws = [Buf("ws0"), Buf("ws1")]
    b_lA = Buf("lA"); b_oA = Buf("oA"); b_lB = Buf("lB"); b_u = Buf("u")
    b_nw = Buf("nw"); b_ident = Buf("ident")
    b_stat = Buf("stat"); b_statt = Buf("stat_t"); b_eps = Buf("eps")
    b_mrep = Buf("mrep"); b_qro = Buf("qro"); b_vro = Buf("vro")
    b_qn = Buf("qn"); b_qsw = Buf("qsw"); b_rt = Buf("rt")
    b_pbf = Buf("pbf"); b_pbt = Buf("pbt"); b_ptb2 = Buf("ptb")
    b_dg = [Buf(f"dg{i}") for i in range(4)]
    b_mxq = Buf("mxq"); b_zq = Buf("zq")
    b_ys = Buf("ys"); b_ifr = Buf("ifr")

    emit_ident(p, ident, b_ident)
    p.op("gp", lambda q: q.memset(epsc[:], 1e-5), writes=[b_eps])
    p.dma("sy", nw_sb[:], nwc[:], writes=[b_nw])
    p.dma("gp", lA[:, 0:NCC * R1].rearrange("p (j r) -> p j r", j=NCC),
          qAT.rearrange("(j p) r -> p j r", p=128), writes=[b_lA])
    p.dma("gp", lA[:, NCC * R1:2 * NCC * R1].rearrange("p (j r) -> p j r", j=NCC),
          kAT.rearrange("(j p) r -> p j r", p=128), writes=[b_lA])
    p.dma("gp", lA[:, 2 * NCC * R1:3 * NCC * R1].rearrange("p (j r) -> p j r", j=NCC),
          vAT.rearrange("(j p) r -> p j r", p=128), writes=[b_lA])
    p.dma("gp", oA_sb[:].rearrange("p (j r) -> p j r", j=NCC),
          oAT.rearrange("(j p) r -> p j r", p=128), writes=[b_oA])
    p.dma("gp", lB[:, 0:C], qBT[:], writes=[b_lB])
    p.dma("gp", lB[:, C:2 * C], kBT[:], writes=[b_lB])
    p.dma("gp", lB[:, 2 * C:3 * C], vBT[:], writes=[b_lB])
    p.dma("gp", lB[:, 3 * C:4 * C], oBT[:], writes=[b_lB])
    p.op("gp", lambda q: q.memset(mrep[:], 1.0), writes=[b_mrep])
    for t in range(4):
        p.op("gp", lambda q, t=t: q.affine_select(
            out=mrep[:, t * 128:(t + 1) * 128],
            in_=mrep[:, t * 128:(t + 1) * 128], compare_op=ALU.is_ge,
            fill=0.0, base=-1, pattern=[[1, 128]], channel_multiplier=-1),
            reads=[b_mrep], writes=[b_mrep])
        p.op("gp", lambda q, t=t: q.affine_select(
            out=mrep[:, 512 + t * 128:512 + (t + 1) * 128],
            in_=mrep[:, 512 + t * 128:512 + (t + 1) * 128],
            compare_op=ALU.is_ge, fill=0.0, base=0, pattern=[[-1, 128]],
            channel_multiplier=1),
            reads=[b_mrep], writes=[b_mrep])
    p.op("gp", lambda q: q.iota(ifr[:, 0:1], pattern=[[0, 1]], base=0,
                                channel_multiplier=1,
                                allow_small_or_imprecise_dtypes=True),
         writes=[b_ifr])
    p.op("ve", lambda q: q.tensor_scalar(out=ifr[:, 1:2], in0=ifr[:, 0:1],
                                         scalar1=63.5, scalar2=-64.0,
                                         op0=ALU.is_gt, op1=ALU.mult),
         reads=[b_ifr], writes=[b_ifr], flush=True)
    p.op("ve", lambda q: q.tensor_add(ifr[:, 0:1], ifr[:, 0:1], ifr[:, 1:2]),
         reads=[b_ifr], writes=[b_ifr], flush=True)
    p.op("ve", lambda q: q.tensor_scalar(out=ifr[:, 1:2], in0=ifr[:, 1:2],
                                         scalar1=-1.0 / 32.0, scalar2=-1.0,
                                         op0=ALU.mult, op1=ALU.add),
         reads=[b_ifr], writes=[b_ifr], flush=True)
    p.op("sc", lambda q: q.activation(out=ifr[:, 0:1], in_=ifr[:, 0:1],
                                      func=AF.Exp,
                                      scale=-math.log(10000.0) / 64),
         reads=[b_ifr], writes=[b_ifr], flush=True)
    frq = x_sb[:, 0:T]
    nsc = x_sb[:, T:2 * T]
    b_frq = Buf("frq"); b_nsc = Buf("nsc")
    p.op("gp", lambda q: q.iota(frq, pattern=[[1, T]], base=0,
                                channel_multiplier=0,
                                allow_small_or_imprecise_dtypes=True),
         writes=[b_frq])
    p.op("ve", lambda q: q.tensor_scalar(out=frq, in0=frq,
                                         scalar1=ifr[:, 0:1], scalar2=None,
                                         op0=ALU.mult),
         reads=[b_ifr], writes=[b_frq])
    p.op("ve", lambda q: q.tensor_scalar(out=nsc, in0=frq,
                                         scalar1=1.0 / TWO_PI, scalar2=MAGIC,
                                         op0=ALU.mult, op1=ALU.add),
         reads=[b_frq], writes=[b_nsc])
    p.op("ve", lambda q: q.tensor_scalar(out=nsc, in0=nsc,
                                         scalar1=MAGIC, scalar2=None,
                                         op0=ALU.subtract),
         reads=[], writes=[b_nsc])
    for cw in (CW1, CW2, CW3):
        p.op("ve", lambda q, cw=cw: q.scalar_tensor_tensor(
            out=frq, in0=nsc, scalar=-cw, in1=frq,
            op0=ALU.mult, op1=ALU.add),
            reads=[b_nsc], writes=[b_frq])
    p.op("ve", lambda q: q.tensor_scalar(out=frq, in0=frq,
                                         scalar1=math.pi, scalar2=-math.pi,
                                         op0=ALU.min, op1=ALU.max),
         reads=[], writes=[b_frq])
    p.op("sc", lambda q: q.activation(out=sinT[:], in_=frq, func=AF.Sin,
                                      scale=ifr[:, 1:2]),
         reads=[b_frq, b_ifr], writes=[b_sin])
    p.op("sc", lambda q: q.activation(out=frq, in_=frq, func=AF.Abs),
         reads=[b_frq], writes=[b_frq])
    p.op("ve", lambda q: q.tensor_scalar(out=frq, in0=frq,
                                         scalar1=-1.0, scalar2=math.pi / 2,
                                         op0=ALU.mult, op1=ALU.add),
         reads=[], writes=[b_frq])
    p.op("sc", lambda q: q.activation(out=cosT[:], in_=frq, func=AF.Sin),
         reads=[b_frq], writes=[b_cos])
    p.barrier()

    sums = stat[:, 0:NTT]; sumsq = stat[:, NTT:2 * NTT]
    acol = stat[:, 2 * NTT:3 * NTT]; bcol = stat[:, 3 * NTT:4 * NTT]
    a2col = stat[:, 4 * NTT:5 * NTT]; b2col = stat[:, 5 * NTT:6 * NTT]

    def emit_rope1(ch, psrc, b_psrc, dst, b_dst, dcol):
        sl = slice(ch * TC, (ch + 1) * TC)
        cs = cosT[:, sl]; sn = sinT[:, sl]
        p.op("sc", lambda q: q.activation(out=qn[:], in_=psrc[:],
                                          func=AF.Identity),
             reads=[b_psrc], writes=[b_qn])
        p.dma("sy", qsw[0:64, :], qn[64:128, :], reads=[b_qn],
              writes=[b_qsw])
        p.dma("sy", qsw[64:128, :], qn[0:64, :], reads=[b_qn],
              writes=[b_qsw])
        p.op("ve", lambda q: q.tensor_mul(rtmp[:, 0:512], qn[:, :], cs),
             reads=[b_qn, b_cos], writes=[b_rt])
        p.op("ve", lambda q: q.tensor_mul(rtmp[:, 512:1024], qsw[:, :], sn),
             reads=[b_qsw, b_sin], writes=[b_rt])
        p.op("ve", lambda q: q.tensor_add(dst[:, dcol:dcol + 512],
                                          rtmp[:, 0:512],
                                          rtmp[:, 512:1024]),
             reads=[b_rt], writes=[b_dst])

    def emit_proj_head(wT_ap, lB_off, ot, psq, b_psq):
        wb = ot % 2
        p.dma("gp", ws[wb][:].rearrange("p (j o) -> p j o", j=NCC),
              wT_ap.rearrange("(j p) o -> p j o", p=128)[:, :, ot * 128:(ot + 1) * 128],
              writes=[b_ws[wb]])
        for j in range(NCC):
            p.op("te", lambda q, j=j, wb=wb, psq=psq: q.matmul(
                psq, ws[wb][:, j * 128:(j + 1) * 128],
                ln_fm[:, j * TC:(j + 1) * TC],
                start=(j == 0), stop=False),
                reads=[b_ws[wb], b_lnfm], writes=[b_psq], inc=False)
        p.op("te", lambda q, ot=ot, psq=psq, lB_off=lB_off: q.matmul(
            psq, lB[:, lB_off + ot * 128:lB_off + (ot + 1) * 128],
            u_bf[:], start=False, stop=True),
            reads=[b_lB, b_u], writes=[b_psq])

    for ch in range(NCH):
        p.dma("sy", x_sb[:].rearrange("p (i c) -> p i c", i=NTT),
              x.rearrange("(ch i p) c -> ch p i c", ch=NCH, p=128)[ch],
              writes=[b_x])
        for i in range(NTT):
            emit_ln_stats(p, x_sb[:, i * C:(i + 1) * C], sq_scr, b_sq,
                          sums, sumsq, b_stat, b_stat, i, b_x, n=C)
        emit_ln_coeffs(p, sums, sumsq, b_stat, b_stat, acol, bcol, stat_t,
                       b_stat, b_stat, b_statt, NTT, epsc[:, 0:1], b_eps)
        p.op("ve", lambda q: q.tensor_mul(a2col, acol, acol),
             reads=[b_stat], writes=[b_stat], flush=True)
        p.op("ve", lambda q: q.tensor_scalar(
            out=a2col, in0=a2col, scalar1=-1e-5, scalar2=1.0 + 1e-6,
            op0=ALU.mult, op1=ALU.add),
            reads=[b_stat], writes=[b_stat], flush=True)
        p.op("sc", lambda q: q.activation(out=a2col, in_=a2col,
                                          func=AF.Sqrt),
             reads=[b_stat], writes=[b_stat], flush=True)
        p.op("ve", lambda q: q.reciprocal(a2col, a2col),
             reads=[b_stat], writes=[b_stat], flush=True)
        p.op("ve", lambda q: q.tensor_mul(b2col, bcol, a2col),
             reads=[b_stat], writes=[b_stat], flush=True)
        p.op("ve", lambda q: q.tensor_mul(a2col, acol, a2col),
             reads=[b_stat], writes=[b_stat], flush=True)
        for i in range(NTT):
            sl2 = i % 2
            p.op("sc", lambda q, i=i, sl2=sl2: q.activation(
                out=lnst[:, sl2 * C:(sl2 + 1) * C],
                in_=x_sb[:, i * C:(i + 1) * C], func=AF.Identity,
                scale=acol[:, i:i + 1], bias=bcol[:, i:i + 1]),
                reads=[b_x, b_stat], writes=[b_lnst[sl2]])
            p.dma("sy",
                  Ls.rearrange("(ch i p) c -> ch p i c", ch=NCH, p=128)[ch][:, i, :],
                  lnst[:, sl2 * C:(sl2 + 1) * C], reads=[b_lnst[sl2]])
            p.op("sc", lambda q, i=i, sl2=sl2: q.activation(
                out=ln_tm[:, sl2 * C:(sl2 + 1) * C],
                in_=x_sb[:, i * C:(i + 1) * C], func=AF.Identity,
                scale=a2col[:, i:i + 1], bias=b2col[:, i:i + 1]),
                reads=[b_x, b_stat], writes=[b_lntm[sl2]])
            for jg in range(NCC // 4):
                pb = jg % 2
                for jj in range(4):
                    j = jg * 4 + jj
                    p.op("te", lambda q, j=j, jj=jj, pb=pb, sl2=sl2: q.transpose(
                        pt[pb][:, jj * 128:(jj + 1) * 128],
                        ln_tm[:, sl2 * C + j * 128: sl2 * C + (j + 1) * 128],
                        ident[:]),
                        reads=[b_lntm[sl2], b_ident], writes=[b_pt[pb]],
                        inc=(jj == 3))
                dst = ln_fm[:].rearrange("p (j t) -> p j t", j=NCC)[
                    :, jg * 4:(jg + 1) * 4, i * 128:(i + 1) * 128]
                p.op("ve", lambda q, pb=pb, dst=dst: q.tensor_copy(
                    dst, pt[pb][:].rearrange("p (a b) -> p a b", a=4)),
                    reads=[b_pt[pb]], writes=[b_lnfm])
        for j in range(NCC):
            p.op("ve", lambda q, j=j: q.tensor_scalar(
                out=ln_fm[:, j * TC:(j + 1) * TC],
                in0=ln_fm[:, j * TC:(j + 1) * TC],
                scalar1=nw_sb[:, j:j + 1], scalar2=None, op0=ALU.mult),
                reads=[b_nw], writes=[b_lnfm])
        for which, wT_ap, Aoff, Boff in (
                (0, wqT, 0, 0), (1, wkT, 1, C), (2, wvT, 2, 2 * C)):
            for j in range(NCC):
                p.op("te", lambda q, j=j, Aoff=Aoff: q.matmul(
                    psC[0:R1, :],
                    lA[:, Aoff * NCC * R1 + j * R1: Aoff * NCC * R1 + (j + 1) * R1],
                    ln_fm[:, j * TC:(j + 1) * TC],
                    start=(j == 0), stop=(j == NCC - 1)),
                    reads=[b_lA, b_lnfm], writes=[b_psC],
                    inc=(j == NCC - 1))
            p.op("sc", lambda q: q.activation(out=u_bf[:], in_=psC[0:R1, :],
                                              func=AF.Identity, scale=LSC),
                 reads=[b_psC], writes=[b_u])
            for ot in range(H1):
                qtr = ot % 4
                psq = psA[:, qtr * 512:(qtr + 1) * 512]
                b_psq = b_psA[qtr]
                emit_proj_head(wT_ap, Boff, ot, psq, b_psq)
                if which == 0:
                    qs = ot % 2
                    emit_rope1(ch, psq, b_psq, qro, b_qro, qs * 512)
                    p.dma("sy", Qs[ot * 128:(ot + 1) * 128,
                                   ch * TC:(ch + 1) * TC],
                          qro[:, qs * 512:(qs + 1) * 512], reads=[b_qro])
                elif which == 1:
                    qs = ot % 2
                    emit_rope1(ch, psq, b_psq, qro, b_qro, qs * 512)
                    p.dma("sy", Ks[ot * 128:(ot + 1) * 128,
                                   ch * TC:(ch + 1) * TC],
                          qro[:, qs * 512:(qs + 1) * 512], reads=[b_qro])
                else:
                    vs = ot % 2
                    p.op("sc", lambda q, vs=vs, psq=psq: q.activation(
                        out=qro[:, vs * 512:(vs + 1) * 512],
                        in_=psq, func=AF.Identity),
                        reads=[b_psq], writes=[b_qro])
                    for ii in range(NTT):
                        p.op("te", lambda q, ii=ii, vs=vs: q.transpose(
                            pt[vs][:, ii * 128:(ii + 1) * 128],
                            qro[:, vs * 512 + ii * 128: vs * 512 + (ii + 1) * 128],
                            ident[:]),
                            reads=[b_qro, b_ident], writes=[b_pt[vs]],
                            inc=(ii == NTT - 1))
                    p.op("ve", lambda q, vs=vs: q.tensor_copy(
                        vro[:, vs * 512:(vs + 1) * 512], pt[vs][:]),
                        reads=[b_pt[vs]], writes=[b_vro])
                    p.dma("sy",
                          Vs[ch * TC:(ch + 1) * TC, ot * 128:(ot + 1) * 128].rearrange("(q p) c -> p q c", p=128),
                          vro[:, vs * 512:(vs + 1) * 512].rearrange("p (q c) -> p q c", q=NTT),
                          reads=[b_vro])
    p.barrier()
    if upto <= 1:
        return

    kw4 = kwin[:].rearrange("p (h f t) -> p h f t", h=H1, f=2)
    pbf3 = pbf[:].rearrange("p (i w) -> p i w", i=4)
    pbt3 = pbt[:].rearrange("p (i w) -> p i w", i=4)
    ptb4 = ptb[:].rearrange("p (j i d) -> p j i d", j=8, i=4)
    p.op("gp", lambda q: q.memset(ptb[:], 0.0), writes=[b_ptb2])
    p.op("gp", lambda q: q.memset(kwin[:], 0.0), writes=[b_kw[0], b_kw[1]])
    p.op("gp", lambda q: q.memset(vwin[:], 0.0), writes=[b_vw[0], b_vw[1]])
    for qb in range(NCH):
        kb = qb % 2
        ob2 = (qb + 1) % 2
        p.dma("sy", qblk[:, 0:8 * 512].rearrange("p (h t) -> p h t", h=8),
              Qs.rearrange("(h p) t -> p h t", p=128)[:, 0:8, qb * TC:(qb + 1) * TC],
              writes=[b_qb[0]])
        p.dma("sy", qblk[:, 8 * 512:].rearrange("p (h t) -> p h t", h=8),
              Qs.rearrange("(h p) t -> p h t", p=128)[:, 8:16, qb * TC:(qb + 1) * TC],
              writes=[b_qb[1]])
        p.dma("sy", kw4[:, :, kb, :],
              Ks.rearrange("(h p) t -> p h t", p=128)[:, :, qb * TC:(qb + 1) * TC],
              writes=[b_kw[kb]])
        p.dma("sy", vwin[:, kb * NTT * C:(kb + 1) * NTT * C].rearrange("p (q c) -> p q c", q=NTT),
              Vs[qb * TC:(qb + 1) * TC, :].rearrange("(q p) c -> p q c", p=128),
              writes=[b_vw[kb]])
        for h in range(H1):
            bq = b_qb[h // 8]
            hv = h % 4

            for i in range(4):
                n_old = 4 - i
                qt = qblk[:, h * 512 + i * 128:h * 512 + (i + 1) * 128]
                p.op("te", lambda q, i=i, n_old=n_old, h=h, ob2=ob2, qt=qt: q.matmul(
                    psA[:, i * 512:i * 512 + n_old * 128],
                    qt,
                    kwin[:, h * 1024 + ob2 * 512 + i * 128:
                          h * 1024 + ob2 * 512 + (i + n_old) * 128],
                    start=True, stop=True),
                    reads=[bq, b_kw[ob2]], writes=[b_psA[i]])
                if i > 0:
                    p.op("te", lambda q, i=i, n_old=n_old, h=h, kb=kb, qt=qt: q.matmul(
                        psA[:, i * 512 + n_old * 128:(i + 1) * 512],
                        qt,
                        kwin[:, h * 1024 + kb * 512:
                              h * 1024 + kb * 512 + i * 128],
                        start=True, stop=True),
                        reads=[bq, b_kw[kb]], writes=[b_psA[i]])
                p.op("te", lambda q, i=i, h=h, kb=kb, qt=qt: q.matmul(
                    psC[:, i * 128:(i + 1) * 128],
                    qt,
                    kwin[:, h * 1024 + kb * 512 + i * 128:
                          h * 1024 + kb * 512 + (i + 1) * 128],
                    start=True, stop=True),
                    reads=[bq, b_kw[kb]], writes=[b_psC])
            if debug and qb == 2 and h == 0:
                p.dma("sy", io["d_qblk"][:], qblk[:], reads=[b_qb[0], b_qb[1]])
                p.dma("sy", io["d_kwin"][:], kwin[:],
                      reads=[b_kw[0], b_kw[1]])
                p.op("ve", lambda q: q.tensor_copy(rtmp[:], psA[:, 0:1024]),
                     reads=[b_psA[0], b_psA[1]], writes=[b_rt])
                p.dma("sy", io["d_sc"][:, 0:1024], rtmp[:], reads=[b_rt])
                p.op("ve", lambda q: q.tensor_copy(rtmp[:], psA[:, 1024:2048]),
                     reads=[b_psA[2], b_psA[3]], writes=[b_rt])
                p.dma("sy", io["d_sc"][:, 1024:2048], rtmp[:], reads=[b_rt])
                p.op("ve", lambda q: q.tensor_copy(rtmp[:, 0:512], psC[:]),
                     reads=[b_psC], writes=[b_rt])
                p.dma("sy", io["d_sct"][:], rtmp[:, 0:512], reads=[b_rt])
            p.op("ve", lambda q: q.reduce_max(
                mxq[:, 0:4], psA[:].rearrange("p (i w) -> p i w", i=4),
                axis=AX.X),
                reads=b_psA, writes=[b_mxq], inc=False)
            p.op("ve", lambda q: q.reduce_max(
                mxq[:, 4:8], psC[:].rearrange("p (i w) -> p i w", i=4),
                axis=AX.X),
                reads=[b_psC], writes=[b_mxq], flush=True)
            p.op("ve", lambda q: q.tensor_max(mxq[:, 8:12], mxq[:, 0:4],
                                              mxq[:, 4:8]),
                 reads=[b_mxq], writes=[b_mxq], flush=True)
            p.op("ve", lambda q: q.tensor_scalar(
                out=mxq[:, 12:16], in0=mxq[:, 8:12], scalar1=-RSQD1,
                scalar2=None, op0=ALU.mult),
                reads=[b_mxq], writes=[b_mxq], flush=True)
            for i in range(4):
                p.op("sc", lambda q, i=i: q.activation(
                    out=pbf[:, i * 512:(i + 1) * 512],
                    in_=psA[:, i * 512:(i + 1) * 512], func=AF.Exp,
                    scale=RSQD1, bias=mxq[:, 12 + i:13 + i]),
                    reads=[b_psA[i], b_mxq], writes=[b_pbf])
            for i in range(4):
                p.op("sc", lambda q, i=i: q.activation(
                    out=pbt[:, i * 128:(i + 1) * 128],
                    in_=psC[:, i * 128:(i + 1) * 128], func=AF.Exp,
                    scale=RSQD1, bias=mxq[:, 12 + i:13 + i]),
                    reads=[b_psC, b_mxq], writes=[b_pbt])
            if qb > 0:
                p.op("ve", lambda q: q.tensor_mul(
                    pbf3[:, :, 0:128], pbf3[:, :, 0:128],
                    mrep[:, 0:512].rearrange("p (i w) -> p i w", i=4)),
                    reads=[b_mrep], writes=[b_pbf])
            else:
                for i in range(4):
                    p.op("ve", lambda q, i=i: q.tensor_scalar(
                        out=pbf[:, i * 512:i * 512 + (4 - i) * 128],
                        in0=pbf[:, i * 512:i * 512 + (4 - i) * 128],
                        scalar1=0.0, scalar2=None, op0=ALU.mult),
                        reads=[], writes=[b_pbf])
            p.op("ve", lambda q: q.tensor_mul(
                pbt[:], pbt[:], mrep[:, 512:1024]),
                reads=[b_mrep], writes=[b_pbt])
            p.op("ve", lambda q: q.reduce_sum(zq[:, 0:4], pbf3, axis=AX.X),
                 reads=[b_pbf], writes=[b_zq], inc=False)
            p.op("ve", lambda q: q.reduce_sum(zq[:, 4:8], pbt3, axis=AX.X),
                 reads=[b_pbt], writes=[b_zq], flush=True)
            p.op("ve", lambda q: q.tensor_add(zq[:, 8:12], zq[:, 0:4],
                                              zq[:, 4:8]),
                 reads=[b_zq], writes=[b_zq], flush=True)
            p.op("ve", lambda q: q.reciprocal(zq[:, 12:16], zq[:, 8:12]),
                 reads=[b_zq], writes=[b_zq], flush=True)
            for i in range(4):
                p.op("ve", lambda q, i=i: q.tensor_scalar(
                    out=dgc[:, i * 128:(i + 1) * 128], in0=ident[:],
                    scalar1=zq[:, 12 + i:13 + i], scalar2=None,
                    op0=ALU.mult),
                    reads=[b_ident, b_zq], writes=[b_dg[i]])
                for t in range(4):
                    src = pbf[:, i * 512 + t * 128:i * 512 + (t + 1) * 128]
                    p.op("te", lambda q, t=t, i=i, src=src: q.matmul(
                        psB[:, t * 128:(t + 1) * 128], src,
                        dgc[:, i * 128:(i + 1) * 128],
                        start=True, stop=True),
                        reads=[b_pbf, b_dg[i]], writes=[b_psB],
                        inc=(t == 3))
                p.op("te", lambda q, i=i: q.matmul(
                    psC[:, i * 128:(i + 1) * 128],
                    pbt[:, i * 128:(i + 1) * 128],
                    dgc[:, i * 128:(i + 1) * 128],
                    start=True, stop=True),
                    reads=[b_pbt, b_dg[i]], writes=[b_psC])
                p.op("ve", lambda q, i=i: q.tensor_copy(
                    ptb4[:, i:i + 4, i, :],
                    psB[:].rearrange("p (a b) -> p a b", a=4)),
                    reads=[b_psB], writes=[b_ptb2])
                p.op("ve", lambda q, i=i: q.tensor_copy(
                    ptb4[:, i + 4, i, :], psC[:, i * 128:(i + 1) * 128]),
                    reads=[b_psC], writes=[b_ptb2])
            if debug and qb == 2 and h == 0:
                p.dma("sy", io["d_pbf"][:], pbf[:], reads=[b_pbf])
                p.dma("sy", io["d_pbt"][:], pbt[:], reads=[b_pbt])
                p.dma("sy", io["d_ptb"][:], ptb[:], reads=[b_ptb2])
                p.dma("sy", io["d_zq"][:], zq[:], reads=[b_zq])
                p.dma("sy", io["d_mxq"][:], mxq[:], reads=[b_mxq])
            for j in range(8):
                half = kb if j >= 4 else ob2
                tile = j % 4
                vcol = half * NTT * C + tile * C + h * 128
                p.op("te", lambda q, j=j, vcol=vcol, hv=hv: q.matmul(
                    psA[:, hv * 512:(hv + 1) * 512],
                    vwin[:, vcol:vcol + 128],
                    ptb[:, j * 512:(j + 1) * 512],
                    start=(j == 0), stop=(j == 7)),
                    reads=[b_vw[0], b_vw[1], b_ptb2], writes=[b_psA[hv]],
                    inc=(j == 7))
            p.op("ve", lambda q, hv=hv: q.tensor_copy(
                ystage[:], psA[:, hv * 512:(hv + 1) * 512]),
                reads=[b_psA[hv]], writes=[b_ys])
            p.dma("sy", Ys[h * 128:(h + 1) * 128, qb * TC:(qb + 1) * TC],
                  ystage[:], reads=[b_ys])
    p.barrier()
    if upto <= 2:
        return

    b_yc = Buf("yc"); b_wo = [Buf("wo0"), Buf("wo1")]
    b_ls = Buf("lsc")
    for ch in range(NCH):
        p.dma("sy", x_sb[:].rearrange("p (i c) -> p i c", i=NTT),
              x.rearrange("(ch i p) c -> ch p i c", ch=NCH, p=128)[ch],
              writes=[b_x])
        p.dma("sy", ln_fm[:].rearrange("p (j t) -> p j t", j=NCC),
              Ys.rearrange("(j p) t -> p j t", p=128)[:, :, ch * TC:(ch + 1) * TC],
              writes=[b_yc])
        p.dma("sy", lnst[:, 0:2 * C].rearrange("p (i c) -> p i c", i=2),
              Ls.rearrange("(hh i p) c -> hh p i c", hh=2 * NCH, p=128)[2 * ch],
              writes=[b_ls])
        p.dma("sy", kwin[:, 0:2 * C].rearrange("p (i c) -> p i c", i=2),
              Ls.rearrange("(hh i p) c -> hh p i c", hh=2 * NCH, p=128)[2 * ch + 1],
              writes=[b_ls])
        for j in range(NCC):
            p.op("te", lambda q, j=j: q.matmul(
                psC[0:R1, :],
                oA_sb[:, j * R1:(j + 1) * R1],
                ln_fm[:, j * TC:(j + 1) * TC],
                start=(j == 0), stop=(j == NCC - 1)),
                reads=[b_oA, b_yc], writes=[b_psC],
                inc=(j == NCC - 1))
        p.op("sc", lambda q: q.activation(out=u_bf[:], in_=psC[0:R1, :],
                                          func=AF.Identity, scale=LSC),
             reads=[b_psC], writes=[b_u])
        for ob in range(4):
            wb = ob % 2
            p.dma("gp", vwin[:, wb * 8192: wb * 8192 + NCC * 512].rearrange("p (j o) -> p j o", j=NCC),
                  woT.rearrange("(j p) o -> p j o", p=128)[:, :, ob * 512:(ob + 1) * 512],
                  writes=[b_wo[wb]])
            for i in range(NTT):
                qtr = (ob * NTT + i) % 4
                psq = psA[:, qtr * 512:(qtr + 1) * 512]
                b_psq = b_psA[qtr]
                for j in range(NCC):
                    p.op("te", lambda q, i=i, j=j, wb=wb, psq=psq: q.matmul(
                        psq,
                        ln_fm[:, j * TC + i * 128: j * TC + (i + 1) * 128],
                        vwin[:, wb * 8192 + j * 512: wb * 8192 + (j + 1) * 512],
                        start=(j == 0), stop=False),
                        reads=[b_yc, b_wo[wb]], writes=[b_psq],
                        inc=False)
                p.op("te", lambda q, i=i, ob=ob, psq=psq: q.matmul(
                    psq,
                    u_bf[:, i * 128:(i + 1) * 128],
                    lB[:, 3 * C + ob * 512: 3 * C + (ob + 1) * 512],
                    start=False, stop=True),
                    reads=[b_u, b_lB], writes=[b_psq])
                xa = x_sb[:, i * C + ob * 512: i * C + (ob + 1) * 512]
                la = (lnst[:, (i % 2) * C + ob * 512:(i % 2) * C + (ob + 1) * 512]
                      if i < 2 else
                      kwin[:, (i % 2) * C + ob * 512:(i % 2) * C + (ob + 1) * 512])
                bla = b_ls
                p.op("ve", lambda q, xa=xa, psq=psq: q.tensor_add(
                    xa, psq, xa),
                    reads=[b_psq], writes=[b_x])
                p.op("ve", lambda q, xa=xa, la=la: q.tensor_add(xa, la, xa),
                     reads=[bla], writes=[b_x])
        for i in range(NTT):
            p.dma("sy",
                  y.rearrange("(ch i p) c -> ch p i c", ch=NCH, p=128)[ch][:, i, :],
                  x_sb[:, i * C:(i + 1) * C], reads=[b_x])


def build_l1a(debug=False, upto=3):
    nc = bass.Bass("TRN2", target_bir_lowering=False)
    x = nc.dram_tensor("x", [T, C], F32, kind="ExternalInput").ap()
    wqT = nc.dram_tensor("wqT", [C, C], F32, kind="ExternalInput").ap()
    wkT = nc.dram_tensor("wkT", [C, C], F32, kind="ExternalInput").ap()
    wvT = nc.dram_tensor("wvT", [C, C], F32, kind="ExternalInput").ap()
    woT = nc.dram_tensor("woT", [C, C], F32, kind="ExternalInput").ap()
    qAT = nc.dram_tensor("qAT", [C, R1], F32, kind="ExternalInput").ap()
    qBT = nc.dram_tensor("qBT", [R1, C], F32, kind="ExternalInput").ap()
    kAT = nc.dram_tensor("kAT", [C, R1], F32, kind="ExternalInput").ap()
    kBT = nc.dram_tensor("kBT", [R1, C], F32, kind="ExternalInput").ap()
    vAT = nc.dram_tensor("vAT", [C, R1], F32, kind="ExternalInput").ap()
    vBT = nc.dram_tensor("vBT", [R1, C], F32, kind="ExternalInput").ap()
    oAT = nc.dram_tensor("oAT", [C, R1], F32, kind="ExternalInput").ap()
    oBT = nc.dram_tensor("oBT", [R1, C], F32, kind="ExternalInput").ap()
    nwc = nc.dram_tensor("nwc", [128, NCC], F32, kind="ExternalInput").ap()
    wqTb = nc.dram_tensor("wqTb", [C, C], BF16, kind="Internal").ap()
    if debug:
        dbgio = {
            "d_sc": nc.dram_tensor("d_sc", [128, 2048], F32, kind="ExternalOutput").ap(),
            "d_sct": nc.dram_tensor("d_sct", [128, 512], F32, kind="ExternalOutput").ap(),
            "d_pbf": nc.dram_tensor("d_pbf", [128, 2048], BF16, kind="ExternalOutput").ap(),
            "d_pbt": nc.dram_tensor("d_pbt", [128, 512], BF16, kind="ExternalOutput").ap(),
            "d_ptb": nc.dram_tensor("d_ptb", [128, 4096], BF16, kind="ExternalOutput").ap(),
            "d_zq": nc.dram_tensor("d_zq", [128, 16], F32, kind="ExternalOutput").ap(),
            "d_mxq": nc.dram_tensor("d_mxq", [128, 16], F32, kind="ExternalOutput").ap(),
            "d_qblk": nc.dram_tensor("d_qblk", [128, H1 * 512], BF16, kind="ExternalOutput").ap(),
            "d_kwin": nc.dram_tensor("d_kwin", [128, H1 * 1024], BF16, kind="ExternalOutput").ap(),
        }
    else:
        dbgio = {}
    wkTb = nc.dram_tensor("wkTb", [C, C], BF16, kind="Internal").ap()
    wvTb = nc.dram_tensor("wvTb", [C, C], BF16, kind="Internal").ap()
    woTb = nc.dram_tensor("woTb", [C, C], BF16, kind="Internal").ap()
    y = nc.dram_tensor("y", [T, C], F32, kind="ExternalOutput").ap()
    Qs = nc.dram_tensor("Qs", [C, T], BF16, kind="Internal").ap()
    Ks = nc.dram_tensor("Ks", [C, T], BF16, kind="Internal").ap()
    Vs = nc.dram_tensor("Vs", [T, C], BF16, kind="Internal").ap()
    Ys = nc.dram_tensor("Ys", [C, T], BF16, kind="Internal").ap()
    Ls = nc.dram_tensor("Ls", [T, C], BF16, kind="Internal").ap()
    io = {"x": x, "wqT": wqT, "wkT": wkT, "wvT": wvT, "woT": woT,
          "wqTb": wqTb, "wkTb": wkTb, "wvTb": wvTb, "woTb": woTb,
          "qAT": qAT, "qBT": qBT, "kAT": kAT, "kBT": kBT, "vAT": vAT,
          "vBT": vBT, "oAT": oAT, "oBT": oBT, "nwc": nwc, "y": y,
          "Qs": Qs, "Ks": Ks, "Vs": Vs, "Ys": Ys, "Ls": Ls, **dbgio}
    with ExitStack() as stack:
        p = Prog(nc, stack)
        emit_convert(p, nc, [(wqT, wqTb), (wkT, wkTb), (wvT, wvTb),
                             (woT, woTb)])
        with ExitStack() as sstack:
            emit_l1a(p, nc, sstack, io, debug, upto)
        p.wait_all("sy")
        block = stack.enter_context(nc.Block())
        p.finalize(block)
    return nc



E1 = 8
F1 = 2048


def emit_l1f(p, nc, stack, io, debug=False):
    x = io["x"]
    wr = io["wr"]
    wg = io["wg"]
    wu = io["wu"]
    wd = io["wd"]
    y = io["y"]
    Wgb = io["Wgb"]
    Wub = io["Wub"]
    Wdb = io["Wdb"]
    e = stack.enter_context
    x_sb = e(nc.sbuf_tensor("f1_x_sb", [128, NTT * C], F32))
    acc = e(nc.sbuf_tensor("f1_acc", [128, NTT * C], F32))
    sq_scr = e(nc.sbuf_tensor("f1_sq_scr", [128, C], F32))
    ln_tm = e(nc.sbuf_tensor("f1_ln_tm", [128, 2 * C], BF16))
    ln_fm = e(nc.sbuf_tensor("f1_ln_fm", [128, NCC * TC], BF16))
    h_sb = e(nc.sbuf_tensor("f1_h_sb", [128, NCC * TC], BF16))
    wgs = [e(nc.sbuf_tensor(f"f1_wgs{i}", [128, NCC * 128], BF16)) for i in range(2)]
    wus = [e(nc.sbuf_tensor(f"f1_wus{i}", [128, NCC * 128], BF16)) for i in range(2)]
    wds = [e(nc.sbuf_tensor(f"f1_wds{i}", [128, NCC * 512], BF16)) for i in range(2)]
    wr_sb = e(nc.sbuf_tensor("f1_wr_sb", [128, NCC * E1], BF16))
    gs_bf = e(nc.sbuf_tensor("f1_gs_bf", [128, 512], BF16))
    wcols = e(nc.sbuf_tensor("f1_wcols", [128, NTT * E1], F32))
    er8 = e(nc.sbuf_tensor("f1_er8", [128, 2 * E1], F32))
    mxc = e(nc.sbuf_tensor("f1_mxc", [128, 4], F32))
    ident = e(nc.sbuf_tensor("f1_ident", [128, 128], BF16))
    stat = e(nc.sbuf_tensor("f1_stat", [128, 4 * NTT], F32))
    stat_t = e(nc.sbuf_tensor("f1_stat_t", [128, NTT], F32))
    epsc = e(nc.sbuf_tensor("f1_epsc", [128, 1], F32))
    ps = [e(nc.psum_tensor(f"f1_ps{i}", [128, 512], F32)) for i in range(6)]
    pt = [e(nc.psum_tensor(f"f1_pt{i}", [128, 512], BF16)) for i in range(2)]
    b_ps = [Buf(f"ps{i}") for i in range(6)]
    b_pt = [Buf(f"pt{i}") for i in range(2)]
    b_x = Buf("x"); b_acc = Buf("acc")
    b_sq = [Buf("sq0")]; b_sq.append(b_sq[0])
    b_lntm = [Buf("ln_tm0"), Buf("ln_tm1")]; b_lnfm = Buf("ln_fm")
    b_h = Buf("h")
    b_wg = [Buf("wg0"), Buf("wg1")]; b_wu = [Buf("wu0"), Buf("wu1")]
    b_wd = [Buf("wd0"), Buf("wd1")]
    b_wr = Buf("wr"); b_gs = Buf("gs"); b_wc = Buf("wc"); b_er = Buf("er")
    b_mx = Buf("mx"); b_ident = Buf("ident")
    b_stat = Buf("stat"); b_statt = Buf("stat_t"); b_eps = Buf("eps")

    emit_ident(p, ident, b_ident)
    p.op("gp", lambda q: q.memset(epsc[:], 1e-5), writes=[b_eps])
    p.dma("gp", wr_sb[:].rearrange("p (j e) -> p j e", j=NCC),
          wr.rearrange("(j p) e -> p j e", p=128), writes=[b_wr])
    emit_convert(p, nc, [(wg[0], Wgb[0]), (wu[0], Wub[0]), (wd[0], Wdb[0])])

    sums = stat[:, 0:NTT]; sumsq = stat[:, NTT:2 * NTT]
    acol = stat[:, 2 * NTT:3 * NTT]; bcol = stat[:, 3 * NTT:4 * NTT]
    NFT = F1 // 128
    for ch in range(NCH):
        p.dma("sy", x_sb[:].rearrange("p (i c) -> p i c", i=NTT),
              x.rearrange("(ch i p) c -> ch p i c", ch=NCH, p=128)[ch],
              writes=[b_x])
        for i in range(NTT):
            emit_ln_stats(p, x_sb[:, i * C:(i + 1) * C], sq_scr, b_sq,
                          sums, sumsq, b_stat, b_stat, i, b_x, n=C)
        emit_ln_coeffs(p, sums, sumsq, b_stat, b_stat, acol, bcol, stat_t,
                       b_stat, b_stat, b_statt, NTT, epsc[:, 0:1], b_eps)
        for i in range(NTT):
            sl2 = i % 2
            p.op("sc", lambda q, i=i, sl2=sl2: q.activation(
                out=ln_tm[:, sl2 * C:(sl2 + 1) * C],
                in_=x_sb[:, i * C:(i + 1) * C], func=AF.Identity,
                scale=acol[:, i:i + 1], bias=bcol[:, i:i + 1]),
                reads=[b_x, b_stat], writes=[b_lntm[sl2]])
            emit_transpose_tile(p, ident, b_ident,
                                ln_tm[:, sl2 * C:(sl2 + 1) * C],
                                b_lntm[sl2], ln_fm, b_lnfm, i, pt, b_pt)
        for i in range(NTT):
            for j in range(NCC):
                p.op("te", lambda q, i=i, j=j: q.matmul(
                    ps[4][:, i * E1:(i + 1) * E1],
                    ln_fm[:, j * TC + i * 128: j * TC + (i + 1) * 128],
                    wr_sb[:, j * E1:(j + 1) * E1],
                    start=(j == 0), stop=(j == NCC - 1)),
                    reads=[b_lnfm, b_wr], writes=[b_ps[4]],
                    inc=(j == NCC - 1))
            sl8 = slice(0, E1)
            p.op("ve", lambda q, i=i: q.reduce_max(
                mxc[:, 0:1], ps[4][:, i * E1:(i + 1) * E1], axis=AX.X),
                reads=[b_ps[4]], writes=[b_mx], flush=True)
            p.op("ve", lambda q: q.tensor_scalar(
                out=mxc[:, 1:2], in0=mxc[:, 0:1], scalar1=-1.0,
                scalar2=None, op0=ALU.mult),
                reads=[b_mx], writes=[b_mx], flush=True)
            p.op("sc", lambda q, i=i: q.activation(
                out=er8[:, 0:E1], in_=ps[4][:, i * E1:(i + 1) * E1],
                func=AF.Exp, bias=mxc[:, 1:2]),
                reads=[b_ps[4], b_mx], writes=[b_er])
            p.op("ve", lambda q: q.reduce_max(mxc[:, 0:1], er8[:, 0:E1],
                                              axis=AX.X),
                 reads=[b_er], writes=[b_mx], flush=True)
            p.op("ve", lambda q: q.tensor_scalar(
                out=er8[:, E1:2 * E1], in0=er8[:, 0:E1],
                scalar1=mxc[:, 0:1], scalar2=-1e30,
                op0=ALU.is_ge, op1=ALU.mult),
                reads=[b_er, b_mx], writes=[b_er], flush=True)
            p.op("ve", lambda q: q.tensor_add(er8[:, E1:2 * E1],
                                              er8[:, 0:E1],
                                              er8[:, E1:2 * E1]),
                 reads=[b_er], writes=[b_er], flush=True)
            p.op("ve", lambda q: q.reduce_max(mxc[:, 1:2],
                                              er8[:, E1:2 * E1], axis=AX.X),
                 reads=[b_er], writes=[b_mx], flush=True)
            p.op("ve", lambda q: q.tensor_add(mxc[:, 2:3], mxc[:, 0:1],
                                              mxc[:, 1:2]),
                 reads=[b_mx], writes=[b_mx], flush=True)
            p.op("ve", lambda q: q.reciprocal(mxc[:, 2:3], mxc[:, 2:3]),
                 reads=[b_mx], writes=[b_mx], flush=True)
            p.op("ve", lambda q: q.tensor_scalar(
                out=er8[:, E1:2 * E1], in0=er8[:, 0:E1],
                scalar1=mxc[:, 1:2], scalar2=None, op0=ALU.is_ge),
                reads=[b_er, b_mx], writes=[b_er], flush=True)
            p.op("ve", lambda q: q.tensor_mul(er8[:, E1:2 * E1],
                                              er8[:, 0:E1],
                                              er8[:, E1:2 * E1]),
                 reads=[b_er], writes=[b_er], flush=True)
            p.op("ve", lambda q, i=i: q.tensor_scalar(
                out=wcols[:, i * E1:(i + 1) * E1], in0=er8[:, E1:2 * E1],
                scalar1=mxc[:, 2:3], scalar2=None, op0=ALU.mult),
                reads=[b_er, b_mx], writes=[b_wc], flush=True)
        for ei in range(E1):
            if ch == 0 and ei + 1 < E1:
                emit_convert(p, nc, [(wg[ei + 1], Wgb[ei + 1]),
                                     (wu[ei + 1], Wub[ei + 1]),
                                     (wd[ei + 1], Wdb[ei + 1])])
            for ft in range(NFT):
                fb = ft % 2
                p.dma("gp", wgs[fb][:].rearrange("p (j o) -> p j o", j=NCC),
                      Wgb[ei].rearrange("(j p) o -> p j o", p=128)[:, :, ft * 128:(ft + 1) * 128],
                      writes=[b_wg[fb]])
                p.dma("gp", wus[fb][:].rearrange("p (j o) -> p j o", j=NCC),
                      Wub[ei].rearrange("(j p) o -> p j o", p=128)[:, :, ft * 128:(ft + 1) * 128],
                      writes=[b_wu[fb]])
                for j in range(NCC):
                    p.op("te", lambda q, j=j, fb=fb: q.matmul(
                        ps[0][:], wgs[fb][:, j * 128:(j + 1) * 128],
                        ln_fm[:, j * TC:(j + 1) * TC],
                        start=(j == 0), stop=(j == NCC - 1)),
                        reads=[b_wg[fb], b_lnfm], writes=[b_ps[0]],
                        inc=(j == NCC - 1))
                for j in range(NCC):
                    p.op("te", lambda q, j=j, fb=fb: q.matmul(
                        ps[1][:], wus[fb][:, j * 128:(j + 1) * 128],
                        ln_fm[:, j * TC:(j + 1) * TC],
                        start=(j == 0), stop=(j == NCC - 1)),
                        reads=[b_wu[fb], b_lnfm], writes=[b_ps[1]],
                        inc=(j == NCC - 1))
                p.op("sc", lambda q: q.activation(out=gs_bf[:],
                                                  in_=ps[0][:],
                                                  func=AF.Silu),
                     reads=[b_ps[0]], writes=[b_gs])
                p.op("ve", lambda q, ft=ft: q.tensor_mul(
                    h_sb[:, ft * 512:(ft + 1) * 512], gs_bf[:], ps[1][:]),
                    reads=[b_gs, b_ps[1]], writes=[b_h])
            for ob in range(4):
                wb = ob % 2
                p.dma("gp", wds[wb][:].rearrange("p (j o) -> p j o", j=NCC),
                      Wdb[ei].rearrange("(j p) o -> p j o", p=128)[:, :, ob * 512:(ob + 1) * 512],
                      writes=[b_wd[wb]])
                for i in range(NTT):
                    bank = 2 + (ob * NTT + i) % 2
                    for j in range(NCC):
                        p.op("te", lambda q, i=i, j=j, wb=wb, bank=bank: q.matmul(
                            ps[bank][:],
                            h_sb[:, j * TC + i * 128: j * TC + (i + 1) * 128],
                            wds[wb][:, j * 512:(j + 1) * 512],
                            start=(j == 0), stop=(j == NCC - 1)),
                            reads=[b_h, b_wd[wb]], writes=[b_ps[bank]],
                            inc=(j == NCC - 1))
                    aa = acc[:, i * C + ob * 512: i * C + (ob + 1) * 512]
                    wcol = wcols[:, i * E1 + ei:i * E1 + ei + 1]
                    if ei == 0:
                        p.op("ve", lambda q, aa=aa, wcol=wcol, bank=bank: q.tensor_scalar(
                            out=aa, in0=ps[bank][:], scalar1=wcol,
                            scalar2=None, op0=ALU.mult),
                            reads=[b_ps[bank], b_wc], writes=[b_acc])
                    else:
                        p.op("ve", lambda q, aa=aa, wcol=wcol, bank=bank: q.scalar_tensor_tensor(
                            out=aa, in0=ps[bank][:], scalar=wcol, in1=aa,
                            op0=ALU.mult, op1=ALU.add),
                            reads=[b_ps[bank], b_wc], writes=[b_acc])
        for i in range(NTT):
            for ob in range(4):
                xa = x_sb[:, i * C + ob * 512: i * C + (ob + 1) * 512]
                aa = acc[:, i * C + ob * 512: i * C + (ob + 1) * 512]
                p.op("ve", lambda q, xa=xa, aa=aa: q.tensor_add(xa, aa, xa),
                     reads=[b_acc], writes=[b_x])
            p.dma("sy",
                  y.rearrange("(ch i p) c -> ch p i c", ch=NCH, p=128)[ch][:, i, :],
                  x_sb[:, i * C:(i + 1) * C], reads=[b_x])


def build_l1f(debug=False):
    nc = bass.Bass("TRN2", target_bir_lowering=False)
    x = nc.dram_tensor("x", [T, C], F32, kind="ExternalInput").ap()
    wr = nc.dram_tensor("wr", [C, E1], F32, kind="ExternalInput").ap()
    wg = nc.dram_tensor("wg", [E1, C, F1], F32, kind="ExternalInput").ap()
    wu = nc.dram_tensor("wu", [E1, C, F1], F32, kind="ExternalInput").ap()
    wd = nc.dram_tensor("wd", [E1, F1, C], F32, kind="ExternalInput").ap()
    y = nc.dram_tensor("y", [T, C], F32, kind="ExternalOutput").ap()
    Wgb = nc.dram_tensor("Wgb", [E1, C, F1], BF16, kind="Internal").ap()
    Wub = nc.dram_tensor("Wub", [E1, C, F1], BF16, kind="Internal").ap()
    Wdb = nc.dram_tensor("Wdb", [E1, F1, C], BF16, kind="Internal").ap()
    io = {"x": x, "wr": wr, "wg": wg, "wu": wu, "wd": wd, "y": y,
          "Wgb": Wgb, "Wub": Wub, "Wdb": Wdb}
    with ExitStack() as stack:
        p = Prog(nc, stack)
        with ExitStack() as sstack:
            emit_l1f(p, nc, sstack, io, debug)
        p.wait_all("sy")
        block = stack.enter_context(nc.Block())
        p.finalize(block)
    return nc



def build_fused():
    nc = bass.Bass("TRN2", target_bir_lowering=False)
    D = lambda n, s, dt, k: nc.dram_tensor(n, s, dt, kind=k).ap()
    x = D("x", [T, C], F32, "ExternalInput")
    y = D("y", [T, C], F32, "ExternalOutput")
    io0a = {
        "x": x,
        "wqT": D("a0_wqT", [C, C], F32, "ExternalInput"),
        "wkT": D("a0_wkT", [C, KV0], F32, "ExternalInput"),
        "wvT": D("a0_wvT", [C, KV0], F32, "ExternalInput"),
        "woT": D("a0_woT", [C, C], F32, "ExternalInput"),
        "wqTb": D("a0_wqTb", [C, C], BF16, "Internal"),
        "wkTb": D("a0_wkTb", [C, KV0], BF16, "Internal"),
        "wvTb": D("a0_wvTb", [C, KV0], BF16, "Internal"),
        "woTb": D("a0_woTb", [C, C], BF16, "Internal"),
        "bo": D("a0_bo", [1, C], F32, "ExternalInput"),
        "qwc": D("a0_qwc", [128, 1], F32, "ExternalInput"),
        "kwc": D("a0_kwc", [128, 1], F32, "ExternalInput"),
        "y": D("x1", [T, C], F32, "Internal"),
        "Qs": D("a0_Qs", [C, T], BF16, "Internal"),
        "Ys": D("a0_Ys", [C, T], BF16, "Internal"),
    }
    io0f = {
        "x": io0a["y"],
        "wT": D("f0_wT", [C, C], F32, "ExternalInput"),
        "bias": D("f0_bias", [1, C], F32, "ExternalInput"),
        "y": D("x2", [T, C], F32, "Internal"),
    }
    io1a = {
        "x": io0f["y"],
        "wqT": D("a1_wqT", [C, C], F32, "ExternalInput"),
        "wkT": D("a1_wkT", [C, C], F32, "ExternalInput"),
        "wvT": D("a1_wvT", [C, C], F32, "ExternalInput"),
        "woT": D("a1_woT", [C, C], F32, "ExternalInput"),
        "wqTb": D("a1_wqTb", [C, C], BF16, "Internal"),
        "wkTb": D("a1_wkTb", [C, C], BF16, "Internal"),
        "wvTb": D("a1_wvTb", [C, C], BF16, "Internal"),
        "woTb": D("a1_woTb", [C, C], BF16, "Internal"),
        "qAT": D("a1_qAT", [C, R1], F32, "ExternalInput"),
        "qBT": D("a1_qBT", [R1, C], F32, "ExternalInput"),
        "kAT": D("a1_kAT", [C, R1], F32, "ExternalInput"),
        "kBT": D("a1_kBT", [R1, C], F32, "ExternalInput"),
        "vAT": D("a1_vAT", [C, R1], F32, "ExternalInput"),
        "vBT": D("a1_vBT", [R1, C], F32, "ExternalInput"),
        "oAT": D("a1_oAT", [C, R1], F32, "ExternalInput"),
        "oBT": D("a1_oBT", [R1, C], F32, "ExternalInput"),
        "nwc": D("a1_nwc", [128, NCC], F32, "ExternalInput"),
        "y": D("x3", [T, C], F32, "Internal"),
        "Qs": D("a1_Qs", [C, T], BF16, "Internal"),
        "Ks": D("a1_Ks", [C, T], BF16, "Internal"),
        "Vs": D("a1_Vs", [T, C], BF16, "Internal"),
        "Ys": D("a1_Ys", [C, T], BF16, "Internal"),
        "Ls": D("a1_Ls", [T, C], BF16, "Internal"),
    }
    io1f = {
        "x": io1a["y"],
        "wr": D("f1_wr", [C, E1], F32, "ExternalInput"),
        "wg": D("f1_wg", [E1, C, F1], F32, "ExternalInput"),
        "wu": D("f1_wu", [E1, C, F1], F32, "ExternalInput"),
        "wd": D("f1_wd", [E1, F1, C], F32, "ExternalInput"),
        "y": y,
        "Wgb": D("f1_Wgb", [E1, C, F1], BF16, "Internal"),
        "Wub": D("f1_Wub", [E1, C, F1], BF16, "Internal"),
        "Wdb": D("f1_Wdb", [E1, F1, C], BF16, "Internal"),
    }
    with ExitStack() as stack:
        p = Prog(nc, stack)
        pairs0 = [(io0a["wqT"], io0a["wqTb"]), (io0a["wkT"], io0a["wkTb"]),
                  (io0a["wvT"], io0a["wvTb"]), (io0a["woT"], io0a["woTb"])]
        pairs1 = [(io1a["wqT"], io1a["wqTb"]), (io1a["wkT"], io1a["wkTb"]),
                  (io1a["wvT"], io1a["wvTb"]), (io1a["woT"], io1a["woTb"])]
        emit_convert(p, nc, pairs0)
        with ExitStack() as s1:
            emit_l0a(p, nc, s1, io0a)
        emit_convert(p, nc, pairs1)
        p.barrier()
        with ExitStack() as s2:
            emit_l0f(p, nc, s2, io0f)
        p.barrier()
        with ExitStack() as s3:
            emit_l1a(p, nc, s3, io1a)
        p.barrier()
        with ExitStack() as s4:
            emit_l1f(p, nc, s4, io1f)
        p.wait_all("sy")
        block = stack.enter_context(nc.Block())
        p.finalize(block)
    return nc



_NC_CACHE = {}


def _get_nc(name):
    if name not in _NC_CACHE:
        _NC_CACHE[name] = {"l0a": build_l0a, "l0f": build_l0f,
                           "l1a": build_l1a, "l1f": build_l1f,
                           "fused": build_fused}[name]()
    return _NC_CACHE[name]


def kernel(**inputs):
    gi = lambda k: np.ascontiguousarray(np.asarray(inputs[k], dtype=np.float32))
    x = gi("L0_attn_x")
    cores = list(range(N_CORES))
    w = {
        "a0_wqT": np.ascontiguousarray(gi("L0_attn_Wq").T),
        "a0_wkT": np.ascontiguousarray(gi("L0_attn_Wk").T),
        "a0_wvT": np.ascontiguousarray(gi("L0_attn_Wv").T),
        "a0_woT": np.ascontiguousarray(gi("L0_attn_Wo").T),
        "a0_bo": gi("L0_attn_bo").reshape(1, C),
        "a0_qwc": gi("L0_attn_q_norm_w").reshape(128, 1),
        "a0_kwc": gi("L0_attn_k_norm_w").reshape(128, 1),
        "f0_wT": np.ascontiguousarray(gi("L0_ffn_weight").T),
        "f0_bias": gi("L0_ffn_bias").reshape(1, C),
        "a1_wqT": np.ascontiguousarray(gi("L1_attn_wq").T),
        "a1_wkT": np.ascontiguousarray(gi("L1_attn_wk").T),
        "a1_wvT": np.ascontiguousarray(gi("L1_attn_wv").T),
        "a1_woT": np.ascontiguousarray(gi("L1_attn_wo").T),
        "a1_qAT": np.ascontiguousarray(gi("L1_attn_qA").T),
        "a1_qBT": np.ascontiguousarray(gi("L1_attn_qB").T),
        "a1_kAT": np.ascontiguousarray(gi("L1_attn_kA").T),
        "a1_kBT": np.ascontiguousarray(gi("L1_attn_kB").T),
        "a1_vAT": np.ascontiguousarray(gi("L1_attn_vA").T),
        "a1_vBT": np.ascontiguousarray(gi("L1_attn_vB").T),
        "a1_oAT": np.ascontiguousarray(gi("L1_attn_oA").T),
        "a1_oBT": np.ascontiguousarray(gi("L1_attn_oB").T),
        "a1_nwc": np.ascontiguousarray(gi("L1_attn_norm_w").reshape(NCC, 128).T),
        "f1_wr": gi("L1_ffn_Wr"),
        "f1_wg": gi("L1_ffn_Wg"),
        "f1_wu": gi("L1_ffn_Wu"),
        "f1_wd": gi("L1_ffn_Wd"),
    }
    res = run_bass_kernel_spmd(_get_nc("fused"),
                               [dict(w, x=x[i]) for i in cores], cores)
    out = np.stack([res.results[i]["y"] for i in cores], axis=0)
    return out.astype(np.float32)


# version 08
# speedup vs baseline: 1.1736x; 1.1736x over previous
import math
import numpy as np
import concourse.bass as bass
import concourse.mybir as mybir
from concourse.bass_utils import run_bass_kernel_spmd
from contextlib import ExitStack

F32 = mybir.dt.float32
BF16 = mybir.dt.bfloat16
AF = mybir.ActivationFunctionType
ALU = mybir.AluOpType
AX = mybir.AxisListType

MAGIC = 3 * 2.0**22
TWO_PI = 2 * math.pi
CW1 = float(np.float32(6.28125))
CW2 = float(np.float32(TWO_PI - 6.28125))
CW3 = float(TWO_PI - CW1 - float(np.float32(TWO_PI - 6.28125)))

T = 4096
C = 2048
TC = 512
NTT = TC // 128
NCH = T // TC
NCC = C // 128

N_CORES = 8



class Buf:
    __slots__ = ("name", "writer", "readers")

    def __init__(self, name=""):
        self.name = name
        self.writer = None
        self.readers = []


class Prog:
    ENGINES = ("te", "ve", "sc", "gp", "sy")
    _SECT = {"te": "tensor", "ve": "vector", "sc": "scalar",
             "gp": "gpsimd", "sy": "sync"}

    def __init__(self, nc, stack):
        self.nc = nc
        self.q = {"te": nc.tensor, "ve": nc.vector, "sc": nc.scalar,
                  "gp": nc.gpsimd, "sy": nc.sync}
        self.sem = {}
        for e in self.ENGINES:
            self.sem[e] = stack.enter_context(nc.semaphore(f"s_{e}"))
            self.sem["d" + e] = stack.enter_context(nc.semaphore(f"sd_{e}"))
        self.cnt = {k: 0 for k in self.sem}
        self.wm = {e: {} for e in self.ENGINES}
        self.pending = {e: [] for e in self.ENGINES}
        self.unresolved = {e: [] for e in self.ENGINES}
        self.n_ins = 0

    def op(self, eng, fn, reads=(), writes=(), deps=(), inc=True, dma=False,
           flush=False):
        toks = list(deps)
        for b in reads:
            if b.writer is not None:
                toks.append(b.writer)
        for b in writes:
            if b.writer is not None:
                toks.append(b.writer)
            toks.extend(b.readers)
        best = {}
        for t in toks:
            k, v = t[0], t[1]
            if k == eng:
                continue
            if v is None:
                raise RuntimeError(f"{eng} dep on un-inc'd op on {k}")
            if best.get(k, -1) < v:
                best[k] = v
        for k in sorted(best):
            v = best[k]
            if self.wm[eng].get(k, 0) < v:
                self.pending[eng].append(("w", k, v))
                self.wm[eng][k] = v
        key = ("d" + eng) if dma else eng
        if flush:
            self.pending[eng].append(("i", fn, None, 0))
            self.cnt[key] += 1
            tok = [key, self.cnt[key]]
            self.pending[eng].append(("i", lambda q: q.drain(), key, 1))
            for ent in self.unresolved[eng]:
                ent[0] = key
                ent[1] = self.cnt[key]
            self.unresolved[eng] = []
            self.n_ins += 2
            for b in reads:
                b.readers.append(tok)
            for b in writes:
                b.writer = tok
                b.readers = []
            return tok
        if inc:
            amt = 16 if dma else 1
            self.cnt[key] += amt
            tok = [key, self.cnt[key]]
            self.pending[eng].append(("i", fn, key, amt))
            if not dma:
                for ent in self.unresolved[eng]:
                    ent[0] = key
                    ent[1] = self.cnt[key]
                self.unresolved[eng] = []
        else:
            tok = [key, None]
            self.pending[eng].append(("i", fn, None, 0))
            self.unresolved[eng].append(tok)
        self.n_ins += 1
        for b in reads:
            b.readers.append(tok)
        for b in writes:
            b.writer = tok
            b.readers = []
        return tok

    def dma(self, eng, out, in_, reads=(), writes=(), deps=(), **kw):
        return self.op(eng, lambda q: q.dma_start(out=out, in_=in_, **kw),
                       reads=reads, writes=writes, deps=deps, dma=True)

    def tok(self, eng):
        return (eng, self.cnt[eng])

    def barrier(self, skip=()):
        snap = dict(self.cnt)
        for e in self.ENGINES:
            for k, v in snap.items():
                if v > 0 and k != e and k not in skip and self.wm[e].get(k, 0) < v:
                    self.pending[e].append(("w", k, v))
                    self.wm[e][k] = v

    def wait_all(self, eng):
        for k, v in list(self.cnt.items()):
            if v > 0 and k != eng and self.wm[eng].get(k, 0) < v:
                self.pending[eng].append(("w", k, v))
                self.wm[eng][k] = v

    def finalize(self, block):
        for eng in self.ENGINES:
            items = self.pending[eng]
            if not items:
                continue

            def emit(q, items=items):
                for it in items:
                    if it[0] == "w":
                        q.wait_ge(self.sem[it[1]], it[2])
                    else:
                        ins = it[1](q)
                        if it[2] is not None:
                            ins.then_inc(self.sem[it[2]], it[3])

            getattr(block, self._SECT[eng])(emit)



def emit_ident(p, ident, b_ident):
    p.op("gp", lambda q: q.memset(ident[:], 0.0), writes=[b_ident])
    p.op("gp", lambda q: q.affine_select(
        out=ident[:], in_=ident[:], compare_op=ALU.not_equal, fill=1.0,
        base=0, pattern=[[-1, 128]], channel_multiplier=1),
        reads=[b_ident], writes=[b_ident])


def emit_ln_stats(p, x_ap, sq_scr, b_sq, sums, sumsq, b_sums, b_sumsq, i,
                  b_x, eps=1e-5, n=C):
    nsl = sq_scr.shape[1] // n
    sl = i % nsl
    scr = sq_scr[:, sl * n:(sl + 1) * n]
    p.op("ve", lambda q: q.reduce_sum(sums[:, i:i + 1], x_ap, axis=AX.X),
         reads=[b_x], writes=[b_sums], flush=True)
    p.op("sc", lambda q: q.activation(out=scr, in_=x_ap, func=AF.Square),
         reads=[b_x], writes=[b_sq[sl]])
    p.op("ve", lambda q: q.reduce_sum(sumsq[:, i:i + 1], scr, axis=AX.X),
         reads=[b_sq[sl]], writes=[b_sumsq], flush=True)


def emit_ln_coeffs(p, sums, sumsq, b_sums, b_sumsq, acol, bcol, tmp,
                   b_a, b_b, b_tmp, ncols, eps_col=None, b_eps=None, n=C):
    s = slice(0, ncols)
    p.op("ve", lambda q: q.tensor_scalar(out=tmp[:, s], in0=sums[:, s],
                                         scalar1=1.0 / n, scalar2=None,
                                         op0=ALU.mult),
         reads=[b_sums], writes=[b_tmp], flush=True)
    p.op("ve", lambda q: q.tensor_mul(acol[:, s], tmp[:, s], tmp[:, s]),
         reads=[b_tmp], writes=[b_a], flush=True)
    p.op("ve", lambda q: q.scalar_tensor_tensor(
        out=acol[:, s], in0=sumsq[:, s], scalar=1.0 / n, in1=acol[:, s],
        op0=ALU.mult, op1=ALU.subtract),
        reads=[b_sumsq, b_a], writes=[b_a], flush=True)
    p.op("sc", lambda q: q.activation(out=acol[:, s], in_=acol[:, s],
                                      func=AF.Sqrt, bias=eps_col),
         reads=[b_a, b_eps], writes=[b_a], flush=True)
    p.op("ve", lambda q: q.reciprocal(acol[:, s], acol[:, s]),
         reads=[b_a], writes=[b_a], flush=True)
    p.op("ve", lambda q: q.scalar_tensor_tensor(
        out=bcol[:, s], in0=tmp[:, s], scalar=-1.0, in1=acol[:, s],
        op0=ALU.mult, op1=ALU.mult),
        reads=[b_tmp, b_a], writes=[b_b], flush=True)


def emit_transpose_chunk(p, ident, b_ident, ln_tm, b_lntm, ln_fm, b_lnfm,
                         pt, b_pt, ntt=NTT, tc=TC, ncc=NCC, c=C):
    for j in range(ncc):
        pb = j % 2
        for i in range(ntt):
            src = ln_tm[:, i * c + j * 128: i * c + (j + 1) * 128]
            dst = pt[pb][:, i * 128:(i + 1) * 128]
            rd = b_lntm[i] if isinstance(b_lntm, list) else b_lntm
            p.op("te", lambda q, src=src, dst=dst: q.transpose(dst, src, ident[:]),
                 reads=[rd, b_ident], writes=[b_pt[pb]], inc=(i == ntt - 1))
        p.op("ve", lambda q, j=j, pb=pb: q.tensor_copy(
            ln_fm[:, j * tc: j * tc + ntt * 128], pt[pb][:, 0:ntt * 128]),
            reads=[b_pt[pb]], writes=[b_lnfm])



def emit_l0f(p, nc, stack, io, debug=False):
    x = io["x"]
    wT = io["wT"]
    bias = io["bias"]
    y = io["y"]
    e = stack.enter_context
    wt_bf = e(nc.sbuf_tensor("f0_wt_bf", [128, NCC * C], BF16))
    wq_f = [e(nc.sbuf_tensor(f"f0_wq_f{i}", [128, C], F32)) for i in range(2)]
    x_sb = [e(nc.sbuf_tensor(f"f0_x_sb{i}", [128, NTT * C], F32)) for i in range(2)]
    ln_tm = e(nc.sbuf_tensor("f0_ln_tm", [128, NTT * C], BF16))
    ln_fm = e(nc.sbuf_tensor("f0_ln_fm", [128, NCC * TC], BF16))
    sq_scr = e(nc.sbuf_tensor("f0_sq_scr", [128, 2 * C], F32))
    ident = e(nc.sbuf_tensor("f0_ident", [128, 128], BF16))
    onesr = e(nc.sbuf_tensor("f0_onesr", [1, 128], BF16))
    epsc = e(nc.sbuf_tensor("f0_epsc", [128, 1], F32))
    stat = e(nc.sbuf_tensor("f0_stat", [128, 4 * NTT], F32))
    stat_t = e(nc.sbuf_tensor("f0_stat_t", [128, NTT], F32))
    absr = e(nc.sbuf_tensor("f0_absr", [128, 18], F32))
    srow = e(nc.sbuf_tensor("f0_srow", [1, 130], F32))
    scol = e(nc.sbuf_tensor("f0_scol", [128, 1], F32))
    brow = e(nc.sbuf_tensor("f0_brow", [1, C], BF16))
    ps = [e(nc.psum_tensor(f"f0_ps{i}", [128, 512], F32)) for i in range(4)]
    pt = [e(nc.psum_tensor(f"f0_pt{i}", [128, 512], BF16)) for i in range(2)]
    b_ps = [Buf(f"ps{i}") for i in range(4)]
    b_pt = [Buf(f"pt{i}") for i in range(2)]
    b_wt = Buf("wt"); b_wq = [Buf("wq0"), Buf("wq1")]
    b_x = [Buf("x0"), Buf("x1")]
    b_lntm = Buf("ln_tm"); b_lnfm = Buf("ln_fm")
    b_sq = [Buf("sq0"), Buf("sq1")]
    b_ident = Buf("ident"); b_ones = Buf("ones")
    b_stat = Buf("stat"); b_statt = Buf("stat_t")
    b_abs = Buf("abs"); b_srow = Buf("srow"); b_scol = Buf("scol")
    b_brow = Buf("brow")

    b_eps = Buf("eps")
    emit_ident(p, ident, b_ident)
    p.op("gp", lambda q: q.memset(onesr[:], 1.0), writes=[b_ones])
    p.op("gp", lambda q: q.memset(epsc[:], 1e-5), writes=[b_eps])

    for k in range(16):
        pb = k % 2
        p.dma("sy", wq_f[pb][:],
              wT.rearrange("(k p) c -> k p c", k=16, p=128)[k],
              writes=[b_wq[pb]])
        p.op("sc", lambda q, k=k, pb=pb: q.activation(
            out=sq_scr[:, (k % 2) * C:(k % 2 + 1) * C], in_=wq_f[pb][:],
            func=AF.Abs),
            reads=[b_wq[pb]], writes=[b_sq[k % 2]])
        p.op("ve", lambda q, k=k: q.reduce_sum(
            absr[:, k:k + 1], sq_scr[:, (k % 2) * C:(k % 2 + 1) * C],
            axis=AX.X),
            reads=[b_sq[k % 2]], writes=[b_abs], flush=True)
    p.op("ve", lambda q: q.reduce_sum(absr[:, 16:17], absr[:, 0:16], axis=AX.X),
         reads=[b_abs], writes=[b_abs], flush=True)
    pcol = ps[3]
    idf = sq_scr
    p.op("gp", lambda q: q.memset(idf[:, 0:128], 0.0), writes=[b_sq[0]])
    p.op("gp", lambda q: q.affine_select(
        out=idf[:, 0:128], in_=idf[:, 0:128], compare_op=ALU.not_equal,
        fill=1.0, base=0, pattern=[[-1, 128]], channel_multiplier=1),
        reads=[b_sq[0]], writes=[b_sq[0]])
    p.op("te", lambda q: q.transpose(pcol[0:1, 0:128], absr[:, 16:17],
                                     idf[:, 0:128]),
         reads=[b_abs, b_sq[0]], writes=[b_ps[3]])
    p.op("ve", lambda q: q.tensor_copy(srow[0:1, 0:128], pcol[0:1, 0:128]),
         reads=[b_ps[3]], writes=[b_srow], flush=True)
    p.op("ve", lambda q: q.reduce_sum(srow[0:1, 128:129], srow[0:1, 0:128],
                                      axis=AX.X),
         reads=[b_srow], writes=[b_srow], flush=True)
    p.op("ve", lambda q: q.tensor_scalar(out=srow[0:1, 128:129],
                                         in0=srow[0:1, 128:129],
                                         scalar1=1.0 / (C * C),
                                         scalar2=None, op0=ALU.mult),
         reads=[b_srow], writes=[b_srow], flush=True)
    p.op("ve", lambda q: q.tensor_scalar(out=srow[0:1, 129:130],
                                         in0=srow[0:1, 128:129],
                                         scalar1=1e-10, scalar2=None,
                                         op0=ALU.add),
         reads=[b_srow], writes=[b_srow], flush=True)
    p.op("ve", lambda q: q.reciprocal(srow[0:1, 129:130], srow[0:1, 129:130]),
         reads=[b_srow], writes=[b_srow], flush=True)
    onesr_f = srow[0:1, 0:128]
    p.op("ve", lambda q: q.tensor_scalar(out=onesr_f, in0=onesr_f,
                                         scalar1=0.0, scalar2=1.0,
                                         op0=ALU.mult, op1=ALU.add),
         reads=[b_srow], writes=[b_srow], flush=True)
    p.op("te", lambda q: q.matmul(ps[3][:, 4:5], onesr_f, srow[0:1, 129:130],
                                  start=True, stop=True),
         reads=[b_srow], writes=[b_ps[3]])
    p.op("ve", lambda q: q.tensor_copy(scol[:], ps[3][:, 4:5]),
         reads=[b_ps[3]], writes=[b_scol], flush=True)
    p.op("te", lambda q: q.matmul(ps[3][:, 8:9], onesr_f, srow[0:1, 128:129],
                                  start=True, stop=True),
         reads=[b_srow], writes=[b_ps[3]])
    scol_s = absr[:, 17:18]
    p.op("ve", lambda q: q.tensor_copy(scol_s, ps[3][:, 8:9]),
         reads=[b_ps[3]], writes=[b_abs], flush=True)

    for k in range(16):
        pb = k % 2
        p.dma("sy", wq_f[pb][:],
              wT.rearrange("(k p) c -> k p c", k=16, p=128)[k],
              writes=[b_wq[pb]])
        p.op("ve", lambda q, pb=pb: q.tensor_scalar(
            out=wq_f[pb][:], in0=wq_f[pb][:], scalar1=scol[:, 0:1],
            scalar2=MAGIC, op0=ALU.mult, op1=ALU.add),
            reads=[b_wq[pb], b_scol], writes=[b_wq[pb]])
        p.op("ve", lambda q, pb=pb: q.tensor_scalar(
            out=wq_f[pb][:], in0=wq_f[pb][:], scalar1=MAGIC, scalar2=1.0,
            op0=ALU.subtract, op1=ALU.min),
            reads=[b_wq[pb]], writes=[b_wq[pb]])
        p.op("ve", lambda q, k=k, pb=pb: q.tensor_scalar(
            out=wt_bf[:, k * C:(k + 1) * C],
            in0=wq_f[pb][:],
            scalar1=-1.0, scalar2=None, op0=ALU.max),
            reads=[b_wq[pb]], writes=[b_wt])

    brow_f = wq_f[0][0:1, 0:C]
    p.dma("sy", brow_f, bias, writes=[b_wq[0]])
    p.op("ve", lambda q: q.reciprocal(srow[0:1, 129:130], srow[0:1, 128:129]),
         reads=[b_srow], writes=[b_srow], flush=True)
    p.op("ve", lambda q: q.tensor_scalar(out=brow[:], in0=brow_f,
                                         scalar1=srow[0:1, 129:130],
                                         scalar2=None, op0=ALU.mult),
         reads=[b_wq[0], b_srow], writes=[b_brow], flush=True)

    sums = stat[:, 0:NTT]; sumsq = stat[:, NTT:2 * NTT]
    acol = stat[:, 2 * NTT:3 * NTT]; bcol = stat[:, 3 * NTT:4 * NTT]
    for ch in range(NCH):
        pb = ch % 2
        bx = b_x[pb]
        p.dma("sy", x_sb[pb][:].rearrange("p (i c) -> p i c", i=NTT),
              x.rearrange("(ch i p) c -> ch p i c", ch=NCH, p=128)[ch],
              writes=[bx])
        for i in range(NTT):
            emit_ln_stats(p, x_sb[pb][:, i * C:(i + 1) * C], sq_scr, b_sq,
                          sums, sumsq, b_stat, b_stat, i, bx)
        emit_ln_coeffs(p, sums, sumsq, b_stat, b_stat, acol, bcol, stat_t,
                       b_stat, b_stat, b_statt, NTT, epsc[:, 0:1], b_eps)
        for i in range(NTT):
            p.op("sc", lambda q, i=i, pb=pb: q.activation(
                out=ln_tm[:, i * C:(i + 1) * C],
                in_=x_sb[pb][:, i * C:(i + 1) * C], func=AF.Identity,
                scale=acol[:, i:i + 1], bias=bcol[:, i:i + 1]),
                reads=[bx, b_stat], writes=[b_lntm])
        emit_transpose_chunk(p, ident, b_ident, ln_tm, b_lntm, ln_fm,
                             b_lnfm, pt, b_pt)
        for i in range(NTT):
            for ob in range(4):
                bank = b_ps[(i * 4 + ob) % 2]
                pbank = ps[(i * 4 + ob) % 2]
                p.op("te", lambda q, ob=ob, pbank=pbank: q.matmul(
                    pbank[:], onesr[:], brow[0:1, ob * 512:(ob + 1) * 512],
                    start=True, stop=False),
                    reads=[b_ones, b_brow], writes=[bank], inc=False)
                for j in range(NCC):
                    p.op("te", lambda q, i=i, ob=ob, j=j, pbank=pbank: q.matmul(
                        pbank[:],
                        ln_fm[:, j * TC + i * 128: j * TC + (i + 1) * 128],
                        wt_bf[:, j * C + ob * 512: j * C + (ob + 1) * 512],
                        start=False, stop=(j == NCC - 1)),
                        reads=[b_lnfm, b_wt], writes=[bank],
                        inc=(j == NCC - 1))
                xa = x_sb[pb][:, i * C + ob * 512: i * C + (ob + 1) * 512]
                p.op("ve", lambda q, xa=xa, pbank=pbank: q.scalar_tensor_tensor(
                    out=xa, in0=pbank[:], scalar=scol_s[:, 0:1], in1=xa,
                    op0=ALU.mult, op1=ALU.add),
                    reads=[bank, b_abs], writes=[bx])
            p.dma("sy", y.rearrange("(ch i p) c -> ch p i c", ch=NCH, p=128)[ch][:, i, :],
                  x_sb[pb][:, i * C:(i + 1) * C], reads=[bx])


def build_l0f(debug=False):
    nc = bass.Bass("TRN2", target_bir_lowering=False)
    x = nc.dram_tensor("x", [T, C], F32, kind="ExternalInput").ap()
    wT = nc.dram_tensor("wT", [C, C], F32, kind="ExternalInput").ap()
    bias = nc.dram_tensor("bias", [1, C], F32, kind="ExternalInput").ap()
    y = nc.dram_tensor("y", [T, C], F32, kind="ExternalOutput").ap()
    io = {"x": x, "wT": wT, "bias": bias, "y": y}
    with ExitStack() as stack:
        p = Prog(nc, stack)
        with ExitStack() as sstack:
            emit_l0f(p, nc, sstack, io, debug)
        p.wait_all("sy")
        block = stack.enter_context(nc.Block())
        p.finalize(block)
    return nc


def emit_transpose_tile(p, ident, b_ident, src_ap, b_src, ln_fm, b_lnfm,
                        i, pt, b_pt, tc=TC, ncc=NCC):
    for jg in range(ncc // 4):
        pb = jg % 2
        for jj in range(4):
            j = jg * 4 + jj
            p.op("te", lambda q, j=j, jj=jj, pb=pb: q.transpose(
                pt[pb][:, jj * 128:(jj + 1) * 128],
                src_ap[:, j * 128:(j + 1) * 128], ident[:]),
                reads=[b_src, b_ident], writes=[b_pt[pb]], inc=(jj == 3))
        dst = ln_fm[:].rearrange("p (j t) -> p j t", j=ncc)[
            :, jg * 4:(jg + 1) * 4, i * 128:(i + 1) * 128]
        p.op("ve", lambda q, pb=pb, dst=dst: q.tensor_copy(
            dst, pt[pb][:].rearrange("p (a b) -> p a b", a=4)),
            reads=[b_pt[pb]], writes=[b_lnfm])


def emit_convert(p, nc, pairs, eng="gp", bufs=None):
    for idx, (srcT, dstT) in enumerate(pairs):
        rows = srcT.shape[0]
        for r0 in range(0, rows, 512):
            p.dma(eng, dstT[r0:r0 + 512, :], srcT[r0:r0 + 512, :],
                  writes=[bufs[idx]] if bufs is not None else ())



H0 = 16
G0 = 4
D0 = 128
KV0 = G0 * D0
SCAP = 50.0
RSQD = 1.0 / math.sqrt(D0)


def emit_l0a(p, nc, stack, io, debug=False):
    x = io["x"]
    wqT = io["wqTb"]
    wkT = io["wkTb"]
    wvT = io["wvTb"]
    woT = io["woTb"]
    bo = io["bo"]
    qwc = io["qwc"]
    kwc = io["kwc"]
    y = io["y"]
    Qs = io["Qs"]
    Ys = io["Ys"]
    e = stack.enter_context
    x_sb = e(nc.sbuf_tensor("a0_x_sb", [128, NTT * C], F32))
    sq_scr = e(nc.sbuf_tensor("a0_sq_scr", [128, C], F32))
    ln_tm = e(nc.sbuf_tensor("a0_ln_tm", [128, 2 * C], BF16))
    ln_fm = e(nc.sbuf_tensor("a0_ln_fm", [128, NCC * TC], BF16))
    k_sb = e(nc.sbuf_tensor("a0_k_sb", [128, G0 * T], BF16))
    v_sb = e(nc.sbuf_tensor("a0_v_sb", [128, (T // 128) * KV0], BF16))
    wv_sb = e(nc.sbuf_tensor("a0_wv_sb", [128, NCC * KV0], BF16))
    qh_sb = None
    qro = e(nc.sbuf_tensor("a0_qro", [128, 2 * 512], BF16))
    ws = [e(nc.sbuf_tensor(f"a0_ws{i}", [128, NCC * 128], BF16)) for i in range(2)]
    cosT = e(nc.sbuf_tensor("a0_cosT", [128, T], BF16))
    sinT = e(nc.sbuf_tensor("a0_sinT", [128, T], BF16))
    masks = e(nc.sbuf_tensor("a0_masks", [128, 4 * 512], BF16))
    ident = e(nc.sbuf_tensor("a0_ident", [128, 128], BF16))
    stat = e(nc.sbuf_tensor("a0_stat", [128, 4 * NTT], F32))
    stat_t = e(nc.sbuf_tensor("a0_stat_t", [128, NTT], F32))
    epsc = e(nc.sbuf_tensor("a0_epsc", [128, 1], F32))
    m50c = e(nc.sbuf_tensor("a0_m50c", [128, 1], F32))
    qw_sb = e(nc.sbuf_tensor("a0_qw_sb", [128, 2], F32))
    onesr = e(nc.sbuf_tensor("a0_onesr", [1, 128], BF16))
    onesc = e(nc.sbuf_tensor("a0_onesc", [128, 1], BF16))
    rowsc = e(nc.sbuf_tensor("a0_rowsc", [1, 2 * 512], F32))
    rrbf = e(nc.sbuf_tensor("a0_rrbf", [1, 2 * 512], BF16))
    qf = e(nc.sbuf_tensor("a0_qf", [128, 2 * 512], F32))
    qn = e(nc.sbuf_tensor("a0_qn", [128, 512], F32))
    qsw = e(nc.sbuf_tensor("a0_qsw", [128, 512], F32))
    rtmp = e(nc.sbuf_tensor("a0_rtmp", [128, 1024], F32))
    q2b = e(nc.sbuf_tensor("a0_q2b", [128, 512], BF16))
    pt_sb = [e(nc.sbuf_tensor(f"a0_pt_sb{i}", [128, 1024], BF16)) for i in range(2)]
    zb_sb = e(nc.sbuf_tensor("a0_zb_sb", [128, 512], BF16))
    yo_sb = e(nc.sbuf_tensor("a0_yo_sb", [128, 2 * 512], BF16))
    brow = e(nc.sbuf_tensor("a0_brow", [1, C], BF16))
    ifr = e(nc.sbuf_tensor("a0_ifr", [128, 2], F32))
    psA = e(nc.psum_tensor("a0_psA", [128, 1024], F32))
    psB = e(nc.psum_tensor("a0_psB", [128, 1024], F32))
    psY = e(nc.psum_tensor("a0_psY", [128, 512], F32))
    psZ = e(nc.psum_tensor("a0_psZ", [128, 512], F32))
    psR = e(nc.psum_tensor("a0_psR", [128, 512], F32))
    pt = e(nc.psum_tensor("a0_pt", [128, 1024], BF16))
    b_psA = Buf("psA"); b_psB = Buf("psB"); b_psY = Buf("psY")
    b_psZ = Buf("psZ"); b_psR = Buf("psR"); b_pt = Buf("pt")
    mmslots = [(psA[:, 0:512], b_psA), (psA[:, 512:1024], b_psA),
               (psB[:, 0:512], b_psB), (psB[:, 512:1024], b_psB)]
    b_x = Buf("x")
    b_sq = [Buf("sq0")]; b_sq.append(b_sq[0])
    b_lntm = [Buf("ln_tm0"), Buf("ln_tm1")]; b_lnfm = Buf("ln_fm")
    b_k = Buf("k"); b_v = Buf("v"); b_wv = Buf("wv")
    b_qh = Buf("qh"); b_qro = Buf("qro")
    b_ws = [Buf("ws0"), Buf("ws1")]
    b_cos = Buf("cos"); b_sin = Buf("sin"); b_mask = Buf("mask")
    b_ident = Buf("ident"); b_stat = Buf("stat"); b_statt = Buf("stat_t")
    b_eps = Buf("eps"); b_m50 = Buf("m50"); b_qw = Buf("qw")
    b_ones = Buf("ones"); b_onesc = Buf("onesc"); b_rows = Buf("rows")
    b_rrb = Buf("rrb")
    b_qf = Buf("qf"); b_qn = Buf("qn"); b_qsw = Buf("qsw"); b_rt = Buf("rt")
    b_q2 = Buf("q2")
    b_ptb = [Buf("ptb0"), Buf("ptb1")]
    b_zb = Buf("zb"); b_yo = Buf("yo"); b_brow = Buf("brow"); b_ifr = Buf("ifr")

    emit_ident(p, ident, b_ident)
    p.op("gp", lambda q: q.memset(onesr[:], 1.0), writes=[b_ones])
    p.op("gp", lambda q: q.memset(onesc[:], 1.0), writes=[b_onesc])
    p.op("gp", lambda q: q.memset(epsc[:], 1e-5), writes=[b_eps])
    p.op("gp", lambda q: q.memset(m50c[:], -SCAP), writes=[b_m50])
    p.dma("sy", qw_sb[:, 0:1], qwc[:], writes=[b_qw])
    p.dma("sy", qw_sb[:, 1:2], kwc[:], writes=[b_qw])
    p.dma("gp", brow[:], bo[:], writes=[b_brow])
    p.dma("gp", wv_sb[:].rearrange("p (j o) -> p j o", j=NCC),
          wvT.rearrange("(j p) o -> p j o", p=128), writes=[b_wv])
    p.op("gp", lambda q: q.memset(masks[:], 1.0), writes=[b_mask])
    for m in range(4):
        p.op("gp", lambda q, m=m: q.affine_select(
            out=masks[:, m * 512:(m + 1) * 512],
            in_=masks[:, m * 512:(m + 1) * 512],
            compare_op=ALU.is_ge, fill=0.0, base=-m * 128,
            pattern=[[1, 512]], channel_multiplier=-1),
            reads=[b_mask], writes=[b_mask])
    p.op("gp", lambda q: q.iota(ifr[:, 0:1], pattern=[[0, 1]], base=0,
                                channel_multiplier=1,
                                allow_small_or_imprecise_dtypes=True),
         writes=[b_ifr])
    p.op("ve", lambda q: q.tensor_scalar(out=ifr[:, 1:2], in0=ifr[:, 0:1],
                                         scalar1=63.5, scalar2=-64.0,
                                         op0=ALU.is_gt, op1=ALU.mult),
         reads=[b_ifr], writes=[b_ifr], flush=True)
    p.op("ve", lambda q: q.tensor_add(ifr[:, 0:1], ifr[:, 0:1], ifr[:, 1:2]),
         reads=[b_ifr], writes=[b_ifr], flush=True)
    p.op("ve", lambda q: q.tensor_scalar(out=ifr[:, 1:2], in0=ifr[:, 1:2],
                                         scalar1=1.0 / 32.0, scalar2=1.0,
                                         op0=ALU.mult, op1=ALU.add),
         reads=[b_ifr], writes=[b_ifr], flush=True)
    p.op("sc", lambda q: q.activation(out=ifr[:, 0:1], in_=ifr[:, 0:1],
                                      func=AF.Exp,
                                      scale=-math.log(10000.0) / 64),
         reads=[b_ifr], writes=[b_ifr], flush=True)
    frq = x_sb[:, 0:T]
    nsc = x_sb[:, T:2 * T]
    b_frq = Buf("frq"); b_nsc = Buf("nsc")
    p.op("gp", lambda q: q.iota(frq, pattern=[[1, T]], base=0,
                                channel_multiplier=0,
                                allow_small_or_imprecise_dtypes=True),
         writes=[b_frq])
    p.op("ve", lambda q: q.tensor_scalar(out=frq, in0=frq,
                                         scalar1=ifr[:, 0:1], scalar2=None,
                                         op0=ALU.mult),
         reads=[b_ifr], writes=[b_frq])
    p.op("ve", lambda q: q.tensor_scalar(out=nsc, in0=frq,
                                         scalar1=1.0 / TWO_PI, scalar2=MAGIC,
                                         op0=ALU.mult, op1=ALU.add),
         reads=[b_frq], writes=[b_nsc])
    p.op("ve", lambda q: q.tensor_scalar(out=nsc, in0=nsc,
                                         scalar1=MAGIC, scalar2=None,
                                         op0=ALU.subtract),
         reads=[], writes=[b_nsc])
    for cw in (CW1, CW2, CW3):
        p.op("ve", lambda q, cw=cw: q.scalar_tensor_tensor(
            out=frq, in0=nsc, scalar=-cw, in1=frq,
            op0=ALU.mult, op1=ALU.add),
            reads=[b_nsc], writes=[b_frq])
    p.op("ve", lambda q: q.tensor_scalar(out=frq, in0=frq,
                                         scalar1=math.pi, scalar2=-math.pi,
                                         op0=ALU.min, op1=ALU.max),
         reads=[], writes=[b_frq])
    p.op("sc", lambda q: q.activation(out=sinT[:], in_=frq, func=AF.Sin,
                                      scale=ifr[:, 1:2]),
         reads=[b_frq, b_ifr], writes=[b_sin])
    p.op("sc", lambda q: q.activation(out=frq, in_=frq, func=AF.Abs),
         reads=[b_frq], writes=[b_frq])
    p.op("ve", lambda q: q.tensor_scalar(out=frq, in0=frq,
                                         scalar1=-1.0, scalar2=math.pi / 2,
                                         op0=ALU.mult, op1=ALU.add),
         reads=[], writes=[b_frq])
    p.op("sc", lambda q: q.activation(out=cosT[:], in_=frq, func=AF.Sin),
         reads=[b_frq], writes=[b_cos])
    p.barrier()

    sums = stat[:, 0:NTT]; sumsq = stat[:, NTT:2 * NTT]
    acol = stat[:, 2 * NTT:3 * NTT]; bcol = stat[:, 3 * NTT:4 * NTT]

    def rms_a(ch, job, sl):
        (kind, ot, psrc, b_psrc) = job
        s5 = slice(sl * 512, (sl + 1) * 512)
        p.op("sc", lambda q: q.activation(out=q2b[:], in_=psrc,
                                          func=AF.Square),
             reads=[b_psrc], writes=[b_q2])
        p.op("sc", lambda q: q.activation(out=qf[:, s5], in_=psrc,
                                          func=AF.Identity),
             reads=[b_psrc], writes=[b_qf])
        p.op("te", lambda q: q.matmul(psZ[0:1, :], onesc[:], q2b[:],
                                      start=True, stop=True),
             reads=[b_onesc, b_q2], writes=[b_psZ])
        p.op("sc", lambda q: q.activation(out=rowsc[0:1, s5],
                                          in_=psZ[0:1, :], func=AF.Sqrt,
                                          scale=1.0 / D0),
             reads=[b_psZ], writes=[b_rows], flush=True)
        p.op("ve", lambda q: q.tensor_scalar(out=rowsc[0:1, s5],
                                             in0=rowsc[0:1, s5],
                                             scalar1=1e-6, scalar2=None,
                                             op0=ALU.add),
             reads=[], writes=[b_rows], flush=True)
        p.op("ve", lambda q: q.reciprocal(rowsc[0:1, s5], rowsc[0:1, s5]),
             reads=[b_rows], writes=[b_rows], flush=True)
        p.op("ve", lambda q: q.tensor_copy(rrbf[0:1, s5], rowsc[0:1, s5]),
             reads=[b_rows], writes=[b_rrb], flush=True)

    def rms_b(ch, job, sl):
        (kind, ot, psrc, b_psrc) = job
        s5 = slice(sl * 512, (sl + 1) * 512)
        csl = slice(ch * TC, (ch + 1) * TC)
        cs = cosT[:, csl]; sn = sinT[:, csl]
        wcol = qw_sb[:, 0:1] if kind == "q" else qw_sb[:, 1:2]
        p.op("te", lambda q: q.matmul(psR[:, :], onesr[:], rrbf[0:1, s5],
                                      start=True, stop=True),
             reads=[b_ones, b_rrb], writes=[b_psR])
        p.op("ve", lambda q: q.scalar_tensor_tensor(
            out=qn[:], in0=qf[:, s5], scalar=wcol, in1=psR[:, :],
            op0=ALU.mult, op1=ALU.mult),
            reads=[b_qf, b_qw, b_psR], writes=[b_qn])
        p.dma("sy", qsw[0:64, :], qn[64:128, :], reads=[b_qn], writes=[b_qsw])
        p.dma("sy", qsw[64:128, :], qn[0:64, :], reads=[b_qn], writes=[b_qsw])
        if kind == "q":
            dst, b_dst, dcol = qro, b_qro, (ot % 2) * 512
        else:
            dst, b_dst, dcol = k_sb, b_k, (ot * NCH + ch) * 512
        p.op("ve", lambda q: q.tensor_mul(rtmp[:, 0:512], qn[:, :], cs),
             reads=[b_qn, b_cos], writes=[b_rt])
        p.op("ve", lambda q: q.tensor_mul(rtmp[:, 512:1024], qsw[:, :], sn),
             reads=[b_qsw, b_sin], writes=[b_rt])
        p.op("ve", lambda q: q.tensor_add(dst[:, dcol:dcol + 512],
                                          rtmp[:, 0:512],
                                          rtmp[:, 512:1024]),
             reads=[b_rt], writes=[b_dst])
        if kind == "q":
            p.dma("sy", Qs[ot * 128:(ot + 1) * 128, ch * TC:(ch + 1) * TC],
                  qro[:, dcol:dcol + 512], reads=[b_qro])

    def v_epilogue(ch, job):
        (kind, i, psrc, b_psrc) = job
        p.op("ve", lambda q: q.tensor_copy(
            v_sb[:, (ch * NTT + i) * KV0:(ch * NTT + i + 1) * KV0], psrc),
            reads=[b_psrc], writes=[b_v])

    for ch in range(NCH):
        p.dma("sy", x_sb[:].rearrange("p (i c) -> p i c", i=NTT),
              x.rearrange("(ch i p) c -> ch p i c", ch=NCH, p=128)[ch],
              writes=[b_x])
        for i in range(NTT):
            emit_ln_stats(p, x_sb[:, i * C:(i + 1) * C], sq_scr, b_sq,
                          sums, sumsq, b_stat, b_stat, i, b_x, n=C)
        emit_ln_coeffs(p, sums, sumsq, b_stat, b_stat, acol, bcol, stat_t,
                       b_stat, b_stat, b_statt, NTT, epsc[:, 0:1], b_eps)
        for i in range(NTT):
            sl2 = i % 2
            p.op("sc", lambda q, i=i, sl2=sl2: q.activation(
                out=ln_tm[:, sl2 * C:(sl2 + 1) * C],
                in_=x_sb[:, i * C:(i + 1) * C], func=AF.Identity,
                scale=acol[:, i:i + 1], bias=bcol[:, i:i + 1]),
                reads=[b_x, b_stat], writes=[b_lntm[sl2]])
            for jg in range(2):
                for jj in range(8):
                    j = jg * 8 + jj
                    p.op("te", lambda q, j=j, jj=jj, sl2=sl2: q.transpose(
                        pt[:, jj * 128:(jj + 1) * 128],
                        ln_tm[:, sl2 * C + j * 128: sl2 * C + (j + 1) * 128],
                        ident[:]),
                        reads=[b_lntm[sl2], b_ident], writes=[b_pt],
                        inc=(jj == 7))
                dst = ln_fm[:].rearrange("p (j t) -> p j t", j=NCC)[
                    :, jg * 8:(jg + 1) * 8, i * 128:(i + 1) * 128]
                p.op("ve", lambda q, dst=dst: q.tensor_copy(
                    dst, pt[:].rearrange("p (a b) -> p a b", a=8)),
                    reads=[b_pt], writes=[b_lnfm])
        jobs = []
        for ot in range(H0):
            jobs.append(("q", ot, wqT))
        for ot in range(G0):
            jobs.append(("k", ot, wkT))
        for i in range(NTT):
            jobs.append(("v", i, None))
        em = []
        for jid, (kind, ot, wT_ap) in enumerate(jobs):
            slot, b_slot = mmslots[jid % 4]
            if kind in ("q", "k"):
                wb = jid % 2
                p.dma("gp", ws[wb][:].rearrange("p (j o) -> p j o", j=NCC),
                      wT_ap.rearrange("(j p) o -> p j o", p=128)[:, :, ot * 128:(ot + 1) * 128],
                      writes=[b_ws[wb]])
                for j in range(NCC):
                    p.op("te", lambda q, j=j, wb=wb, slot=slot: q.matmul(
                        slot, ws[wb][:, j * 128:(j + 1) * 128],
                        ln_fm[:, j * TC:(j + 1) * TC],
                        start=(j == 0), stop=(j == NCC - 1)),
                        reads=[b_ws[wb], b_lnfm], writes=[b_slot],
                        inc=(j == NCC - 1))
                em.append((kind, ot, slot, b_slot))
                rms_a(ch, em[-1], jid % 2)
            else:
                for j in range(NCC):
                    p.op("te", lambda q, j=j, ot=ot, slot=slot: q.matmul(
                        slot,
                        ln_fm[:, j * TC + ot * 128: j * TC + (ot + 1) * 128],
                        wv_sb[:, j * KV0:(j + 1) * KV0],
                        start=(j == 0), stop=(j == NCC - 1)),
                        reads=[b_wv, b_lnfm], writes=[b_slot],
                        inc=(j == NCC - 1))
                em.append((kind, ot, slot, b_slot))
            if len(em) >= 2:
                pj = em[-2]
                if pj[0] == "v":
                    v_epilogue(ch, pj)
                else:
                    rms_b(ch, pj, (jid - 1) % 2)
        pj = em[-1]
        if pj[0] == "v":
            v_epilogue(ch, pj)
        else:
            rms_b(ch, pj, (len(jobs) - 1) % 2)
    p.barrier()

    if debug:
        d_Q = io["d_Q"]; d_K = io["d_K"]; d_V = io["d_V"]; d_Y = io["d_Y"]
        d_qh = io["d_qh"]
        p.barrier()
        p.dma("gp", d_Q[:], Qs[:])
        p.dma("sy", d_K[:], k_sb[:], reads=[b_k])
        p.dma("sy", d_V[:], v_sb[:], reads=[b_v])
        p.barrier()

    qh_sb = ln_tm[:, 0:T]
    NQB = T // 512
    for h in range(H0):
        g = h // 4
        p.dma("sy", qh_sb, Qs[h * 128:(h + 1) * 128, :], writes=[b_qh])
        if debug and h == 0:
            p.dma("sy", d_qh[:], qh_sb, reads=[b_qh])
        for qb in range(NQB):
            ntk = 4 * (qb + 1)
            npr = ntk // 2
            prev = None

            def sm_pair(pr, bank, b_bank, pbuf):
                p.op("sc", lambda q, bank=bank: q.activation(
                    out=rtmp[:], in_=bank[:], func=AF.Tanh,
                    scale=RSQD / SCAP),
                    reads=[b_bank], writes=[b_rt])
                p.op("sc", lambda q, bank=bank, pbuf=pbuf: q.activation(
                    out=pt_sb[pbuf][:], in_=rtmp[:], func=AF.Exp,
                    scale=SCAP, bias=m50c[:, 0:1]),
                    reads=[b_rt, b_m50], writes=[b_ptb[pbuf]])
                for half in range(2):
                    kt = 2 * pr + half
                    m = kt - 4 * qb
                    if m >= 0:
                        p.op("ve", lambda q, half=half, m=m, pbuf=pbuf: q.tensor_mul(
                            pt_sb[pbuf][:, half * 512:(half + 1) * 512],
                            pt_sb[pbuf][:, half * 512:(half + 1) * 512],
                            masks[:, m * 512:(m + 1) * 512]),
                            reads=[b_ptb[pbuf], b_mask], writes=[b_ptb[pbuf]])
                for half in range(2):
                    kt = 2 * pr + half
                    p.op("te", lambda q, kt=kt, half=half, pbuf=pbuf, g=g,
                         ntk=ntk: q.matmul(
                        psY[:],
                        v_sb[:, kt * KV0 + g * 128: kt * KV0 + (g + 1) * 128],
                        pt_sb[pbuf][:, half * 512:(half + 1) * 512],
                        start=(kt == 0), stop=(kt == ntk - 1)),
                        reads=[b_v, b_ptb[pbuf]], writes=[b_psY],
                        inc=(kt == ntk - 1))
                    p.op("te", lambda q, kt=kt, half=half, pbuf=pbuf,
                         ntk=ntk: q.matmul(
                        psZ[0:1, :], onesc[:],
                        pt_sb[pbuf][:, half * 512:(half + 1) * 512],
                        start=(kt == 0), stop=(kt == ntk - 1)),
                        reads=[b_onesc, b_ptb[pbuf]], writes=[b_psZ],
                        inc=(kt == ntk - 1))

            for pr in range(npr):
                bank, b_bank = (psA, b_psA) if pr % 2 == 0 else (psB, b_psB)
                for half in range(2):
                    kt = 2 * pr + half
                    p.op("te", lambda q, g=g, kt=kt, qb=qb, bank=bank,
                         half=half: q.matmul(
                             bank[:, half * 512:(half + 1) * 512],
                             k_sb[:, g * T + kt * 128: g * T + (kt + 1) * 128],
                             qh_sb[:, qb * 512:(qb + 1) * 512],
                             start=True, stop=True),
                         reads=[b_k, b_qh], writes=[b_bank])
                if prev is not None:
                    sm_pair(*prev)
                prev = (pr, bank, b_bank, pr % 2)
            sm_pair(*prev)
            p.op("ve", lambda q: q.reciprocal(rowsc[0:1, 0:512], psZ[0:1, :]),
                 reads=[b_psZ], writes=[b_rows], flush=True)
            p.op("ve", lambda q: q.tensor_copy(rrbf[0:1, 0:512],
                                               rowsc[0:1, 0:512]),
                 reads=[b_rows], writes=[b_rrb], flush=True)
            p.op("te", lambda q: q.matmul(psR[:, :], onesr[:],
                                          rrbf[0:1, 0:512], start=True,
                                          stop=True),
                 reads=[b_ones, b_rrb], writes=[b_psR])
            p.op("sc", lambda q: q.activation(out=zb_sb[:], in_=psR[:, :],
                                              func=AF.Identity),
                 reads=[b_psR], writes=[b_zb])
            yb = qb % 2
            p.op("ve", lambda q, yb=yb: q.tensor_mul(
                yo_sb[:, yb * 512:(yb + 1) * 512], psY[:], zb_sb[:]),
                reads=[b_psY, b_zb], writes=[b_yo])
            p.dma("sy", Ys[h * 128:(h + 1) * 128, qb * 512:(qb + 1) * 512],
                  yo_sb[:, yb * 512:(yb + 1) * 512], reads=[b_yo])
    p.barrier()

    if debug:
        p.barrier()
        p.dma("gp", d_Y[:], Ys[:])
        p.barrier()

    b_yc = Buf("yc"); b_wo = [Buf("wo0"), Buf("wo1")]
    p3slots = [(psA[:, 0:512], b_psA), (psB[:, 0:512], b_psB),
               (psA[:, 512:1024], b_psA), (psB[:, 512:1024], b_psB)]
    for ch in range(NCH):
        p.dma("sy", x_sb[:].rearrange("p (i c) -> p i c", i=NTT),
              x.rearrange("(ch i p) c -> ch p i c", ch=NCH, p=128)[ch],
              writes=[b_x])
        p.dma("sy", ln_fm[:].rearrange("p (j t) -> p j t", j=NCC),
              Ys.rearrange("(j p) t -> p j t", p=128)[:, :, ch * TC:(ch + 1) * TC],
              writes=[b_yc])
        for ob in range(4):
            wb = ob % 2
            p.dma("gp", k_sb[:, wb * 8192: wb * 8192 + NCC * 512].rearrange("p (j o) -> p j o", j=NCC),
                  woT.rearrange("(j p) o -> p j o", p=128)[:, :, ob * 512:(ob + 1) * 512],
                  writes=[b_wo[wb]])
            for i in range(NTT):
                slot, b_slot = p3slots[(ob * NTT + i) % 4]
                p.op("te", lambda q, ob=ob, slot=slot: q.matmul(
                    slot, onesr[:],
                    brow[0:1, ob * 512:(ob + 1) * 512],
                    start=True, stop=False),
                    reads=[b_ones, b_brow], writes=[b_slot], inc=False)
                for j in range(NCC):
                    p.op("te", lambda q, i=i, j=j, wb=wb, slot=slot: q.matmul(
                        slot,
                        ln_fm[:, j * TC + i * 128: j * TC + (i + 1) * 128],
                        k_sb[:, wb * 8192 + j * 512: wb * 8192 + (j + 1) * 512],
                        start=False, stop=(j == NCC - 1)),
                        reads=[b_yc, b_wo[wb]], writes=[b_slot],
                        inc=(j == NCC - 1))
                xa = x_sb[:, i * C + ob * 512: i * C + (ob + 1) * 512]
                p.op("ve", lambda q, xa=xa, slot=slot: q.tensor_add(
                    xa, slot, xa),
                    reads=[b_slot], writes=[b_x])
        for i in range(NTT):
            p.dma("sy",
                  y.rearrange("(ch i p) c -> ch p i c", ch=NCH, p=128)[ch][:, i, :],
                  x_sb[:, i * C:(i + 1) * C], reads=[b_x])


def build_l0a(debug=False):
    nc = bass.Bass("TRN2", target_bir_lowering=False)
    x = nc.dram_tensor("x", [T, C], F32, kind="ExternalInput").ap()
    wqT = nc.dram_tensor("wqT", [C, C], F32, kind="ExternalInput").ap()
    wkT = nc.dram_tensor("wkT", [C, KV0], F32, kind="ExternalInput").ap()
    wvT = nc.dram_tensor("wvT", [C, KV0], F32, kind="ExternalInput").ap()
    woT = nc.dram_tensor("woT", [C, C], F32, kind="ExternalInput").ap()
    wqTb = nc.dram_tensor("wqTb", [C, C], BF16, kind="Internal").ap()
    if debug:
        dbgio = {
            "d_Q": nc.dram_tensor("d_Q", [C, T], BF16, kind="ExternalOutput").ap(),
            "d_K": nc.dram_tensor("d_K", [128, G0 * T], BF16, kind="ExternalOutput").ap(),
            "d_V": nc.dram_tensor("d_V", [128, (T // 128) * KV0], BF16, kind="ExternalOutput").ap(),
            "d_Y": nc.dram_tensor("d_Y", [C, T], BF16, kind="ExternalOutput").ap(),
            "d_qh": nc.dram_tensor("d_qh", [128, T], BF16, kind="ExternalOutput").ap(),
        }
    else:
        dbgio = {}
    wkTb = nc.dram_tensor("wkTb", [C, KV0], BF16, kind="Internal").ap()
    wvTb = nc.dram_tensor("wvTb", [C, KV0], BF16, kind="Internal").ap()
    woTb = nc.dram_tensor("woTb", [C, C], BF16, kind="Internal").ap()
    bo = nc.dram_tensor("bo", [1, C], F32, kind="ExternalInput").ap()
    qwc = nc.dram_tensor("qwc", [128, 1], F32, kind="ExternalInput").ap()
    kwc = nc.dram_tensor("kwc", [128, 1], F32, kind="ExternalInput").ap()
    y = nc.dram_tensor("y", [T, C], F32, kind="ExternalOutput").ap()
    Qs = nc.dram_tensor("Qs", [C, T], BF16, kind="Internal").ap()
    Ys = nc.dram_tensor("Ys", [C, T], BF16, kind="Internal").ap()
    io = {"x": x, "wqT": wqT, "wkT": wkT, "wvT": wvT, "woT": woT,
          "wqTb": wqTb, "wkTb": wkTb, "wvTb": wvTb, "woTb": woTb,
          "bo": bo, "qwc": qwc, "kwc": kwc, "y": y, "Qs": Qs, "Ys": Ys,
          **dbgio}
    with ExitStack() as stack:
        p = Prog(nc, stack)
        emit_convert(p, nc, [(wqT, wqTb), (wkT, wkTb), (wvT, wvTb),
                             (woT, woTb)])
        with ExitStack() as sstack:
            emit_l0a(p, nc, sstack, io, debug)
        p.wait_all("sy")
        block = stack.enter_context(nc.Block())
        p.finalize(block)
    return nc



H1 = 16
D1 = 128
R1 = 16
WIN = 512
LSC = 1.0 / 16.0


def emit_l1a(p, nc, stack, io, debug=False, upto=3, cv1=None, bg=None):
    if cv1 is None:
        cv1 = [Buf(f"cv1_{i}") for i in range(4)]
    b_cvq, b_cvk, b_cvv, b_cvo = cv1
    x = io["x"]
    wqT = io["wqTb"]
    wkT = io["wkTb"]
    wvT = io["wvTb"]
    woT = io["woTb"]
    qAT = io["qAT"]
    qBT = io["qBT"]
    kAT = io["kAT"]
    kBT = io["kBT"]
    vAT = io["vAT"]
    vBT = io["vBT"]
    oAT = io["oAT"]
    oBT = io["oBT"]
    nwc = io["nwc"]
    y = io["y"]
    Qs = io["Qs"]
    Ks = io["Ks"]
    Vs = io["Vs"]
    Ys = io["Ys"]
    Ls = io["Ls"]
    e = stack.enter_context
    RSQD1 = 1.0 / math.sqrt(D1)
    x_sb = e(nc.sbuf_tensor("a1_x_sb", [128, NTT * C], F32))
    sq_scr = e(nc.sbuf_tensor("a1_sq_scr", [128, C], F32))
    ln_tm = e(nc.sbuf_tensor("a1_ln_tm", [128, 2 * C], BF16))
    lnst = e(nc.sbuf_tensor("a1_lnst", [128, 2 * C], BF16))
    ln_fm = e(nc.sbuf_tensor("a1_ln_fm", [128, NCC * TC], BF16))
    kwin = e(nc.sbuf_tensor("a1_kwin", [128, H1 * 2 * 512], BF16))
    cosT = kwin[:, 0:T]
    sinT = kwin[:, T:2 * T]
    vwin = e(nc.sbuf_tensor("a1_vwin", [128, 2 * NTT * C], BF16))
    qblk = e(nc.sbuf_tensor("a1_qblk", [128, H1 * 512], BF16))
    ws = [e(nc.sbuf_tensor(f"a1_ws{i}", [128, NCC * 128], BF16)) for i in range(2)]
    lA = e(nc.sbuf_tensor("a1_lA", [128, 3 * NCC * R1], BF16))
    oA_sb = e(nc.sbuf_tensor("a1_oA_sb", [128, NCC * R1], BF16))
    lB = e(nc.sbuf_tensor("a1_lB", [R1, 4 * C], BF16))
    u_bf = e(nc.sbuf_tensor("a1_u_bf", [R1, 512], BF16))
    nw_sb = e(nc.sbuf_tensor("a1_nw_sb", [128, NCC], F32))
    ident = e(nc.sbuf_tensor("a1_ident", [128, 128], BF16))
    stat = e(nc.sbuf_tensor("a1_stat", [128, 6 * NTT], F32))
    stat_t = e(nc.sbuf_tensor("a1_stat_t", [128, NTT], F32))
    epsc = e(nc.sbuf_tensor("a1_epsc", [128, 1], F32))
    mrep = e(nc.sbuf_tensor("a1_mrep", [128, 2 * 512], BF16))
    qro = e(nc.sbuf_tensor("a1_qro", [128, 2 * 512], BF16))
    qn = e(nc.sbuf_tensor("a1_qn", [128, 512], F32))
    qsw = e(nc.sbuf_tensor("a1_qsw", [128, 512], F32))
    rtmp = e(nc.sbuf_tensor("a1_rtmp", [128, 1024], F32))
    pbf = e(nc.sbuf_tensor("a1_pbf", [128, 4 * 512], BF16))
    vro = pbf
    pbt = e(nc.sbuf_tensor("a1_pbt", [128, 4 * 128], BF16))
    ptb = e(nc.sbuf_tensor("a1_ptb", [128, 8 * 512], BF16))
    dgc = e(nc.sbuf_tensor("a1_dgc", [128, 4 * 128], BF16))
    mxq = e(nc.sbuf_tensor("a1_mxq", [128, 16], F32))
    zq = e(nc.sbuf_tensor("a1_zq", [128, 16], F32))
    ystage = e(nc.sbuf_tensor("a1_ystage", [128, 512], BF16))
    ifr = e(nc.sbuf_tensor("a1_ifr", [128, 2], F32))
    psA = e(nc.psum_tensor("a1_psA", [128, 2048], F32))
    psB = e(nc.psum_tensor("a1_psB", [128, 512], F32))
    psC = e(nc.psum_tensor("a1_psC", [128, 512], F32))
    pt = [e(nc.psum_tensor(f"a1_pt{i}", [128, 512], BF16)) for i in range(2)]
    b_psA = [Buf(f"psA{i}") for i in range(4)]
    b_psB = Buf("psB"); b_psC = Buf("psC")
    b_pt = [Buf("ptp0"), Buf("ptp1")]
    b_x = Buf("x")
    b_sq = [Buf("sq0")]; b_sq.append(b_sq[0])
    b_lntm = [Buf("ln_tm0"), Buf("ln_tm1")]
    b_lnst = [Buf("lnst0"), Buf("lnst1")]
    b_lnfm = Buf("ln_fm")
    b_cos = Buf("cos"); b_sin = Buf("sin")
    b_kw = [Buf("kw0"), Buf("kw1")]; b_vw = [Buf("vw0"), Buf("vw1")]
    b_qb = [Buf("qblk0"), Buf("qblk1")]
    b_ws = [Buf("ws0"), Buf("ws1")]
    b_lA = Buf("lA"); b_oA = Buf("oA"); b_lB = Buf("lB"); b_u = Buf("u")
    b_nw = Buf("nw"); b_ident = Buf("ident")
    b_stat = Buf("stat"); b_statt = Buf("stat_t"); b_eps = Buf("eps")
    b_mrep = Buf("mrep"); b_qro = Buf("qro"); b_vro = Buf("vro")
    b_qn = Buf("qn"); b_qsw = Buf("qsw"); b_rt = Buf("rt")
    b_pbf = Buf("pbf"); b_pbt = Buf("pbt"); b_ptb2 = Buf("ptb")
    b_dg = [Buf(f"dg{i}") for i in range(4)]
    b_mxq = Buf("mxq"); b_zq = Buf("zq")
    b_ys = Buf("ys"); b_ifr = Buf("ifr")

    emit_ident(p, ident, b_ident)
    p.op("gp", lambda q: q.memset(epsc[:], 1e-5), writes=[b_eps])
    p.dma("sy", nw_sb[:], nwc[:], writes=[b_nw])
    p.dma("gp", lA[:, 0:NCC * R1].rearrange("p (j r) -> p j r", j=NCC),
          qAT.rearrange("(j p) r -> p j r", p=128), writes=[b_lA])
    p.dma("gp", lA[:, NCC * R1:2 * NCC * R1].rearrange("p (j r) -> p j r", j=NCC),
          kAT.rearrange("(j p) r -> p j r", p=128), writes=[b_lA])
    p.dma("gp", lA[:, 2 * NCC * R1:3 * NCC * R1].rearrange("p (j r) -> p j r", j=NCC),
          vAT.rearrange("(j p) r -> p j r", p=128), writes=[b_lA])
    p.dma("gp", oA_sb[:].rearrange("p (j r) -> p j r", j=NCC),
          oAT.rearrange("(j p) r -> p j r", p=128), writes=[b_oA])
    p.dma("gp", lB[:, 0:C], qBT[:], writes=[b_lB])
    p.dma("gp", lB[:, C:2 * C], kBT[:], writes=[b_lB])
    p.dma("gp", lB[:, 2 * C:3 * C], vBT[:], writes=[b_lB])
    p.dma("gp", lB[:, 3 * C:4 * C], oBT[:], writes=[b_lB])
    p.op("gp", lambda q: q.memset(mrep[:], 1.0), writes=[b_mrep])
    for t in range(4):
        p.op("gp", lambda q, t=t: q.affine_select(
            out=mrep[:, t * 128:(t + 1) * 128],
            in_=mrep[:, t * 128:(t + 1) * 128], compare_op=ALU.is_ge,
            fill=0.0, base=-1, pattern=[[1, 128]], channel_multiplier=-1),
            reads=[b_mrep], writes=[b_mrep])
        p.op("gp", lambda q, t=t: q.affine_select(
            out=mrep[:, 512 + t * 128:512 + (t + 1) * 128],
            in_=mrep[:, 512 + t * 128:512 + (t + 1) * 128],
            compare_op=ALU.is_ge, fill=0.0, base=0, pattern=[[-1, 128]],
            channel_multiplier=1),
            reads=[b_mrep], writes=[b_mrep])
    p.op("gp", lambda q: q.iota(ifr[:, 0:1], pattern=[[0, 1]], base=0,
                                channel_multiplier=1,
                                allow_small_or_imprecise_dtypes=True),
         writes=[b_ifr])
    p.op("ve", lambda q: q.tensor_scalar(out=ifr[:, 1:2], in0=ifr[:, 0:1],
                                         scalar1=63.5, scalar2=-64.0,
                                         op0=ALU.is_gt, op1=ALU.mult),
         reads=[b_ifr], writes=[b_ifr], flush=True)
    p.op("ve", lambda q: q.tensor_add(ifr[:, 0:1], ifr[:, 0:1], ifr[:, 1:2]),
         reads=[b_ifr], writes=[b_ifr], flush=True)
    p.op("ve", lambda q: q.tensor_scalar(out=ifr[:, 1:2], in0=ifr[:, 1:2],
                                         scalar1=-1.0 / 32.0, scalar2=-1.0,
                                         op0=ALU.mult, op1=ALU.add),
         reads=[b_ifr], writes=[b_ifr], flush=True)
    p.op("sc", lambda q: q.activation(out=ifr[:, 0:1], in_=ifr[:, 0:1],
                                      func=AF.Exp,
                                      scale=-math.log(10000.0) / 64),
         reads=[b_ifr], writes=[b_ifr], flush=True)
    frq = x_sb[:, 0:T]
    nsc = x_sb[:, T:2 * T]
    b_frq = Buf("frq"); b_nsc = Buf("nsc")
    p.op("gp", lambda q: q.iota(frq, pattern=[[1, T]], base=0,
                                channel_multiplier=0,
                                allow_small_or_imprecise_dtypes=True),
         writes=[b_frq])
    p.op("ve", lambda q: q.tensor_scalar(out=frq, in0=frq,
                                         scalar1=ifr[:, 0:1], scalar2=None,
                                         op0=ALU.mult),
         reads=[b_ifr], writes=[b_frq])
    p.op("ve", lambda q: q.tensor_scalar(out=nsc, in0=frq,
                                         scalar1=1.0 / TWO_PI, scalar2=MAGIC,
                                         op0=ALU.mult, op1=ALU.add),
         reads=[b_frq], writes=[b_nsc])
    p.op("ve", lambda q: q.tensor_scalar(out=nsc, in0=nsc,
                                         scalar1=MAGIC, scalar2=None,
                                         op0=ALU.subtract),
         reads=[], writes=[b_nsc])
    for cw in (CW1, CW2, CW3):
        p.op("ve", lambda q, cw=cw: q.scalar_tensor_tensor(
            out=frq, in0=nsc, scalar=-cw, in1=frq,
            op0=ALU.mult, op1=ALU.add),
            reads=[b_nsc], writes=[b_frq])
    p.op("ve", lambda q: q.tensor_scalar(out=frq, in0=frq,
                                         scalar1=math.pi, scalar2=-math.pi,
                                         op0=ALU.min, op1=ALU.max),
         reads=[], writes=[b_frq])
    p.op("sc", lambda q: q.activation(out=sinT[:], in_=frq, func=AF.Sin,
                                      scale=ifr[:, 1:2]),
         reads=[b_frq, b_ifr], writes=[b_sin])
    p.op("sc", lambda q: q.activation(out=frq, in_=frq, func=AF.Abs),
         reads=[b_frq], writes=[b_frq])
    p.op("ve", lambda q: q.tensor_scalar(out=frq, in0=frq,
                                         scalar1=-1.0, scalar2=math.pi / 2,
                                         op0=ALU.mult, op1=ALU.add),
         reads=[], writes=[b_frq])
    p.op("sc", lambda q: q.activation(out=cosT[:], in_=frq, func=AF.Sin),
         reads=[b_frq], writes=[b_cos])
    p.barrier()

    sums = stat[:, 0:NTT]; sumsq = stat[:, NTT:2 * NTT]
    acol = stat[:, 2 * NTT:3 * NTT]; bcol = stat[:, 3 * NTT:4 * NTT]
    a2col = stat[:, 4 * NTT:5 * NTT]; b2col = stat[:, 5 * NTT:6 * NTT]

    def emit_rope1(ch, psrc, b_psrc, dst, b_dst, dcol):
        sl = slice(ch * TC, (ch + 1) * TC)
        cs = cosT[:, sl]; sn = sinT[:, sl]
        p.op("sc", lambda q: q.activation(out=qn[:], in_=psrc[:],
                                          func=AF.Identity),
             reads=[b_psrc], writes=[b_qn])
        p.dma("sy", qsw[0:64, :], qn[64:128, :], reads=[b_qn],
              writes=[b_qsw])
        p.dma("sy", qsw[64:128, :], qn[0:64, :], reads=[b_qn],
              writes=[b_qsw])
        p.op("ve", lambda q: q.tensor_mul(rtmp[:, 0:512], qn[:, :], cs),
             reads=[b_qn, b_cos], writes=[b_rt])
        p.op("ve", lambda q: q.tensor_mul(rtmp[:, 512:1024], qsw[:, :], sn),
             reads=[b_qsw, b_sin], writes=[b_rt])
        p.op("ve", lambda q: q.tensor_add(dst[:, dcol:dcol + 512],
                                          rtmp[:, 0:512],
                                          rtmp[:, 512:1024]),
             reads=[b_rt], writes=[b_dst])

    def emit_proj_head(wT_ap, lB_off, ot, psq, b_psq, b_wdep):
        wb = ot % 2
        p.dma("gp", ws[wb][:].rearrange("p (j o) -> p j o", j=NCC),
              wT_ap.rearrange("(j p) o -> p j o", p=128)[:, :, ot * 128:(ot + 1) * 128],
              reads=[b_wdep], writes=[b_ws[wb]])
        for j in range(NCC):
            p.op("te", lambda q, j=j, wb=wb, psq=psq: q.matmul(
                psq, ws[wb][:, j * 128:(j + 1) * 128],
                ln_fm[:, j * TC:(j + 1) * TC],
                start=(j == 0), stop=False),
                reads=[b_ws[wb], b_lnfm], writes=[b_psq], inc=False)
        p.op("te", lambda q, ot=ot, psq=psq, lB_off=lB_off: q.matmul(
            psq, lB[:, lB_off + ot * 128:lB_off + (ot + 1) * 128],
            u_bf[:], start=False, stop=True),
            reads=[b_lB, b_u], writes=[b_psq])

    for ch in range(NCH):
        p.dma("sy", x_sb[:].rearrange("p (i c) -> p i c", i=NTT),
              x.rearrange("(ch i p) c -> ch p i c", ch=NCH, p=128)[ch],
              writes=[b_x])
        for i in range(NTT):
            emit_ln_stats(p, x_sb[:, i * C:(i + 1) * C], sq_scr, b_sq,
                          sums, sumsq, b_stat, b_stat, i, b_x, n=C)
        emit_ln_coeffs(p, sums, sumsq, b_stat, b_stat, acol, bcol, stat_t,
                       b_stat, b_stat, b_statt, NTT, epsc[:, 0:1], b_eps)
        p.op("ve", lambda q: q.tensor_mul(a2col, acol, acol),
             reads=[b_stat], writes=[b_stat], flush=True)
        p.op("ve", lambda q: q.tensor_scalar(
            out=a2col, in0=a2col, scalar1=-1e-5, scalar2=1.0 + 1e-6,
            op0=ALU.mult, op1=ALU.add),
            reads=[b_stat], writes=[b_stat], flush=True)
        p.op("sc", lambda q: q.activation(out=a2col, in_=a2col,
                                          func=AF.Sqrt),
             reads=[b_stat], writes=[b_stat], flush=True)
        p.op("ve", lambda q: q.reciprocal(a2col, a2col),
             reads=[b_stat], writes=[b_stat], flush=True)
        p.op("ve", lambda q: q.tensor_mul(b2col, bcol, a2col),
             reads=[b_stat], writes=[b_stat], flush=True)
        p.op("ve", lambda q: q.tensor_mul(a2col, acol, a2col),
             reads=[b_stat], writes=[b_stat], flush=True)
        for i in range(NTT):
            sl2 = i % 2
            p.op("sc", lambda q, i=i, sl2=sl2: q.activation(
                out=lnst[:, sl2 * C:(sl2 + 1) * C],
                in_=x_sb[:, i * C:(i + 1) * C], func=AF.Identity,
                scale=acol[:, i:i + 1], bias=bcol[:, i:i + 1]),
                reads=[b_x, b_stat], writes=[b_lnst[sl2]])
            p.dma("sy",
                  Ls.rearrange("(ch i p) c -> ch p i c", ch=NCH, p=128)[ch][:, i, :],
                  lnst[:, sl2 * C:(sl2 + 1) * C], reads=[b_lnst[sl2]])
            p.op("sc", lambda q, i=i, sl2=sl2: q.activation(
                out=ln_tm[:, sl2 * C:(sl2 + 1) * C],
                in_=x_sb[:, i * C:(i + 1) * C], func=AF.Identity,
                scale=a2col[:, i:i + 1], bias=b2col[:, i:i + 1]),
                reads=[b_x, b_stat], writes=[b_lntm[sl2]])
            for jg in range(NCC // 4):
                pb = jg % 2
                for jj in range(4):
                    j = jg * 4 + jj
                    p.op("te", lambda q, j=j, jj=jj, pb=pb, sl2=sl2: q.transpose(
                        pt[pb][:, jj * 128:(jj + 1) * 128],
                        ln_tm[:, sl2 * C + j * 128: sl2 * C + (j + 1) * 128],
                        ident[:]),
                        reads=[b_lntm[sl2], b_ident], writes=[b_pt[pb]],
                        inc=(jj == 3))
                dst = ln_fm[:].rearrange("p (j t) -> p j t", j=NCC)[
                    :, jg * 4:(jg + 1) * 4, i * 128:(i + 1) * 128]
                p.op("ve", lambda q, pb=pb, dst=dst: q.tensor_copy(
                    dst, pt[pb][:].rearrange("p (a b) -> p a b", a=4)),
                    reads=[b_pt[pb]], writes=[b_lnfm])
        for j in range(NCC):
            p.op("ve", lambda q, j=j: q.tensor_scalar(
                out=ln_fm[:, j * TC:(j + 1) * TC],
                in0=ln_fm[:, j * TC:(j + 1) * TC],
                scalar1=nw_sb[:, j:j + 1], scalar2=None, op0=ALU.mult),
                reads=[b_nw], writes=[b_lnfm])
        for which, wT_ap, Aoff, Boff, b_wdep in (
                (0, wqT, 0, 0, b_cvq), (1, wkT, 1, C, b_cvk),
                (2, wvT, 2, 2 * C, b_cvv)):
            for j in range(NCC):
                p.op("te", lambda q, j=j, Aoff=Aoff: q.matmul(
                    psC[0:R1, :],
                    lA[:, Aoff * NCC * R1 + j * R1: Aoff * NCC * R1 + (j + 1) * R1],
                    ln_fm[:, j * TC:(j + 1) * TC],
                    start=(j == 0), stop=(j == NCC - 1)),
                    reads=[b_lA, b_lnfm], writes=[b_psC],
                    inc=(j == NCC - 1))
            p.op("sc", lambda q: q.activation(out=u_bf[:], in_=psC[0:R1, :],
                                              func=AF.Identity, scale=LSC),
                 reads=[b_psC], writes=[b_u])
            for ot in range(H1):
                qtr = ot % 4
                psq = psA[:, qtr * 512:(qtr + 1) * 512]
                b_psq = b_psA[qtr]
                emit_proj_head(wT_ap, Boff, ot, psq, b_psq, b_wdep)
                if which == 0:
                    qs = ot % 2
                    emit_rope1(ch, psq, b_psq, qro, b_qro, qs * 512)
                    p.dma("sy", Qs[ot * 128:(ot + 1) * 128,
                                   ch * TC:(ch + 1) * TC],
                          qro[:, qs * 512:(qs + 1) * 512], reads=[b_qro])
                elif which == 1:
                    qs = ot % 2
                    emit_rope1(ch, psq, b_psq, qro, b_qro, qs * 512)
                    p.dma("sy", Ks[ot * 128:(ot + 1) * 128,
                                   ch * TC:(ch + 1) * TC],
                          qro[:, qs * 512:(qs + 1) * 512], reads=[b_qro])
                else:
                    vs = ot % 2
                    p.op("sc", lambda q, vs=vs, psq=psq: q.activation(
                        out=qro[:, vs * 512:(vs + 1) * 512],
                        in_=psq, func=AF.Identity),
                        reads=[b_psq], writes=[b_qro])
                    for ii in range(NTT):
                        p.op("te", lambda q, ii=ii, vs=vs: q.transpose(
                            pt[vs][:, ii * 128:(ii + 1) * 128],
                            qro[:, vs * 512 + ii * 128: vs * 512 + (ii + 1) * 128],
                            ident[:]),
                            reads=[b_qro, b_ident], writes=[b_pt[vs]],
                            inc=(ii == NTT - 1))
                    p.op("ve", lambda q, vs=vs: q.tensor_copy(
                        vro[:, vs * 512:(vs + 1) * 512], pt[vs][:]),
                        reads=[b_pt[vs]], writes=[b_vro])
                    p.dma("sy",
                          Vs[ch * TC:(ch + 1) * TC, ot * 128:(ot + 1) * 128].rearrange("(q p) c -> p q c", p=128),
                          vro[:, vs * 512:(vs + 1) * 512].rearrange("p (q c) -> p q c", q=NTT),
                          reads=[b_vro])
    p.barrier()
    if upto <= 1:
        return

    kw4 = kwin[:].rearrange("p (h f t) -> p h f t", h=H1, f=2)
    pbf3 = pbf[:].rearrange("p (i w) -> p i w", i=4)
    pbt3 = pbt[:].rearrange("p (i w) -> p i w", i=4)
    ptb4 = ptb[:].rearrange("p (j i d) -> p j i d", j=8, i=4)
    p.op("gp", lambda q: q.memset(ptb[:], 0.0), writes=[b_ptb2])
    p.op("gp", lambda q: q.memset(kwin[:], 0.0), writes=[b_kw[0], b_kw[1]])
    p.op("gp", lambda q: q.memset(vwin[:], 0.0), writes=[b_vw[0], b_vw[1]])
    for qb in range(NCH):
        if bg is not None and qb < len(bg):
            bg[qb]()
        kb = qb % 2
        ob2 = (qb + 1) % 2
        p.dma("sy", qblk[:, 0:8 * 512].rearrange("p (h t) -> p h t", h=8),
              Qs.rearrange("(h p) t -> p h t", p=128)[:, 0:8, qb * TC:(qb + 1) * TC],
              writes=[b_qb[0]])
        p.dma("sy", qblk[:, 8 * 512:].rearrange("p (h t) -> p h t", h=8),
              Qs.rearrange("(h p) t -> p h t", p=128)[:, 8:16, qb * TC:(qb + 1) * TC],
              writes=[b_qb[1]])
        p.dma("sy", kw4[:, :, kb, :],
              Ks.rearrange("(h p) t -> p h t", p=128)[:, :, qb * TC:(qb + 1) * TC],
              writes=[b_kw[kb]])
        p.dma("sy", vwin[:, kb * NTT * C:(kb + 1) * NTT * C].rearrange("p (q c) -> p q c", q=NTT),
              Vs[qb * TC:(qb + 1) * TC, :].rearrange("(q p) c -> p q c", p=128),
              writes=[b_vw[kb]])
        for h in range(H1):
            bq = b_qb[h // 8]
            hv = h % 4

            for i in range(4):
                n_old = 4 - i
                qt = qblk[:, h * 512 + i * 128:h * 512 + (i + 1) * 128]
                p.op("te", lambda q, i=i, n_old=n_old, h=h, ob2=ob2, qt=qt: q.matmul(
                    psA[:, i * 512:i * 512 + n_old * 128],
                    qt,
                    kwin[:, h * 1024 + ob2 * 512 + i * 128:
                          h * 1024 + ob2 * 512 + (i + n_old) * 128],
                    start=True, stop=True),
                    reads=[bq, b_kw[ob2]], writes=[b_psA[i]])
                if i > 0:
                    p.op("te", lambda q, i=i, n_old=n_old, h=h, kb=kb, qt=qt: q.matmul(
                        psA[:, i * 512 + n_old * 128:(i + 1) * 512],
                        qt,
                        kwin[:, h * 1024 + kb * 512:
                              h * 1024 + kb * 512 + i * 128],
                        start=True, stop=True),
                        reads=[bq, b_kw[kb]], writes=[b_psA[i]])
                p.op("te", lambda q, i=i, h=h, kb=kb, qt=qt: q.matmul(
                    psC[:, i * 128:(i + 1) * 128],
                    qt,
                    kwin[:, h * 1024 + kb * 512 + i * 128:
                          h * 1024 + kb * 512 + (i + 1) * 128],
                    start=True, stop=True),
                    reads=[bq, b_kw[kb]], writes=[b_psC])
            if debug and qb == 2 and h == 0:
                p.dma("sy", io["d_qblk"][:], qblk[:], reads=[b_qb[0], b_qb[1]])
                p.dma("sy", io["d_kwin"][:], kwin[:],
                      reads=[b_kw[0], b_kw[1]])
                p.op("ve", lambda q: q.tensor_copy(rtmp[:], psA[:, 0:1024]),
                     reads=[b_psA[0], b_psA[1]], writes=[b_rt])
                p.dma("sy", io["d_sc"][:, 0:1024], rtmp[:], reads=[b_rt])
                p.op("ve", lambda q: q.tensor_copy(rtmp[:], psA[:, 1024:2048]),
                     reads=[b_psA[2], b_psA[3]], writes=[b_rt])
                p.dma("sy", io["d_sc"][:, 1024:2048], rtmp[:], reads=[b_rt])
                p.op("ve", lambda q: q.tensor_copy(rtmp[:, 0:512], psC[:]),
                     reads=[b_psC], writes=[b_rt])
                p.dma("sy", io["d_sct"][:], rtmp[:, 0:512], reads=[b_rt])
            p.op("ve", lambda q: q.reduce_max(
                mxq[:, 0:4], psA[:].rearrange("p (i w) -> p i w", i=4),
                axis=AX.X),
                reads=b_psA, writes=[b_mxq], inc=False)
            p.op("ve", lambda q: q.reduce_max(
                mxq[:, 4:8], psC[:].rearrange("p (i w) -> p i w", i=4),
                axis=AX.X),
                reads=[b_psC], writes=[b_mxq], flush=True)
            p.op("ve", lambda q: q.tensor_max(mxq[:, 8:12], mxq[:, 0:4],
                                              mxq[:, 4:8]),
                 reads=[b_mxq], writes=[b_mxq], flush=True)
            p.op("ve", lambda q: q.tensor_scalar(
                out=mxq[:, 12:16], in0=mxq[:, 8:12], scalar1=-RSQD1,
                scalar2=None, op0=ALU.mult),
                reads=[b_mxq], writes=[b_mxq], flush=True)
            for i in range(4):
                p.op("sc", lambda q, i=i: q.activation(
                    out=pbf[:, i * 512:(i + 1) * 512],
                    in_=psA[:, i * 512:(i + 1) * 512], func=AF.Exp,
                    scale=RSQD1, bias=mxq[:, 12 + i:13 + i]),
                    reads=[b_psA[i], b_mxq], writes=[b_pbf])
            for i in range(4):
                p.op("sc", lambda q, i=i: q.activation(
                    out=pbt[:, i * 128:(i + 1) * 128],
                    in_=psC[:, i * 128:(i + 1) * 128], func=AF.Exp,
                    scale=RSQD1, bias=mxq[:, 12 + i:13 + i]),
                    reads=[b_psC, b_mxq], writes=[b_pbt])
            if qb > 0:
                p.op("ve", lambda q: q.tensor_mul(
                    pbf3[:, :, 0:128], pbf3[:, :, 0:128],
                    mrep[:, 0:512].rearrange("p (i w) -> p i w", i=4)),
                    reads=[b_mrep], writes=[b_pbf])
            else:
                for i in range(4):
                    p.op("ve", lambda q, i=i: q.tensor_scalar(
                        out=pbf[:, i * 512:i * 512 + (4 - i) * 128],
                        in0=pbf[:, i * 512:i * 512 + (4 - i) * 128],
                        scalar1=0.0, scalar2=None, op0=ALU.mult),
                        reads=[], writes=[b_pbf])
            p.op("ve", lambda q: q.tensor_mul(
                pbt[:], pbt[:], mrep[:, 512:1024]),
                reads=[b_mrep], writes=[b_pbt])
            p.op("ve", lambda q: q.reduce_sum(zq[:, 0:4], pbf3, axis=AX.X),
                 reads=[b_pbf], writes=[b_zq], inc=False)
            p.op("ve", lambda q: q.reduce_sum(zq[:, 4:8], pbt3, axis=AX.X),
                 reads=[b_pbt], writes=[b_zq], flush=True)
            p.op("ve", lambda q: q.tensor_add(zq[:, 8:12], zq[:, 0:4],
                                              zq[:, 4:8]),
                 reads=[b_zq], writes=[b_zq], flush=True)
            p.op("ve", lambda q: q.reciprocal(zq[:, 12:16], zq[:, 8:12]),
                 reads=[b_zq], writes=[b_zq], flush=True)
            for i in range(4):
                p.op("ve", lambda q, i=i: q.tensor_scalar(
                    out=dgc[:, i * 128:(i + 1) * 128], in0=ident[:],
                    scalar1=zq[:, 12 + i:13 + i], scalar2=None,
                    op0=ALU.mult),
                    reads=[b_ident, b_zq], writes=[b_dg[i]])
                for t in range(4):
                    src = pbf[:, i * 512 + t * 128:i * 512 + (t + 1) * 128]
                    p.op("te", lambda q, t=t, i=i, src=src: q.matmul(
                        psB[:, t * 128:(t + 1) * 128], src,
                        dgc[:, i * 128:(i + 1) * 128],
                        start=True, stop=True),
                        reads=[b_pbf, b_dg[i]], writes=[b_psB],
                        inc=(t == 3))
                p.op("te", lambda q, i=i: q.matmul(
                    psC[:, i * 128:(i + 1) * 128],
                    pbt[:, i * 128:(i + 1) * 128],
                    dgc[:, i * 128:(i + 1) * 128],
                    start=True, stop=True),
                    reads=[b_pbt, b_dg[i]], writes=[b_psC])
                p.op("ve", lambda q, i=i: q.tensor_copy(
                    ptb4[:, i:i + 4, i, :],
                    psB[:].rearrange("p (a b) -> p a b", a=4)),
                    reads=[b_psB], writes=[b_ptb2])
                p.op("ve", lambda q, i=i: q.tensor_copy(
                    ptb4[:, i + 4, i, :], psC[:, i * 128:(i + 1) * 128]),
                    reads=[b_psC], writes=[b_ptb2])
            if debug and qb == 2 and h == 0:
                p.dma("sy", io["d_pbf"][:], pbf[:], reads=[b_pbf])
                p.dma("sy", io["d_pbt"][:], pbt[:], reads=[b_pbt])
                p.dma("sy", io["d_ptb"][:], ptb[:], reads=[b_ptb2])
                p.dma("sy", io["d_zq"][:], zq[:], reads=[b_zq])
                p.dma("sy", io["d_mxq"][:], mxq[:], reads=[b_mxq])
            for j in range(8):
                half = kb if j >= 4 else ob2
                tile = j % 4
                vcol = half * NTT * C + tile * C + h * 128
                p.op("te", lambda q, j=j, vcol=vcol, hv=hv: q.matmul(
                    psA[:, hv * 512:(hv + 1) * 512],
                    vwin[:, vcol:vcol + 128],
                    ptb[:, j * 512:(j + 1) * 512],
                    start=(j == 0), stop=(j == 7)),
                    reads=[b_vw[0], b_vw[1], b_ptb2], writes=[b_psA[hv]],
                    inc=(j == 7))
            p.op("ve", lambda q, hv=hv: q.tensor_copy(
                ystage[:], psA[:, hv * 512:(hv + 1) * 512]),
                reads=[b_psA[hv]], writes=[b_ys])
            p.dma("sy", Ys[h * 128:(h + 1) * 128, qb * TC:(qb + 1) * TC],
                  ystage[:], reads=[b_ys])
    p.barrier()
    if upto <= 2:
        return

    b_yc = Buf("yc"); b_wo = [Buf("wo0"), Buf("wo1")]
    b_ls = Buf("lsc")
    for ch in range(NCH):
        p.dma("sy", x_sb[:].rearrange("p (i c) -> p i c", i=NTT),
              x.rearrange("(ch i p) c -> ch p i c", ch=NCH, p=128)[ch],
              writes=[b_x])
        p.dma("sy", ln_fm[:].rearrange("p (j t) -> p j t", j=NCC),
              Ys.rearrange("(j p) t -> p j t", p=128)[:, :, ch * TC:(ch + 1) * TC],
              writes=[b_yc])
        p.dma("sy", lnst[:, 0:2 * C].rearrange("p (i c) -> p i c", i=2),
              Ls.rearrange("(hh i p) c -> hh p i c", hh=2 * NCH, p=128)[2 * ch],
              writes=[b_ls])
        p.dma("sy", kwin[:, 0:2 * C].rearrange("p (i c) -> p i c", i=2),
              Ls.rearrange("(hh i p) c -> hh p i c", hh=2 * NCH, p=128)[2 * ch + 1],
              writes=[b_ls])
        for j in range(NCC):
            p.op("te", lambda q, j=j: q.matmul(
                psC[0:R1, :],
                oA_sb[:, j * R1:(j + 1) * R1],
                ln_fm[:, j * TC:(j + 1) * TC],
                start=(j == 0), stop=(j == NCC - 1)),
                reads=[b_oA, b_yc], writes=[b_psC],
                inc=(j == NCC - 1))
        p.op("sc", lambda q: q.activation(out=u_bf[:], in_=psC[0:R1, :],
                                          func=AF.Identity, scale=LSC),
             reads=[b_psC], writes=[b_u])
        for ob in range(4):
            wb = ob % 2
            p.dma("gp", vwin[:, wb * 8192: wb * 8192 + NCC * 512].rearrange("p (j o) -> p j o", j=NCC),
                  woT.rearrange("(j p) o -> p j o", p=128)[:, :, ob * 512:(ob + 1) * 512],
                  reads=[b_cvo], writes=[b_wo[wb]])
            for i in range(NTT):
                qtr = (ob * NTT + i) % 4
                psq = psA[:, qtr * 512:(qtr + 1) * 512]
                b_psq = b_psA[qtr]
                for j in range(NCC):
                    p.op("te", lambda q, i=i, j=j, wb=wb, psq=psq: q.matmul(
                        psq,
                        ln_fm[:, j * TC + i * 128: j * TC + (i + 1) * 128],
                        vwin[:, wb * 8192 + j * 512: wb * 8192 + (j + 1) * 512],
                        start=(j == 0), stop=False),
                        reads=[b_yc, b_wo[wb]], writes=[b_psq],
                        inc=False)
                p.op("te", lambda q, i=i, ob=ob, psq=psq: q.matmul(
                    psq,
                    u_bf[:, i * 128:(i + 1) * 128],
                    lB[:, 3 * C + ob * 512: 3 * C + (ob + 1) * 512],
                    start=False, stop=True),
                    reads=[b_u, b_lB], writes=[b_psq])
                xa = x_sb[:, i * C + ob * 512: i * C + (ob + 1) * 512]
                la = (lnst[:, (i % 2) * C + ob * 512:(i % 2) * C + (ob + 1) * 512]
                      if i < 2 else
                      kwin[:, (i % 2) * C + ob * 512:(i % 2) * C + (ob + 1) * 512])
                bla = b_ls
                p.op("ve", lambda q, xa=xa, psq=psq: q.tensor_add(
                    xa, psq, xa),
                    reads=[b_psq], writes=[b_x])
                p.op("ve", lambda q, xa=xa, la=la: q.tensor_add(xa, la, xa),
                     reads=[bla], writes=[b_x])
        for i in range(NTT):
            p.dma("sy",
                  y.rearrange("(ch i p) c -> ch p i c", ch=NCH, p=128)[ch][:, i, :],
                  x_sb[:, i * C:(i + 1) * C], reads=[b_x])


def build_l1a(debug=False, upto=3):
    nc = bass.Bass("TRN2", target_bir_lowering=False)
    x = nc.dram_tensor("x", [T, C], F32, kind="ExternalInput").ap()
    wqT = nc.dram_tensor("wqT", [C, C], F32, kind="ExternalInput").ap()
    wkT = nc.dram_tensor("wkT", [C, C], F32, kind="ExternalInput").ap()
    wvT = nc.dram_tensor("wvT", [C, C], F32, kind="ExternalInput").ap()
    woT = nc.dram_tensor("woT", [C, C], F32, kind="ExternalInput").ap()
    qAT = nc.dram_tensor("qAT", [C, R1], F32, kind="ExternalInput").ap()
    qBT = nc.dram_tensor("qBT", [R1, C], F32, kind="ExternalInput").ap()
    kAT = nc.dram_tensor("kAT", [C, R1], F32, kind="ExternalInput").ap()
    kBT = nc.dram_tensor("kBT", [R1, C], F32, kind="ExternalInput").ap()
    vAT = nc.dram_tensor("vAT", [C, R1], F32, kind="ExternalInput").ap()
    vBT = nc.dram_tensor("vBT", [R1, C], F32, kind="ExternalInput").ap()
    oAT = nc.dram_tensor("oAT", [C, R1], F32, kind="ExternalInput").ap()
    oBT = nc.dram_tensor("oBT", [R1, C], F32, kind="ExternalInput").ap()
    nwc = nc.dram_tensor("nwc", [128, NCC], F32, kind="ExternalInput").ap()
    wqTb = nc.dram_tensor("wqTb", [C, C], BF16, kind="Internal").ap()
    if debug:
        dbgio = {
            "d_sc": nc.dram_tensor("d_sc", [128, 2048], F32, kind="ExternalOutput").ap(),
            "d_sct": nc.dram_tensor("d_sct", [128, 512], F32, kind="ExternalOutput").ap(),
            "d_pbf": nc.dram_tensor("d_pbf", [128, 2048], BF16, kind="ExternalOutput").ap(),
            "d_pbt": nc.dram_tensor("d_pbt", [128, 512], BF16, kind="ExternalOutput").ap(),
            "d_ptb": nc.dram_tensor("d_ptb", [128, 4096], BF16, kind="ExternalOutput").ap(),
            "d_zq": nc.dram_tensor("d_zq", [128, 16], F32, kind="ExternalOutput").ap(),
            "d_mxq": nc.dram_tensor("d_mxq", [128, 16], F32, kind="ExternalOutput").ap(),
            "d_qblk": nc.dram_tensor("d_qblk", [128, H1 * 512], BF16, kind="ExternalOutput").ap(),
            "d_kwin": nc.dram_tensor("d_kwin", [128, H1 * 1024], BF16, kind="ExternalOutput").ap(),
        }
    else:
        dbgio = {}
    wkTb = nc.dram_tensor("wkTb", [C, C], BF16, kind="Internal").ap()
    wvTb = nc.dram_tensor("wvTb", [C, C], BF16, kind="Internal").ap()
    woTb = nc.dram_tensor("woTb", [C, C], BF16, kind="Internal").ap()
    y = nc.dram_tensor("y", [T, C], F32, kind="ExternalOutput").ap()
    Qs = nc.dram_tensor("Qs", [C, T], BF16, kind="Internal").ap()
    Ks = nc.dram_tensor("Ks", [C, T], BF16, kind="Internal").ap()
    Vs = nc.dram_tensor("Vs", [T, C], BF16, kind="Internal").ap()
    Ys = nc.dram_tensor("Ys", [C, T], BF16, kind="Internal").ap()
    Ls = nc.dram_tensor("Ls", [T, C], BF16, kind="Internal").ap()
    io = {"x": x, "wqT": wqT, "wkT": wkT, "wvT": wvT, "woT": woT,
          "wqTb": wqTb, "wkTb": wkTb, "wvTb": wvTb, "woTb": woTb,
          "qAT": qAT, "qBT": qBT, "kAT": kAT, "kBT": kBT, "vAT": vAT,
          "vBT": vBT, "oAT": oAT, "oBT": oBT, "nwc": nwc, "y": y,
          "Qs": Qs, "Ks": Ks, "Vs": Vs, "Ys": Ys, "Ls": Ls, **dbgio}
    with ExitStack() as stack:
        p = Prog(nc, stack)
        emit_convert(p, nc, [(wqT, wqTb), (wkT, wkTb), (wvT, wvTb),
                             (woT, woTb)])
        with ExitStack() as sstack:
            emit_l1a(p, nc, sstack, io, debug, upto)
        p.wait_all("sy")
        block = stack.enter_context(nc.Block())
        p.finalize(block)
    return nc



E1 = 8
F1 = 2048


def emit_l1f(p, nc, stack, io, debug=False, cv=None):
    x = io["x"]
    wr = io["wr"]
    wg = io["wg"]
    wu = io["wu"]
    wd = io["wd"]
    y = io["y"]
    Wgb = io["Wgb"]
    Wub = io["Wub"]
    Wdb = io["Wdb"]
    e = stack.enter_context
    x_sb = e(nc.sbuf_tensor("f1_x_sb", [128, NTT * C], F32))
    acc = e(nc.sbuf_tensor("f1_acc", [128, NTT * C], F32))
    sq_scr = e(nc.sbuf_tensor("f1_sq_scr", [128, C], F32))
    ln_tm = e(nc.sbuf_tensor("f1_ln_tm", [128, 2 * C], BF16))
    ln_fm = e(nc.sbuf_tensor("f1_ln_fm", [128, NCC * TC], BF16))
    h_sb = e(nc.sbuf_tensor("f1_h_sb", [128, NCC * TC], BF16))
    wgs = [e(nc.sbuf_tensor(f"f1_wgs{i}", [128, NCC * 128], BF16)) for i in range(2)]
    wus = [e(nc.sbuf_tensor(f"f1_wus{i}", [128, NCC * 128], BF16)) for i in range(2)]
    wds = [e(nc.sbuf_tensor(f"f1_wds{i}", [128, NCC * 512], BF16)) for i in range(2)]
    wr_sb = e(nc.sbuf_tensor("f1_wr_sb", [128, NCC * E1], BF16))
    gs_bf = e(nc.sbuf_tensor("f1_gs_bf", [128, 512], BF16))
    wcols = e(nc.sbuf_tensor("f1_wcols", [128, NTT * E1], F32))
    er8 = e(nc.sbuf_tensor("f1_er8", [128, 2 * E1], F32))
    mxc = e(nc.sbuf_tensor("f1_mxc", [128, 4], F32))
    ident = e(nc.sbuf_tensor("f1_ident", [128, 128], BF16))
    stat = e(nc.sbuf_tensor("f1_stat", [128, 4 * NTT], F32))
    stat_t = e(nc.sbuf_tensor("f1_stat_t", [128, NTT], F32))
    epsc = e(nc.sbuf_tensor("f1_epsc", [128, 1], F32))
    ps = [e(nc.psum_tensor(f"f1_ps{i}", [128, 512], F32)) for i in range(6)]
    pt = [e(nc.psum_tensor(f"f1_pt{i}", [128, 512], BF16)) for i in range(2)]
    b_ps = [Buf(f"ps{i}") for i in range(6)]
    b_pt = [Buf(f"pt{i}") for i in range(2)]
    b_x = Buf("x"); b_acc = Buf("acc")
    b_sq = [Buf("sq0")]; b_sq.append(b_sq[0])
    b_lntm = [Buf("ln_tm0"), Buf("ln_tm1")]; b_lnfm = Buf("ln_fm")
    b_h = Buf("h")
    b_wg = [Buf("wg0"), Buf("wg1")]; b_wu = [Buf("wu0"), Buf("wu1")]
    b_wd = [Buf("wd0"), Buf("wd1")]
    b_wr = Buf("wr"); b_gs = Buf("gs"); b_wc = Buf("wc"); b_er = Buf("er")
    b_mx = Buf("mx"); b_ident = Buf("ident")
    b_stat = Buf("stat"); b_statt = Buf("stat_t"); b_eps = Buf("eps")

    emit_ident(p, ident, b_ident)
    p.op("gp", lambda q: q.memset(epsc[:], 1e-5), writes=[b_eps])
    p.dma("gp", wr_sb[:].rearrange("p (j e) -> p j e", j=NCC),
          wr.rearrange("(j p) e -> p j e", p=128), writes=[b_wr])
    local_conv = cv is None
    if local_conv:
        cv = {k: [Buf(f"cv_{k}{i}") for i in range(E1)] for k in "gud"}
        emit_convert(p, nc, [(wg[0], Wgb[0]), (wu[0], Wub[0]),
                             (wd[0], Wdb[0])],
                     bufs=[cv["g"][0], cv["u"][0], cv["d"][0]])

    sums = stat[:, 0:NTT]; sumsq = stat[:, NTT:2 * NTT]
    acol = stat[:, 2 * NTT:3 * NTT]; bcol = stat[:, 3 * NTT:4 * NTT]
    NFT = F1 // 128
    for ch in range(NCH):
        p.dma("sy", x_sb[:].rearrange("p (i c) -> p i c", i=NTT),
              x.rearrange("(ch i p) c -> ch p i c", ch=NCH, p=128)[ch],
              writes=[b_x])
        for i in range(NTT):
            emit_ln_stats(p, x_sb[:, i * C:(i + 1) * C], sq_scr, b_sq,
                          sums, sumsq, b_stat, b_stat, i, b_x, n=C)
        emit_ln_coeffs(p, sums, sumsq, b_stat, b_stat, acol, bcol, stat_t,
                       b_stat, b_stat, b_statt, NTT, epsc[:, 0:1], b_eps)
        for i in range(NTT):
            sl2 = i % 2
            p.op("sc", lambda q, i=i, sl2=sl2: q.activation(
                out=ln_tm[:, sl2 * C:(sl2 + 1) * C],
                in_=x_sb[:, i * C:(i + 1) * C], func=AF.Identity,
                scale=acol[:, i:i + 1], bias=bcol[:, i:i + 1]),
                reads=[b_x, b_stat], writes=[b_lntm[sl2]])
            emit_transpose_tile(p, ident, b_ident,
                                ln_tm[:, sl2 * C:(sl2 + 1) * C],
                                b_lntm[sl2], ln_fm, b_lnfm, i, pt, b_pt)
        for i in range(NTT):
            for j in range(NCC):
                p.op("te", lambda q, i=i, j=j: q.matmul(
                    ps[4][:, i * E1:(i + 1) * E1],
                    ln_fm[:, j * TC + i * 128: j * TC + (i + 1) * 128],
                    wr_sb[:, j * E1:(j + 1) * E1],
                    start=(j == 0), stop=(j == NCC - 1)),
                    reads=[b_lnfm, b_wr], writes=[b_ps[4]],
                    inc=(j == NCC - 1))
            sl8 = slice(0, E1)
            p.op("ve", lambda q, i=i: q.reduce_max(
                mxc[:, 0:1], ps[4][:, i * E1:(i + 1) * E1], axis=AX.X),
                reads=[b_ps[4]], writes=[b_mx], flush=True)
            p.op("ve", lambda q: q.tensor_scalar(
                out=mxc[:, 1:2], in0=mxc[:, 0:1], scalar1=-1.0,
                scalar2=None, op0=ALU.mult),
                reads=[b_mx], writes=[b_mx], flush=True)
            p.op("sc", lambda q, i=i: q.activation(
                out=er8[:, 0:E1], in_=ps[4][:, i * E1:(i + 1) * E1],
                func=AF.Exp, bias=mxc[:, 1:2]),
                reads=[b_ps[4], b_mx], writes=[b_er])
            p.op("ve", lambda q: q.reduce_max(mxc[:, 0:1], er8[:, 0:E1],
                                              axis=AX.X),
                 reads=[b_er], writes=[b_mx], flush=True)
            p.op("ve", lambda q: q.tensor_scalar(
                out=er8[:, E1:2 * E1], in0=er8[:, 0:E1],
                scalar1=mxc[:, 0:1], scalar2=-1e30,
                op0=ALU.is_ge, op1=ALU.mult),
                reads=[b_er, b_mx], writes=[b_er], flush=True)
            p.op("ve", lambda q: q.tensor_add(er8[:, E1:2 * E1],
                                              er8[:, 0:E1],
                                              er8[:, E1:2 * E1]),
                 reads=[b_er], writes=[b_er], flush=True)
            p.op("ve", lambda q: q.reduce_max(mxc[:, 1:2],
                                              er8[:, E1:2 * E1], axis=AX.X),
                 reads=[b_er], writes=[b_mx], flush=True)
            p.op("ve", lambda q: q.tensor_add(mxc[:, 2:3], mxc[:, 0:1],
                                              mxc[:, 1:2]),
                 reads=[b_mx], writes=[b_mx], flush=True)
            p.op("ve", lambda q: q.reciprocal(mxc[:, 2:3], mxc[:, 2:3]),
                 reads=[b_mx], writes=[b_mx], flush=True)
            p.op("ve", lambda q: q.tensor_scalar(
                out=er8[:, E1:2 * E1], in0=er8[:, 0:E1],
                scalar1=mxc[:, 1:2], scalar2=None, op0=ALU.is_ge),
                reads=[b_er, b_mx], writes=[b_er], flush=True)
            p.op("ve", lambda q: q.tensor_mul(er8[:, E1:2 * E1],
                                              er8[:, 0:E1],
                                              er8[:, E1:2 * E1]),
                 reads=[b_er], writes=[b_er], flush=True)
            p.op("ve", lambda q, i=i: q.tensor_scalar(
                out=wcols[:, i * E1:(i + 1) * E1], in0=er8[:, E1:2 * E1],
                scalar1=mxc[:, 2:3], scalar2=None, op0=ALU.mult),
                reads=[b_er, b_mx], writes=[b_wc], flush=True)
        for ei in range(E1):
            if local_conv and ch == 0 and ei + 1 < E1:
                emit_convert(p, nc, [(wg[ei + 1], Wgb[ei + 1]),
                                     (wu[ei + 1], Wub[ei + 1]),
                                     (wd[ei + 1], Wdb[ei + 1])],
                             bufs=[cv["g"][ei + 1], cv["u"][ei + 1],
                                   cv["d"][ei + 1]])
            for ft in range(NFT):
                fb = ft % 2
                p.dma("gp", wgs[fb][:].rearrange("p (j o) -> p j o", j=NCC),
                      Wgb[ei].rearrange("(j p) o -> p j o", p=128)[:, :, ft * 128:(ft + 1) * 128],
                      reads=[cv["g"][ei]], writes=[b_wg[fb]])
                p.dma("gp", wus[fb][:].rearrange("p (j o) -> p j o", j=NCC),
                      Wub[ei].rearrange("(j p) o -> p j o", p=128)[:, :, ft * 128:(ft + 1) * 128],
                      reads=[cv["u"][ei]], writes=[b_wu[fb]])
                for j in range(NCC):
                    p.op("te", lambda q, j=j, fb=fb: q.matmul(
                        ps[0][:], wgs[fb][:, j * 128:(j + 1) * 128],
                        ln_fm[:, j * TC:(j + 1) * TC],
                        start=(j == 0), stop=(j == NCC - 1)),
                        reads=[b_wg[fb], b_lnfm], writes=[b_ps[0]],
                        inc=(j == NCC - 1))
                for j in range(NCC):
                    p.op("te", lambda q, j=j, fb=fb: q.matmul(
                        ps[1][:], wus[fb][:, j * 128:(j + 1) * 128],
                        ln_fm[:, j * TC:(j + 1) * TC],
                        start=(j == 0), stop=(j == NCC - 1)),
                        reads=[b_wu[fb], b_lnfm], writes=[b_ps[1]],
                        inc=(j == NCC - 1))
                p.op("sc", lambda q: q.activation(out=gs_bf[:],
                                                  in_=ps[0][:],
                                                  func=AF.Silu),
                     reads=[b_ps[0]], writes=[b_gs])
                p.op("ve", lambda q, ft=ft: q.tensor_mul(
                    h_sb[:, ft * 512:(ft + 1) * 512], gs_bf[:], ps[1][:]),
                    reads=[b_gs, b_ps[1]], writes=[b_h])
            for ob in range(4):
                wb = ob % 2
                p.dma("gp", wds[wb][:].rearrange("p (j o) -> p j o", j=NCC),
                      Wdb[ei].rearrange("(j p) o -> p j o", p=128)[:, :, ob * 512:(ob + 1) * 512],
                      reads=[cv["d"][ei]], writes=[b_wd[wb]])
                for i in range(NTT):
                    bank = 2 + (ob * NTT + i) % 2
                    for j in range(NCC):
                        p.op("te", lambda q, i=i, j=j, wb=wb, bank=bank: q.matmul(
                            ps[bank][:],
                            h_sb[:, j * TC + i * 128: j * TC + (i + 1) * 128],
                            wds[wb][:, j * 512:(j + 1) * 512],
                            start=(j == 0), stop=(j == NCC - 1)),
                            reads=[b_h, b_wd[wb]], writes=[b_ps[bank]],
                            inc=(j == NCC - 1))
                    aa = acc[:, i * C + ob * 512: i * C + (ob + 1) * 512]
                    wcol = wcols[:, i * E1 + ei:i * E1 + ei + 1]
                    if ei == 0:
                        p.op("ve", lambda q, aa=aa, wcol=wcol, bank=bank: q.tensor_scalar(
                            out=aa, in0=ps[bank][:], scalar1=wcol,
                            scalar2=None, op0=ALU.mult),
                            reads=[b_ps[bank], b_wc], writes=[b_acc])
                    else:
                        p.op("ve", lambda q, aa=aa, wcol=wcol, bank=bank: q.scalar_tensor_tensor(
                            out=aa, in0=ps[bank][:], scalar=wcol, in1=aa,
                            op0=ALU.mult, op1=ALU.add),
                            reads=[b_ps[bank], b_wc], writes=[b_acc])
        for i in range(NTT):
            for ob in range(4):
                xa = x_sb[:, i * C + ob * 512: i * C + (ob + 1) * 512]
                aa = acc[:, i * C + ob * 512: i * C + (ob + 1) * 512]
                p.op("ve", lambda q, xa=xa, aa=aa: q.tensor_add(xa, aa, xa),
                     reads=[b_acc], writes=[b_x])
            p.dma("sy",
                  y.rearrange("(ch i p) c -> ch p i c", ch=NCH, p=128)[ch][:, i, :],
                  x_sb[:, i * C:(i + 1) * C], reads=[b_x])


def build_l1f(debug=False):
    nc = bass.Bass("TRN2", target_bir_lowering=False)
    x = nc.dram_tensor("x", [T, C], F32, kind="ExternalInput").ap()
    wr = nc.dram_tensor("wr", [C, E1], F32, kind="ExternalInput").ap()
    wg = nc.dram_tensor("wg", [E1, C, F1], F32, kind="ExternalInput").ap()
    wu = nc.dram_tensor("wu", [E1, C, F1], F32, kind="ExternalInput").ap()
    wd = nc.dram_tensor("wd", [E1, F1, C], F32, kind="ExternalInput").ap()
    y = nc.dram_tensor("y", [T, C], F32, kind="ExternalOutput").ap()
    Wgb = nc.dram_tensor("Wgb", [E1, C, F1], BF16, kind="Internal").ap()
    Wub = nc.dram_tensor("Wub", [E1, C, F1], BF16, kind="Internal").ap()
    Wdb = nc.dram_tensor("Wdb", [E1, F1, C], BF16, kind="Internal").ap()
    io = {"x": x, "wr": wr, "wg": wg, "wu": wu, "wd": wd, "y": y,
          "Wgb": Wgb, "Wub": Wub, "Wdb": Wdb}
    with ExitStack() as stack:
        p = Prog(nc, stack)
        with ExitStack() as sstack:
            emit_l1f(p, nc, sstack, io, debug)
        p.wait_all("sy")
        block = stack.enter_context(nc.Block())
        p.finalize(block)
    return nc



def build_fused():
    nc = bass.Bass("TRN2", target_bir_lowering=False)
    D = lambda n, s, dt, k: nc.dram_tensor(n, s, dt, kind=k).ap()
    x = D("x", [T, C], F32, "ExternalInput")
    y = D("y", [T, C], F32, "ExternalOutput")
    io0a = {
        "x": x,
        "wqT": D("a0_wqT", [C, C], F32, "ExternalInput"),
        "wkT": D("a0_wkT", [C, KV0], F32, "ExternalInput"),
        "wvT": D("a0_wvT", [C, KV0], F32, "ExternalInput"),
        "woT": D("a0_woT", [C, C], F32, "ExternalInput"),
        "wqTb": D("a0_wqTb", [C, C], BF16, "Internal"),
        "wkTb": D("a0_wkTb", [C, KV0], BF16, "Internal"),
        "wvTb": D("a0_wvTb", [C, KV0], BF16, "Internal"),
        "woTb": D("a0_woTb", [C, C], BF16, "Internal"),
        "bo": D("a0_bo", [1, C], F32, "ExternalInput"),
        "qwc": D("a0_qwc", [128, 1], F32, "ExternalInput"),
        "kwc": D("a0_kwc", [128, 1], F32, "ExternalInput"),
        "y": D("x1", [T, C], F32, "Internal"),
        "Qs": D("a0_Qs", [C, T], BF16, "Internal"),
        "Ys": D("a0_Ys", [C, T], BF16, "Internal"),
    }
    io0f = {
        "x": io0a["y"],
        "wT": D("f0_wT", [C, C], F32, "ExternalInput"),
        "bias": D("f0_bias", [1, C], F32, "ExternalInput"),
        "y": D("x2", [T, C], F32, "Internal"),
    }
    io1a = {
        "x": io0f["y"],
        "wqT": D("a1_wqT", [C, C], F32, "ExternalInput"),
        "wkT": D("a1_wkT", [C, C], F32, "ExternalInput"),
        "wvT": D("a1_wvT", [C, C], F32, "ExternalInput"),
        "woT": D("a1_woT", [C, C], F32, "ExternalInput"),
        "wqTb": D("a1_wqTb", [C, C], BF16, "Internal"),
        "wkTb": D("a1_wkTb", [C, C], BF16, "Internal"),
        "wvTb": D("a1_wvTb", [C, C], BF16, "Internal"),
        "woTb": D("a1_woTb", [C, C], BF16, "Internal"),
        "qAT": D("a1_qAT", [C, R1], F32, "ExternalInput"),
        "qBT": D("a1_qBT", [R1, C], F32, "ExternalInput"),
        "kAT": D("a1_kAT", [C, R1], F32, "ExternalInput"),
        "kBT": D("a1_kBT", [R1, C], F32, "ExternalInput"),
        "vAT": D("a1_vAT", [C, R1], F32, "ExternalInput"),
        "vBT": D("a1_vBT", [R1, C], F32, "ExternalInput"),
        "oAT": D("a1_oAT", [C, R1], F32, "ExternalInput"),
        "oBT": D("a1_oBT", [R1, C], F32, "ExternalInput"),
        "nwc": D("a1_nwc", [128, NCC], F32, "ExternalInput"),
        "y": D("x3", [T, C], F32, "Internal"),
        "Qs": D("a1_Qs", [C, T], BF16, "Internal"),
        "Ks": D("a1_Ks", [C, T], BF16, "Internal"),
        "Vs": D("a1_Vs", [T, C], BF16, "Internal"),
        "Ys": D("a1_Ys", [C, T], BF16, "Internal"),
        "Ls": D("a1_Ls", [T, C], BF16, "Internal"),
    }
    io1f = {
        "x": io1a["y"],
        "wr": D("f1_wr", [C, E1], F32, "ExternalInput"),
        "wg": D("f1_wg", [E1, C, F1], F32, "ExternalInput"),
        "wu": D("f1_wu", [E1, C, F1], F32, "ExternalInput"),
        "wd": D("f1_wd", [E1, F1, C], F32, "ExternalInput"),
        "y": y,
        "Wgb": D("f1_Wgb", [E1, C, F1], BF16, "Internal"),
        "Wub": D("f1_Wub", [E1, C, F1], BF16, "Internal"),
        "Wdb": D("f1_Wdb", [E1, F1, C], BF16, "Internal"),
    }
    with ExitStack() as stack:
        p = Prog(nc, stack)
        pairs0 = [(io0a["wqT"], io0a["wqTb"]), (io0a["wkT"], io0a["wkTb"]),
                  (io0a["wvT"], io0a["wvTb"]), (io0a["woT"], io0a["woTb"])]
        pairs1 = [(io1a["wqT"], io1a["wqTb"]), (io1a["wkT"], io1a["wkTb"]),
                  (io1a["wvT"], io1a["wvTb"]), (io1a["woT"], io1a["woTb"])]
        emit_convert(p, nc, pairs0)
        with ExitStack() as s1:
            emit_l0a(p, nc, s1, io0a)
        emit_convert(p, nc, pairs1)
        p.barrier()
        with ExitStack() as s2:
            emit_l0f(p, nc, s2, io0f)
        p.barrier()
        cvf = {k: [Buf(f"cvf_{k}{i}") for i in range(E1)] for k in "gud"}
        bg = [
            (lambda ei=ei: emit_convert(
                p, nc, [(io1f["wg"][ei], io1f["Wgb"][ei]),
                        (io1f["wu"][ei], io1f["Wub"][ei]),
                        (io1f["wd"][ei], io1f["Wdb"][ei])],
                bufs=[cvf["g"][ei], cvf["u"][ei], cvf["d"][ei]]))
            for ei in range(E1)
        ]
        with ExitStack() as s3:
            emit_l1a(p, nc, s3, io1a, bg=bg)
        p.barrier()
        with ExitStack() as s4:
            emit_l1f(p, nc, s4, io1f, cv=cvf)
        p.wait_all("sy")
        block = stack.enter_context(nc.Block())
        p.finalize(block)
    return nc



_NC_CACHE = {}


def _get_nc(name):
    if name not in _NC_CACHE:
        _NC_CACHE[name] = {"l0a": build_l0a, "l0f": build_l0f,
                           "l1a": build_l1a, "l1f": build_l1f,
                           "fused": build_fused}[name]()
    return _NC_CACHE[name]


def kernel(**inputs):
    gi = lambda k: np.ascontiguousarray(np.asarray(inputs[k], dtype=np.float32))
    x = gi("L0_attn_x")
    cores = list(range(N_CORES))
    w = {
        "a0_wqT": np.ascontiguousarray(gi("L0_attn_Wq").T),
        "a0_wkT": np.ascontiguousarray(gi("L0_attn_Wk").T),
        "a0_wvT": np.ascontiguousarray(gi("L0_attn_Wv").T),
        "a0_woT": np.ascontiguousarray(gi("L0_attn_Wo").T),
        "a0_bo": gi("L0_attn_bo").reshape(1, C),
        "a0_qwc": gi("L0_attn_q_norm_w").reshape(128, 1),
        "a0_kwc": gi("L0_attn_k_norm_w").reshape(128, 1),
        "f0_wT": np.ascontiguousarray(gi("L0_ffn_weight").T),
        "f0_bias": gi("L0_ffn_bias").reshape(1, C),
        "a1_wqT": np.ascontiguousarray(gi("L1_attn_wq").T),
        "a1_wkT": np.ascontiguousarray(gi("L1_attn_wk").T),
        "a1_wvT": np.ascontiguousarray(gi("L1_attn_wv").T),
        "a1_woT": np.ascontiguousarray(gi("L1_attn_wo").T),
        "a1_qAT": np.ascontiguousarray(gi("L1_attn_qA").T),
        "a1_qBT": np.ascontiguousarray(gi("L1_attn_qB").T),
        "a1_kAT": np.ascontiguousarray(gi("L1_attn_kA").T),
        "a1_kBT": np.ascontiguousarray(gi("L1_attn_kB").T),
        "a1_vAT": np.ascontiguousarray(gi("L1_attn_vA").T),
        "a1_vBT": np.ascontiguousarray(gi("L1_attn_vB").T),
        "a1_oAT": np.ascontiguousarray(gi("L1_attn_oA").T),
        "a1_oBT": np.ascontiguousarray(gi("L1_attn_oB").T),
        "a1_nwc": np.ascontiguousarray(gi("L1_attn_norm_w").reshape(NCC, 128).T),
        "f1_wr": gi("L1_ffn_Wr"),
        "f1_wg": gi("L1_ffn_Wg"),
        "f1_wu": gi("L1_ffn_Wu"),
        "f1_wd": gi("L1_ffn_Wd"),
    }
    res = run_bass_kernel_spmd(_get_nc("fused"),
                               [dict(w, x=x[i]) for i in cores], cores)
    out = np.stack([res.results[i]["y"] for i in cores], axis=0)
    return out.astype(np.float32)
